# version 13
# baseline (speedup 1.0000x reference)
import numpy as np
import ml_dtypes
import concourse.bass as bass
import concourse.mybir as mybir
from concourse.bass_utils import run_bass_kernel_spmd
from contextlib import ExitStack

F32 = mybir.dt.float32
BF16 = mybir.dt.bfloat16
I8 = mybir.dt.int8
U8 = mybir.dt.uint8
AF = mybir.ActivationFunctionType
ALU = mybir.AluOpType
AX = mybir.AxisListType

ENGINES = ("pe", "act", "dve", "pool", "sp")
ENGINE_ATTR = {"pe": "tensor", "act": "scalar", "dve": "vector",
               "pool": "gpsimd", "sp": "sync"}

D = 2048
T = 2048
NXC = D // 128
INTER = 8192
NH = 16
HD = 128
ISQ = float(1.0 / np.sqrt(np.float32(128.0)))
EPS = 1e-5
NEGMASK = -10000.0

GROUPS = [[0, 1, 2, 3, 4, 5]]


class Sem:
    __slots__ = ("name", "cnt", "h")

    def __init__(self, name):
        self.name = name
        self.cnt = 0
        self.h = None


class Prog:
    def __init__(self, nc):
        self.nc = nc
        self.q = {e: [] for e in ENGINES}
        self.sems = []
        self._named = {}

    def sem(self, name):
        s = Sem(name)
        self.sems.append(s)
        return s

    def dsem(self, name):
        if name not in self._named:
            self._named[name] = self.sem(name)
        return self._named[name]

    def emit(self, eng, fn, wait=(), inc=()):
        wait = tuple((s, v) for (s, v) in wait if v > 0)
        inc = tuple(inc)
        for s, n in inc:
            s.cnt += n

        def run(h):
            for s, v in wait:
                h.wait_ge(s.h, v)
            instr = fn(h)
            for s, n in inc:
                instr = instr.then_inc(s.h, n)
        self.q[eng].append(run)

    def wait(self, eng, *pairs):
        pairs = tuple((s, v) for (s, v) in pairs if v > 0)
        if not pairs:
            return

        def run(h):
            for s, v in pairs:
                h.wait_ge(s.h, v)
        self.q[eng].append(run)

    def alloc_sems(self, stack):
        for s in self.sems:
            s.h = stack.enter_context(self.nc.semaphore(s.name))

    def replay(self, block):
        for e in ENGINES:
            lst = self.q[e]
            if not lst:
                continue
            getattr(block, ENGINE_ATTR[e])(lambda h, lst=lst: [f(h) for f in lst])


class Env:
    pass


def barrier(env, engines=ENGINES):
    p = env.p
    snap = []
    for s in p.sems:
        if s.cnt != env.bar_snap.get(s.name, 0):
            snap.append((s, s.cnt))
            env.bar_snap[s.name] = s.cnt
    if not snap:
        return
    for e in engines:
        p.wait(e, *snap)


def emit_ln(env, x_dram, t0, tlen, h_tiles, hcol0, bigxb, router=None):
    nc, p = env.nc, env.p
    S = env.s
    sub_done = []
    nsub = tlen // 512
    sxh = [p.dsem(f"xbh{i}") for i in range(4)]
    sxf = [p.dsem(f"xf{i}") for i in range(2)]
    with ExitStack() as st:
        sb = lambda n, shp, dt: st.enter_context(nc.sbuf_tensor(n, shp, dt))
        psa = lambda n, shp, dt: st.enter_context(nc.psum_tensor(n, shp, dt))
        uid = f"ln{env.uid}"
        env.uid += 1
        xsq = [sb(f"{uid}xq{i}", [128, 512], BF16) for i in range(2)]
        t1 = [sb(f"{uid}t1{i}", [128, 512], F32) for i in range(2)]
        if router is not None:
            xf = [sb(f"{uid}xf{i}", [128, 512], F32) for i in range(2)]
        st_mean = sb(f"{uid}mean", [1, 512], F32)
        st_var = sb(f"{uid}var", [1, 512], F32)
        st_tmp = sb(f"{uid}tmp", [1, 512], F32)
        st_r = sb(f"{uid}r", [1, 512], F32)
        st_sd = st_tmp
        st_nmr = st_var
        ps_sum = psa(f"{uid}psum", [1, 512], F32)
        ps_sq = psa(f"{uid}psq", [1, 512], F32)
        ps_nmrb = psa(f"{uid}psnm", [128, 512], F32)
        rb_sb = sb(f"{uid}rbs", [128, 512], F32)
        if router is not None:
            ps_lg = [psa(f"{uid}pslg{i}", [128, 8], F32) for i in range(4)]

        bh_g = env.rg.setdefault("bigxb", [[(None, 0)] for _ in range(4)])
        xsq_g = env.rg.setdefault("xsq", [(None, 0)] * 2)
        xf_g = env.rg.setdefault("xf", [(None, 0)] * 2)
        t1_g = env.rg.setdefault("t1", [(None, 0)] * 2)
        xt = x_dram.tensor
        for sub in range(nsub):
            ts0 = t0 + sub * 512
            bdma = [0, 0, 0, 0]
            for bh in range(4):
                p.emit("pool", lambda e, bh=bh, ts0=ts0: e.dma_start(
                    bass.AP(bigxb, bh * 2048, [[8192, 128], [512, 4], [1, 512]]),
                    bass.AP(xt, bh * 4 * 128 * T + ts0,
                            [[T, 128], [128 * T, 4], [1, 512]])),
                    wait=[(gs, gv) for (gs, gv) in bh_g[bh] if gs],
                    inc=[(sxh[bh], 16)])
                bdma[bh] = sxh[bh].cnt
            for xc in range(NXC):
                bh = xc // 4
                xbv = bigxb[:, xc * 512:(xc + 1) * 512]
                sl = xc % 2
                gs, gv = xsq_g[sl]
                p.emit("act", lambda e, sl=sl, xbv=xbv: e.square(xsq[sl][:], xbv),
                       wait=[(sxh[bh], bdma[bh])] + ([(gs, gv)] if gs else []),
                       inc=[(S.act, 1)])
                sq_cnt = S.act.cnt
                if router is not None:
                    gs, gv = xf_g[sl]
                    p.emit("sp", lambda e, xc=xc, sl=sl, ts0=ts0: e.dma_start(
                        xf[sl][:], x_dram[xc * 128:(xc + 1) * 128, ts0:ts0 + 512]),
                        wait=([(gs, gv)] if gs else []), inc=[(sxf[sl], 16)])
                    f_cnt = sxf[sl].cnt
                w = [(sxh[bh], bdma[bh])]
                if xc == 0 and sub == 0:
                    w.append((S.dact, env.const_dact))
                p.emit("pe", lambda e, xbv=xbv, xc=xc: e.matmul(
                    ps_sum[:], env.onesc_bf[:], xbv,
                    start=(xc == 0), stop=(xc == NXC - 1)), wait=w, inc=[(S.pe, 1)])
                p.emit("pe", lambda e, sl=sl, xc=xc: e.matmul(
                    ps_sq[:], env.onesc_bf[:], xsq[sl][:],
                    start=(xc == 0), stop=(xc == NXC - 1)),
                    wait=[(S.act, sq_cnt)], inc=[(S.pe, 1)])
                xsq_g[sl] = (S.pe, S.pe.cnt)
                if router is not None:
                    for tc4 in range(4):
                        p.emit("pe", lambda e, sl=sl, xc=xc, tc4=tc4: e.matmul(
                            ps_lg[tc4][:],
                            xf[sl][:, tc4 * 128:(tc4 + 1) * 128],
                            router["wg_sb"][:, xc * 8:(xc + 1) * 8],
                            start=(xc == 0), stop=(xc == NXC - 1)),
                            wait=[(sxf[sl], f_cnt)], inc=[(S.pe, 1)])
                    xf_g[sl] = (S.pe, S.pe.cnt)
            stats_pe = S.pe.cnt
            p.emit("dve", lambda e: e.tensor_scalar_mul(st_mean[:], ps_sum[:], 1.0 / D),
                   wait=[(S.pe, stats_pe)], inc=[(S.dve, 1)])
            p.emit("dve", lambda e: e.tensor_scalar_mul(st_var[:], ps_sq[:], 1.0 / D),
                   inc=[(S.dve, 1)])
            p.emit("dve", lambda e: e.tensor_mul(st_tmp[:], st_mean[:], st_mean[:]),
                   wait=[(S.dve, S.dve.cnt)], inc=[(S.dve, 1)])
            p.emit("dve", lambda e: e.scalar_tensor_tensor(
                st_var[:], st_var[:], EPS, st_tmp[:], ALU.add, ALU.subtract),
                wait=[(S.dve, S.dve.cnt)], inc=[(S.dve, 1)])
            var_cnt = S.dve.cnt
            p.emit("act", lambda e: e.sqrt(st_sd[:], st_var[:]),
                   wait=[(S.dve, var_cnt)], inc=[(S.act, 1)])
            sd_cnt = S.act.cnt
            p.emit("dve", lambda e: e.reciprocal(st_r[:], st_sd[:]),
                   wait=[(S.act, sd_cnt)], inc=[(S.dve, 1)])
            p.emit("dve", lambda e: e.scalar_tensor_tensor(
                st_nmr[:], st_mean[:], -1.0, st_r[:], ALU.mult, ALU.mult),
                wait=[(S.dve, S.dve.cnt)], inc=[(S.dve, 1)])
            fin_cnt = S.dve.cnt
            p.emit("pe", lambda e: e.matmul(ps_nmrb[:], env.onesr_f[:], st_r[0:1, :],
                                            start=True, stop=True),
                   wait=[(S.dve, fin_cnt)], inc=[(S.pe, 1)])
            rbmm = S.pe.cnt
            p.emit("dve", lambda e: e.tensor_copy(rb_sb[:], ps_nmrb[:]),
                   wait=[(S.pe, rbmm)], inc=[(S.dve, 1)])
            rbev = S.dve.cnt
            p.emit("pe", lambda e: e.matmul(ps_nmrb[:], env.onesr_f[:], st_nmr[0:1, :],
                                            start=True, stop=True),
                   wait=[(S.dve, rbev)], inc=[(S.pe, 1)])
            bc_cnt = S.pe.cnt
            if router is not None:
                for tc4 in range(4):
                    tc = (t0 - router["t0"]) // 128 + sub * 4 + tc4
                    p.emit("dve", lambda e, tc4=tc4, tc=tc: e.tensor_copy(
                        router["raw"][:, tc * 8:(tc + 1) * 8], ps_lg[tc4][:]),
                        wait=[(S.pe, stats_pe)] if tc4 == 0 else [],
                        inc=[(S.dve, 1)])
                raw_ev = S.dve.cnt
                for tc4 in range(4):
                    tc = (t0 - router["t0"]) // 128 + sub * 4 + tc4
                    p.emit("pe", lambda e, tc4=tc4: e.matmul(
                        ps_lg[0][:, 0:1], st_mean[0:1, tc4 * 128:(tc4 + 1) * 128],
                        env.onesr_f[0:1, 0:1], start=True, stop=True),
                        wait=[(S.dve, max(raw_ev, fin_cnt))] if tc4 == 0 else
                             [(S.dve, S.dve.cnt)],
                        inc=[(S.pe, 1)])
                    mcnt = S.pe.cnt
                    p.emit("dve", lambda e, tc=tc: e.tensor_copy(
                        router["mtok"][:, tc:tc + 1], ps_lg[0][:, 0:1]),
                        wait=[(S.pe, mcnt)], inc=[(S.dve, 1)])
                    p.emit("pe", lambda e, tc4=tc4: e.matmul(
                        ps_lg[0][:, 1:2], st_r[0:1, tc4 * 128:(tc4 + 1) * 128],
                        env.onesr_f[0:1, 0:1], start=True, stop=True),
                        wait=[(S.dve, S.dve.cnt)], inc=[(S.pe, 1)])
                    rcnt = S.pe.cnt
                    p.emit("dve", lambda e, tc=tc: e.tensor_copy(
                        router["rtok"][:, tc:tc + 1], ps_lg[0][:, 1:2]),
                        wait=[(S.pe, rcnt)], inc=[(S.dve, 1)])
            p.wait("dve", (S.pe, bc_cnt))
            half_apply = [0, 0, 0, 0]
            for xc in range(NXC):
                xbv = bigxb[:, xc * 512:(xc + 1) * 512]
                sl1 = xc % 2
                gs, gv = t1_g[sl1]
                p.emit("dve", lambda e, xbv=xbv, sl1=sl1: e.tensor_mul(
                    t1[sl1][:], xbv, rb_sb[:]),
                    wait=([(gs, gv)] if gs else []), inc=[(S.dve, 1)])
                p.emit("dve", lambda e, sl1=sl1, xc=xc, col=hcol0 + sub * 512: e.tensor_add(
                    h_tiles[xc][:, col:col + 512], t1[sl1][:], ps_nmrb[:]),
                    wait=[(S.dve, S.dve.cnt)], inc=[(S.dve, 1)])
                t1_g[sl1] = (S.dve, S.dve.cnt)
                half_apply[xc // 4] = S.dve.cnt
            for bh in range(4):
                bh_g[bh] = [(S.dve, half_apply[bh]), (S.pe, stats_pe)]
            sub_done.append(S.dve.cnt)
    return sub_done


def emit_ffn0(env, x_in, x_out, W):
    nc, p = env.nc, env.p
    S = env.s
    NOG = 16
    swgu = [p.dsem(f"wgu{i}") for i in range(2)]
    swd = [p.dsem(f"wd{i}") for i in range(2)]
    sxr = [p.dsem(f"xr{i}") for i in range(4)]
    sxn = [p.dsem(f"xn{i}") for i in range(4)]

    def _half(half):
        t0 = half * 1024
        with ExitStack() as st:
            sb = lambda n, shp, dt: st.enter_context(nc.sbuf_tensor(n, shp, dt))
            psa = lambda n, shp, dt: st.enter_context(nc.psum_tensor(n, shp, dt))
            uid = f"f{env.uid}"
            env.uid += 1
            h = [sb(f"{uid}h{i}", [128, 1024], BF16) for i in range(NXC)]
            wgu = [sb(f"{uid}wgu{r}", [128, NXC * 512], BF16) for r in range(2)]
            wd = [sb(f"{uid}wd{r}", [128, 4 * 1024], BF16) for r in range(2)]
            gf = sb(f"{uid}gf", [128, 8192], BF16)
            y = [sb(f"{uid}y{i}", [128, 1024], F32) for i in range(NXC)]
            xnw = [sb(f"{uid}xn{i}", [128, 512], F32) for i in range(4)]
            xfr = [sb(f"{uid}xr{i}", [128, 512], F32) for i in range(4)]
            psg = [psa(f"{uid}psg{r}", [128, 512], F32) for r in range(3)]
            psy = [psa(f"{uid}psy{r}", [128, 512], F32) for r in range(2)]

            def load_wgu(slot, Wm, og, guard):
                p.emit("pool", lambda e, slot=slot, og=og, Wm=Wm: e.dma_start(
                    bass.AP(wgu[slot], 0, [[NXC * 512, 128], [512, NXC], [1, 512]]),
                    bass.AP(Wm.tensor, og * 512,
                            [[INTER, 128], [128 * INTER, NXC], [1, 512]])),
                    wait=[guard] if guard[1] > 0 else [],
                    inc=[(swgu[slot], 16)])
                return swgu[slot].cnt

            def load_wd(slot, og, o2h, guard):
                p.emit("pool", lambda e, slot=slot, og=og, o2h=o2h: e.dma_start(
                    bass.AP(wd[slot], 0, [[4 * 1024, 128], [1024, 4], [1, 1024]]),
                    bass.AP(W["wd"].tensor, og * 512 * D + o2h * 1024,
                            [[D, 128], [128 * D, 4], [1, 1024]])),
                    wait=[guard] if guard[1] > 0 else [],
                    inc=[(swd[slot], 16)])
                return swd[slot].cnt

            wg_ready = [0, 0]
            wu_ready = [0, 0]
            wd_ready = [0, 0]
            gate_done = [0, 0]
            up_done = [0, 0]
            wd_mm_done = [0, 0]
            psg_free = [0, 0, 0]
            psg_free_eng = [None, None, None]
            psy_free = [0, 0]
            wg_ready[0] = load_wgu(0, W["wg"], 0, (S.pe, 0))
            wu_ready[1] = load_wgu(1, W["wu"], 0, (S.pe, 0))
            wd_ready[0] = load_wd(0, 0, 0, (S.pe, 0))
            h_sub = emit_ln(env, x_in, t0, 1024, h, 0, gf)
            h_all = h_sub[-1]
            f_done_prev = 0
            for og in range(NOG):
                s_g = og % 2
                s_u = 1 - s_g
                s_d = og % 2
                if og > 0:
                    p.wait("act", (S.dve, f_done_prev))
                p.wait("pe", (swgu[s_g], wg_ready[s_g]))
                for ts in range(2):
                    p.wait("pe", (S.dve, h_sub[ts]))
                    for j in range(4):
                        oc = og * 4 + j
                        bank = (ts * 4 + j) % 3
                        if psg_free_eng[bank] is not None:
                            p.wait("pe", (psg_free_eng[bank], psg_free[bank]))
                        for c in range(NXC):
                            p.emit("pe", lambda e, c=c, j=j, ts=ts, bank=bank, s_g=s_g: e.matmul(
                                psg[bank][:],
                                wgu[s_g][:, c * 512 + j * 128:c * 512 + (j + 1) * 128],
                                h[c][:, ts * 512:(ts + 1) * 512],
                                start=(c == 0), stop=(c == NXC - 1)),
                                inc=[(S.pe, 1)] if c == NXC - 1 else [])
                        mile = S.pe.cnt
                        p.emit("act", lambda e, j=j, ts=ts, oc=oc, bank=bank: e.activation(
                            gf[:, j * 1024 + ts * 512:j * 1024 + (ts + 1) * 512],
                            psg[bank][:], AF.Silu, scale=env.sg_sb[:, oc:oc + 1]),
                            wait=[(S.pe, mile)] +
                                 ([(S.dve, h_all)] if og == 0 and ts == 0 else []),
                            inc=[(S.act, 1)])
                        psg_free[bank] = S.act.cnt
                        psg_free_eng[bank] = S.act
                gate_done[s_g] = S.pe.cnt
                p.wait("pe", (swgu[s_u], wu_ready[s_u]))
                for ts in range(2):
                    for j in range(4):
                        oc = og * 4 + j
                        bank = (ts * 4 + j) % 3
                        p.wait("pe", (psg_free_eng[bank], psg_free[bank]))
                        for c in range(NXC):
                            p.emit("pe", lambda e, c=c, j=j, ts=ts, bank=bank, s_u=s_u: e.matmul(
                                psg[bank][:],
                                wgu[s_u][:, c * 512 + j * 128:c * 512 + (j + 1) * 128],
                                h[c][:, ts * 512:(ts + 1) * 512],
                                start=(c == 0), stop=(c == NXC - 1)),
                                inc=[(S.pe, 1)] if c == NXC - 1 else [])
                        mile = S.pe.cnt
                        acnt = S.act.cnt
                        p.emit("dve", lambda e, j=j, ts=ts, oc=oc, bank=bank: e.scalar_tensor_tensor(
                            gf[:, 4096 + j * 1024 + ts * 512:4096 + j * 1024 + (ts + 1) * 512],
                            psg[bank][:], env.su_sb[:, oc:oc + 1],
                            gf[:, j * 1024 + ts * 512:j * 1024 + (ts + 1) * 512],
                            ALU.mult, ALU.mult),
                            wait=[(S.pe, mile), (S.act, acnt)] +
                                 ([(S.dve, h_all)] if og == 0 and ts == 0 else []),
                            inc=[(S.dve, 1)])
                        psg_free[bank] = S.dve.cnt
                        psg_free_eng[bank] = S.dve
                up_done[s_u] = S.pe.cnt
                f_ready = S.dve.cnt
                f_done_prev = f_ready
                if og + 1 < NOG:
                    wg_ready[1 - s_g] = load_wgu(1 - s_g, W["wg"], og + 1,
                                                 (S.pe, up_done[s_u]))
                    wu_ready[1 - s_u] = load_wgu(1 - s_u, W["wu"], og + 1,
                                                 (S.pe, gate_done[s_g]))
                p.wait("pe", (S.dve, f_ready))
                for o2h in range(2):
                    u = og * 2 + o2h
                    wsl = u % 2
                    if u + 1 < 2 * NOG:
                        wd_ready[1 - wsl] = load_wd(
                            1 - wsl, (u + 1) // 2, (u + 1) % 2,
                            (S.pe, wd_mm_done[1 - wsl]))
                    p.wait("pe", (swd[wsl], wd_ready[wsl]))
                    for o2i in range(8):
                        o2c = o2h * 8 + o2i
                        for ts in range(2):
                            bank = (o2i * 2 + ts) % 2
                            p.wait("pe", (S.dve, psy_free[bank]))
                            for j in range(4):
                                p.emit("pe", lambda e, j=j, o2i=o2i, ts=ts, bank=bank, wsl=wsl: e.matmul(
                                    psy[bank][:],
                                    wd[wsl][:, j * 1024 + o2i * 128:j * 1024 + (o2i + 1) * 128],
                                    gf[:, 4096 + j * 1024 + ts * 512:4096 + j * 1024 + (ts + 1) * 512],
                                    start=(j == 0), stop=(j == 3)),
                                    inc=[(S.pe, 1)] if j == 3 else [])
                            mile = S.pe.cnt
                            if og == 0:
                                p.emit("dve", lambda e, o2c=o2c, ts=ts, bank=bank: e.tensor_copy(
                                    y[o2c][:, ts * 512:(ts + 1) * 512], psy[bank][:]),
                                    wait=[(S.pe, mile)], inc=[(S.dve, 1)])
                            else:
                                p.emit("dve", lambda e, o2c=o2c, ts=ts, bank=bank: e.tensor_add(
                                    y[o2c][:, ts * 512:(ts + 1) * 512],
                                    y[o2c][:, ts * 512:(ts + 1) * 512], psy[bank][:]),
                                    wait=[(S.pe, mile)], inc=[(S.dve, 1)])
                            psy_free[bank] = S.dve.cnt
                    wd_mm_done[wsl] = S.pe.cnt
            xg = env.rg.setdefault("epixg", [(None, 0)] * 4)
            for o2c in range(NXC):
                for ts in range(2):
                    sl = (o2c * 2 + ts) % 4
                    gs, gv = xg[sl]
                    p.emit("sp", lambda e, o2c=o2c, ts=ts, sl=sl, t0=t0: e.dma_start(
                        xfr[sl][:], x_in[o2c * 128:(o2c + 1) * 128,
                                         t0 + ts * 512:t0 + (ts + 1) * 512]),
                        wait=([(gs, gv)] if gs else []), inc=[(sxr[sl], 16)])
                    xcnt = sxr[sl].cnt
                    p.emit("dve", lambda e, o2c=o2c, ts=ts, sl=sl: e.scalar_tensor_tensor(
                        xnw[sl][:], y[o2c][:, ts * 512:(ts + 1) * 512],
                        env.sdh_sb[:, o2c:o2c + 1], xfr[sl][:], ALU.mult, ALU.add),
                        wait=[(sxr[sl], xcnt), (sxn[sl], sxn[sl].cnt)],
                        inc=[(S.dve, 1)])
                    xg[sl] = (S.dve, S.dve.cnt)
                    dcnt = S.dve.cnt
                    p.emit("act", lambda e, o2c=o2c, ts=ts, sl=sl, t0=t0: e.dma_start(
                        x_out[o2c * 128:(o2c + 1) * 128,
                              t0 + ts * 512:t0 + (ts + 1) * 512], xnw[sl][:]),
                        wait=[(S.dve, dcnt)], inc=[(sxn[sl], 16)])
            barrier(env)

    _half(0)
    _half(1)


def emit_attn(env, x_in, x_out, cfg):
    nc, p = env.nc, env.p
    S = env.s
    swqk = [p.dsem(f"wqk{i}") for i in range(2)]
    swv = [p.dsem(f"wv{i}") for i in range(2)]
    sksp = [p.dsem(f"ksp{i}") for i in range(2)]
    svsp = [p.dsem(f"vsp{i}") for i in range(2)]
    skb = [p.dsem(f"kb{i}") for i in range(2)]
    svb = [p.dsem(f"vb{i}") for i in range(2)]
    sqb = [p.dsem(f"qb{i}") for i in range(2)]
    swo = [p.dsem(f"wo{i}") for i in range(2)]
    sxr = [p.dsem(f"xr{i}") for i in range(4)]
    sxn = [p.dsem(f"xn{i}") for i in range(4)]
    with ExitStack() as slst:
        sbl = lambda n, shp, dt: slst.enter_context(nc.sbuf_tensor(n, shp, dt))
        uid = f"a{env.uid}"
        env.uid += 1
        qres = [sbl(f"{uid}q{i}", [128, T], BF16) for i in range(NH)]
        if cfg["l0_bias"]:
            bqkv = sbl(f"{uid}bqkv", [128, 48], F32)
            bqs = sbl(f"{uid}bqs", [128, 16], F32)
            bout = sbl(f"{uid}bout", [128, 16], F32)
            p.emit("act", lambda e: e.dma_start(bqkv[:], cfg["bqkv"]),
                   inc=[(S.dact, 16)])
            p.emit("act", lambda e: e.dma_start(bout[:], cfg["bout"]),
                   inc=[(S.dact, 16)])
            bv_cnt = S.dact.cnt
            for j in range(16):
                p.emit("dve", lambda e, j=j: e.tensor_scalar_mul(
                    bqs[:, j:j + 1], bqkv[:, 3 * j:3 * j + 1], ISQ),
                    wait=[(S.dact, bv_cnt)] if j == 0 else [], inc=[(S.dve, 1)])
            bqs_done = S.dve.cnt

        with ExitStack() as st:
            sb = lambda n, shp, dt: st.enter_context(nc.sbuf_tensor(n, shp, dt))
            psa = lambda n, shp, dt: st.enter_context(nc.psum_tensor(n, shp, dt))
            h = [sb(f"{uid}h{i}", [128, T], BF16) for i in range(NXC)]
            wqk = [sb(f"{uid}wqk{r}", [128, NXC * 256], BF16) for r in range(2)]
            wv = [sb(f"{uid}wv{r}", [128, NXC * 512], BF16) for r in range(2)]
            ksb = [sb(f"{uid}ksb0", [128, T], BF16)]
            vsb = [sb(f"{uid}vsb{r}", [128, 512], BF16) for r in range(2)]
            psq = [psa(f"{uid}psq{r}", [128, 512], F32) for r in range(2)]

            def load_wqk(slot, hh, guard):
                for (c0, wd_, d0) in cfg["qk_slices"](hh):
                    p.emit("pool", lambda e, slot=slot, c0=c0, wd_=wd_, d0=d0: e.dma_start(
                        bass.AP(wqk[slot], d0,
                                [[NXC * 256, 128], [256, NXC], [1, wd_]]),
                        bass.AP(cfg["wqkv"].tensor, c0,
                                [[3 * D, 128], [128 * 3 * D, NXC], [1, wd_]])),
                        wait=[guard] if (d0 == 0 and guard[1] > 0) else [],
                        inc=[(swqk[slot], 16)])
                return swqk[slot].cnt

            def load_wv(slot, g, guard):
                for (c0, wd_, d0) in cfg["v_slices"](g):
                    p.emit("pool", lambda e, slot=slot, c0=c0, wd_=wd_, d0=d0: e.dma_start(
                        bass.AP(wv[slot], d0,
                                [[NXC * 512, 128], [512, NXC], [1, wd_]]),
                        bass.AP(cfg["wqkv"].tensor, c0,
                                [[3 * D, 128], [128 * 3 * D, NXC], [1, wd_]])),
                        wait=[guard] if (d0 == 0 and guard[1] > 0) else [],
                        inc=[(swv[slot], 16)])
                return swv[slot].cnt

            wqk_ready = [0, 0]
            qk_done = [0, 0]
            psq_free = [0, 0]
            ksp_g = [0, 0]
            vsp_g = [0, 0]
            wqk_ready[0] = load_wqk(0, 0, (S.pe, 0))
            h_sub = emit_ln(env, x_in, 0, 1024, h, 0, wv[0])
            h_sub += emit_ln(env, x_in, 1024, 1024, h, 1024, wv[0])
            h_all = h_sub[-1]
            for hh in range(NH):
                sl = hh % 2
                if hh + 1 < NH:
                    wqk_ready[1 - sl] = load_wqk(1 - sl, hh + 1, (S.pe, qk_done[1 - sl]))
                p.wait("pe", (swqk[sl], wqk_ready[sl]))
                for part in range(2):
                    for ts in range(4):
                        bank = ts % 2
                        p.wait("pe", (S.dve, psq_free[bank]), (S.dve, h_sub[ts]))
                        for c in range(NXC):
                            p.emit("pe", lambda e, c=c, sl=sl, part=part, ts=ts, bank=bank: e.matmul(
                                psq[bank][:],
                                wqk[sl][:, c * 256 + part * 128:c * 256 + (part + 1) * 128],
                                h[c][:, ts * 512:(ts + 1) * 512],
                                start=(c == 0), stop=(c == NXC - 1)),
                                inc=[(S.pe, 1)] if c == NXC - 1 else [])
                        mile = S.pe.cnt
                        if part == 0:
                            if cfg["l0_bias"]:
                                p.emit("dve", lambda e, hh=hh, ts=ts, bank=bank: e.tensor_scalar(
                                    qres[hh][:, ts * 512:(ts + 1) * 512], psq[bank][:],
                                    ISQ, bqs[:, hh:hh + 1], ALU.mult, ALU.add),
                                    wait=[(S.pe, mile), (S.dve, bqs_done)],
                                    inc=[(S.dve, 1)])
                            else:
                                p.emit("dve", lambda e, hh=hh, ts=ts, bank=bank: e.tensor_scalar_mul(
                                    qres[hh][:, ts * 512:(ts + 1) * 512], psq[bank][:],
                                    ISQ), wait=[(S.pe, mile)], inc=[(S.dve, 1)])
                        else:
                            wk = [(S.pe, mile)]
                            if ts == 0:
                                wk.append((sksp[0], ksp_g[0]))
                            if cfg["l0_bias"]:
                                p.emit("dve", lambda e, sl=sl, ts=ts, bank=bank, hh=hh: e.tensor_scalar_add(
                                    ksb[0][:, ts * 512:(ts + 1) * 512], psq[bank][:],
                                    bqkv[:, 3 * hh + 1:3 * hh + 2]),
                                    wait=wk, inc=[(S.dve, 1)])
                            else:
                                p.emit("dve", lambda e, sl=sl, ts=ts, bank=bank: e.tensor_copy(
                                    ksb[0][:, ts * 512:(ts + 1) * 512], psq[bank][:]),
                                    wait=wk, inc=[(S.dve, 1)])
                        psq_free[bank] = S.dve.cnt
                qk_done[sl] = S.pe.cnt
                kcnt = S.dve.cnt
                p.emit("sp", lambda e, hh=hh, sl=sl: e.dma_start(
                    env.Ks[hh * 128:(hh + 1) * 128, :], ksb[0][:]),
                    wait=[(S.dve, kcnt)], inc=[(sksp[0], 16)])
                ksp_g[0] = sksp[0].cnt
            wv_ready = [0, 0]
            v_done = [0, 0]
            p.wait("pool", (S.dve, h_all))
            wv_ready[0] = load_wv(0, 0, (S.pe, 0))
            for g in range(4):
                sl = g % 2
                if g + 1 < 4:
                    wv_ready[1 - sl] = load_wv(1 - sl, g + 1, (S.pe, v_done[1 - sl]))
                p.wait("pe", (swv[sl], wv_ready[sl]))
                for tc in range(NXC):
                    bank = tc % 2
                    p.wait("pe", (S.dve, psq_free[bank]), (S.dve, h_sub[tc // 4]))
                    for c in range(NXC):
                        p.emit("pe", lambda e, c=c, sl=sl, tc=tc, bank=bank: e.matmul(
                            psq[bank][:], h[c][:, tc * 128:(tc + 1) * 128],
                            wv[sl][:, c * 512:(c + 1) * 512],
                            start=(c == 0), stop=(c == NXC - 1)),
                            inc=[(S.pe, 1)] if c == NXC - 1 else [])
                    mile = S.pe.cnt
                    vsl = tc % 2
                    p.emit("dve", lambda e, vsl=vsl, bank=bank: e.tensor_copy(
                        vsb[vsl][:], psq[bank][:]),
                        wait=[(S.pe, mile), (svsp[vsl], vsp_g[vsl])],
                        inc=[(S.dve, 1)])
                    psq_free[bank] = S.dve.cnt
                    vcnt = S.dve.cnt
                    p.emit("sp", lambda e, tc=tc, g=g, vsl=vsl: e.dma_start(
                        env.Vs[tc * 128:(tc + 1) * 128, g * 512:(g + 1) * 512],
                        vsb[vsl][:]), wait=[(S.dve, vcnt)], inc=[(svsp[vsl], 16)])
                    vsp_g[vsl] = svsp[vsl].cnt
                v_done[sl] = S.pe.cnt
            barrier(env)

        st23 = slst.enter_context(ExitStack())
        ctx = [st23.enter_context(nc.sbuf_tensor(f"{uid}c{i}", [128, T], BF16))
               for i in range(NH)]

        with ExitStack() as st:
            sb = lambda n, shp, dt: st.enter_context(nc.sbuf_tensor(n, shp, dt))
            psa = lambda n, shp, dt: st.enter_context(nc.psum_tensor(n, shp, dt))
            kbuf = [sb(f"{uid}kb{r}", [128, T], BF16) for r in range(2)]
            vbuf = [sb(f"{uid}vb{r}", [128, T], BF16) for r in range(2)]
            eTg = sb(f"{uid}eT", [128, NXC * 512], BF16)
            negc = sb(f"{uid}negc", [128, 1], F32)
            rzrow = sb(f"{uid}rz", [1, 512], F32)
            zb = sb(f"{uid}zb", [128, 512], F32)
            tx = sb(f"{uid}tx", [128, 512], F32)
            psS = [psa(f"{uid}psS{r}", [128, 1024], F32) for r in range(2)]
            psZ = psa(f"{uid}psZ", [1, 512], F32)
            psB = psa(f"{uid}psB", [128, 512], F32)
            psC = psa(f"{uid}psC", [128, 512], F32)
            sneg = p.dsem(f"{uid}neg")
            p.emit("act", lambda e: e.dma_start(negc[:], env.negc_d),
                   inc=[(sneg, 16)])
            negc_cnt = sneg.cnt
            if cfg["causal"]:
                anot = sb(f"{uid}anot", [128, 4 * 512], U8)
                n10k = sb(f"{uid}n10k", [128, 512], F32)
                for i in range(4):
                    p.emit("act", lambda e, i=i: e.dma_start(
                        anot[:, i * 512:(i + 1) * 512],
                        env.anot_d[i * 128:(i + 1) * 128, :]), inc=[(S.dact, 16)])
                p.emit("act", lambda e: e.dma_start(n10k[:], env.n10k_d),
                       inc=[(S.dact, 16)])
                mask_cnt = S.dact.cnt
                p.wait("dve", (S.dact, mask_cnt))

            kv_ready = [0, 0]
            head_done = [0, 0]

            def load_kv(slot, hh, guard):
                p.emit("act", lambda e, hh=hh, slot=slot: e.dma_start(
                    kbuf[slot][:], env.Ks[hh * 128:(hh + 1) * 128, :]),
                    wait=[guard] if guard[1] > 0 else [], inc=[(skb[slot], 16)])
                vc0 = (hh // 4) * 512 + (hh % 4) * 128
                for tc in range(NXC):
                    p.emit("act", lambda e, tc=tc, slot=slot, vc0=vc0: e.dma_start(
                        vbuf[slot][:, tc * 128:(tc + 1) * 128],
                        env.Vs[tc * 128:(tc + 1) * 128, vc0:vc0 + 128]),
                        inc=[(svb[slot], 16)])
                return (skb[slot].cnt, svb[slot].cnt)

            kv_ready[0] = load_kv(0, 0, (S.pe, 0))
            exp_free = [0, 0]
            eT_war = 0
            z_guard = 0
            b_guard = 0
            c_guard = 0
            for hh in range(NH):
                sl = hh % 2
                if hh + 1 < NH:
                    kv_ready[1 - sl] = load_kv(1 - sl, hh + 1,
                                               (S.pe, head_done[1 - sl]))
                p.wait("pe", (skb[sl], kv_ready[sl][0]), (svb[sl], kv_ready[sl][1]))
                for g in range(4):
                    nk = 4 * (g + 1) if cfg["causal"] else 16
                    for kp in range(nk // 2):
                        slot = kp % 2
                        p.wait("pe", (S.act, exp_free[slot]))
                        if kp == 0:
                            p.wait("pe", (S.dve, z_guard), (S.dve, c_guard))
                        for half2 in range(2):
                            kc = kp * 2 + half2
                            p.emit("pe", lambda e, hh=hh, g=g, kc=kc, sl=sl, slot=slot, half2=half2: e.matmul(
                                psS[slot][:, half2 * 512:(half2 + 1) * 512],
                                kbuf[sl][:, kc * 128:(kc + 1) * 128],
                                qres[hh][:, g * 512:(g + 1) * 512],
                                start=True, stop=True),
                                inc=[(S.pe, 1)] if half2 == 1 else [])
                        mileb = S.pe.cnt
                        wexp = [(S.pe, mileb)]
                        if hh == 0 and g == 0 and kp == 0:
                            wexp.append((sneg, negc_cnt))
                        if kp == 0:
                            wexp.append((S.pe, eT_war))
                        if cfg["causal"] and kp * 2 + 1 >= 4 * g:
                            for half2 in range(2):
                                kc = kp * 2 + half2
                                if kc >= 4 * g:
                                    ci = kc - 4 * g
                                    p.emit("dve", lambda e, ci=ci, slot=slot, half2=half2: e.copy_predicated(
                                        psS[slot][:, half2 * 512:(half2 + 1) * 512],
                                        anot[:, ci * 512:(ci + 1) * 512], n10k[:]),
                                        wait=[(S.pe, mileb), (S.dve, S.dve.cnt)],
                                        inc=[(S.dve, 1)])
                            wexp.append((S.dve, S.dve.cnt))
                        p.emit("act", lambda e, kp=kp, slot=slot: e.activation(
                            eTg[:, kp * 1024:(kp + 1) * 1024], psS[slot][:],
                            AF.Exp, bias=negc[:, 0:1]),
                            wait=wexp, inc=[(S.act, 1)])
                        exp_free[slot] = S.act.cnt
                        p.wait("pe", (S.act, exp_free[slot]))
                        for half2 in range(2):
                            kc = kp * 2 + half2
                            p.emit("pe", lambda e, kc=kc, nk=nk: e.matmul(
                                psZ[:], env.onesc_bf[:],
                                eTg[:, kc * 512:(kc + 1) * 512],
                                start=(kc == 0), stop=(kc == nk - 1)),
                                inc=[(S.pe, 1)] if kc == nk - 1 else [])
                            p.emit("pe", lambda e, kc=kc, nk=nk, sl=sl: e.matmul(
                                psC[:], vbuf[sl][:, kc * 128:(kc + 1) * 128],
                                eTg[:, kc * 512:(kc + 1) * 512],
                                start=(kc == 0), stop=(kc == nk - 1)),
                                inc=[(S.pe, 1)] if kc == nk - 1 else [])
                    zc_mile = S.pe.cnt
                    eT_war = zc_mile
                    p.emit("dve", lambda e: e.reciprocal(rzrow[:], psZ[0:1, :]),
                           wait=[(S.pe, zc_mile), (S.dve, S.dve.cnt)],
                           inc=[(S.dve, 1)])
                    z_guard = S.dve.cnt
                    p.emit("pe", lambda e: e.matmul(psB[:], env.onesr_f[:],
                                                    rzrow[0:1, :],
                                                    start=True, stop=True),
                           wait=[(S.dve, z_guard), (S.dve, b_guard)],
                           inc=[(S.pe, 1)])
                    bmile = S.pe.cnt
                    p.emit("dve", lambda e: e.tensor_copy(zb[:], psB[:]),
                           wait=[(S.pe, bmile)], inc=[(S.dve, 1)])
                    b_guard = S.dve.cnt
                    if cfg["l0_bias"]:
                        p.emit("dve", lambda e: e.tensor_mul(tx[:], psC[:], zb[:]),
                               wait=[(S.dve, S.dve.cnt)], inc=[(S.dve, 1)])
                        p.emit("dve", lambda e, hh=hh, g=g: e.tensor_scalar_add(
                            ctx[hh][:, g * 512:(g + 1) * 512], tx[:],
                            bqkv[:, 3 * hh + 2:3 * hh + 3]),
                            wait=[(S.dve, S.dve.cnt)], inc=[(S.dve, 1)])
                    else:
                        p.emit("dve", lambda e, hh=hh, g=g: e.tensor_mul(
                            ctx[hh][:, g * 512:(g + 1) * 512], psC[:], zb[:]),
                            wait=[(S.dve, S.dve.cnt)], inc=[(S.dve, 1)])
                    c_guard = S.dve.cnt
                head_done[sl] = S.pe.cnt
            barrier(env)

        with ExitStack() as st:
            sb = lambda n, shp, dt: st.enter_context(nc.sbuf_tensor(n, shp, dt))
            psa = lambda n, shp, dt: st.enter_context(nc.psum_tensor(n, shp, dt))
            wo = [sb(f"{uid}wo{r}", [128, NXC * 1024], BF16) for r in range(2)]
            xfr = [sb(f"{uid}xr{i}", [128, 512], F32) for i in range(2)]
            xnw = [sb(f"{uid}xn{i}", [128, 512], F32) for i in range(2)]
            pso = [psa(f"{uid}pso{r}", [128, 512], F32) for r in range(2)]

            def load_wo(slot, ocg):
                p.emit("pool", lambda e, slot=slot, ocg=ocg: e.dma_start(
                    bass.AP(wo[slot], 0, [[NXC * 1024, 128], [1024, NXC], [1, 1024]]),
                    bass.AP(cfg["wout"].tensor, ocg * 1024,
                            [[D, 128], [128 * D, NXC], [1, 1024]])),
                    inc=[(swo[slot], 16)])
                return swo[slot].cnt

            wo_ready = [load_wo(0, 0), load_wo(1, 1)]
            pso_free = [0, 0]
            xg = [(None, 0)] * 2
            for ocg in range(2):
                sl = ocg
                p.wait("pe", (swo[sl], wo_ready[sl]))
                for oi in range(8):
                    oc = ocg * 8 + oi
                    for ts in range(4):
                        bank = (oi * 4 + ts) % 2
                        p.wait("pe", (S.dve, pso_free[bank]))
                        for c in range(NXC):
                            p.emit("pe", lambda e, c=c, sl=sl, oi=oi, ts=ts, bank=bank: e.matmul(
                                pso[bank][:],
                                wo[sl][:, c * 1024 + oi * 128:c * 1024 + (oi + 1) * 128],
                                ctx[c][:, ts * 512:(ts + 1) * 512],
                                start=(c == 0), stop=(c == NXC - 1)),
                                inc=[(S.pe, 1)] if c == NXC - 1 else [])
                        mile = S.pe.cnt
                        sl2 = (oi * 4 + ts) % 2
                        gs, gv = xg[sl2]
                        p.emit("sp", lambda e, oc=oc, ts=ts, sl2=sl2: e.dma_start(
                            xfr[sl2][:], x_in[oc * 128:(oc + 1) * 128,
                                             ts * 512:(ts + 1) * 512]),
                            wait=([(gs, gv)] if gs else []), inc=[(sxr[sl2], 16)])
                        xcnt = sxr[sl2].cnt
                        if cfg["l0_bias"]:
                            p.emit("dve", lambda e, oc=oc, bank=bank, sl2=sl2: e.scalar_tensor_tensor(
                                xnw[sl2][:], pso[bank][:], bout[:, oc:oc + 1],
                                xfr[sl2][:], ALU.add, ALU.add),
                                wait=[(S.pe, mile), (sxr[sl2], xcnt),
                                      (sxn[sl2], sxn[sl2].cnt)],
                                inc=[(S.dve, 1)])
                        else:
                            p.emit("dve", lambda e, bank=bank, sl2=sl2: e.tensor_add(
                                xnw[sl2][:], pso[bank][:], xfr[sl2][:]),
                                wait=[(S.pe, mile), (sxr[sl2], xcnt),
                                      (sxn[sl2], sxn[sl2].cnt)],
                                inc=[(S.dve, 1)])
                        xg[sl2] = (S.dve, S.dve.cnt)
                        pso_free[bank] = S.dve.cnt
                        dcnt = S.dve.cnt
                        p.emit("sp", lambda e, oc=oc, ts=ts, sl2=sl2: e.dma_start(
                            x_out[oc * 128:(oc + 1) * 128, ts * 512:(ts + 1) * 512],
                            xnw[sl2][:]), wait=[(S.dve, dcnt)], inc=[(sxn[sl2], 16)])
            barrier(env)


def emit_moe(env, x_in, x_out, W):
    nc, p = env.nc, env.p
    S = env.s
    swe = [p.dsem(f"we{i}") for i in range(2)]
    sce = [p.dsem(f"ce{i}") for i in range(2)]
    sxr = [p.dsem(f"xr{i}") for i in range(4)]
    sxn = [p.dsem(f"xn{i}") for i in range(4)]

    def _half(half):
        t0 = half * 1024
        with ExitStack() as st:
            sb = lambda n, shp, dt: st.enter_context(nc.sbuf_tensor(n, shp, dt))
            psa = lambda n, shp, dt: st.enter_context(nc.psum_tensor(n, shp, dt))
            uid = f"m{env.uid}"
            env.uid += 1
            h = [sb(f"{uid}h{i}", [128, 1024], BF16) for i in range(NXC)]
            b_sb = sb(f"{uid}bex", [8, D], F32)
            p.emit("act", lambda e: e.dma_start(b_sb[:], W["bex"]), inc=[(S.dact, 16)])
            bex_cnt = S.dact.cnt
            raw = sb(f"{uid}raw", [128, 64], F32)
            mtok = sb(f"{uid}mt", [128, 8], F32)
            rtok = sb(f"{uid}rt", [128, 8], F32)
            lg = sb(f"{uid}lg", [128, 8], F32)
            lgt = sb(f"{uid}lgt", [128, 8], F32)
            p8 = sb(f"{uid}p8", [128, 8], F32)
            msk = sb(f"{uid}msk", [128, 8], F32)
            rstat = sb(f"{uid}rs", [128, 3], F32)
            c_sb = sb(f"{uid}csb", [128, 64], F32)
            cT = sb(f"{uid}cT", [8, 1024], F32)
            ce0 = [sb(f"{uid}ce{r}", [1, 1024], F32) for r in range(2)]
            cb = [sb(f"{uid}cb{r}", [128, 1024], F32) for r in range(2)]
            hs = [sb(f"{uid}hs{i}", [128, 1024], BF16) for i in range(NXC)]
            y = [sb(f"{uid}y{i}", [128, 1024], F32) for i in range(NXC)]
            weF = sb(f"{uid}weF", [128, 8192], BF16)
            xnw = [sb(f"{uid}xn{i}", [128, 512], F32) for i in range(3)]
            xfr = [sb(f"{uid}xr{i}", [128, 512], F32) for i in range(3)]
            def load_we(slot, ee, og2, guard):
                p.emit("pool", lambda e, slot=slot, ee=ee, og2=og2: e.dma_start(
                    bass.AP(weF, slot * 4096,
                            [[8192, 128], [256, NXC], [1, 256]]),
                    bass.AP(W["wt"].tensor, ee * D * D + og2 * 256,
                            [[D, 128], [128 * D, NXC], [1, 256]])),
                    wait=[guard] if guard[1] > 0 else [],
                    inc=[(swe[slot], 16)])
                return swe[slot].cnt

            router = {"wg_sb": env.wg_sb, "raw": raw, "mtok": mtok,
                      "rtok": rtok, "t0": t0}
            h_subm = emit_ln(env, x_in, t0, 1024, h, 0, weF, router=router)
            h_done = S.dve.cnt
            we_ready = [0, 0]
            we_ready[0] = load_we(0, 0, 0, (S.dve, h_subm[-1]))
            we_ready[1] = load_we(1, 0, 1, (S.dve, h_subm[-1]))

            psT8 = psa(f"{uid}psT8", [8, 128], F32)
            ps_cb = psa(f"{uid}pscb", [128, 1024], F32)
            psy = [psa(f"{uid}psy{r}", [128, 512], F32) for r in range(2)]

            raw_done = S.dve.cnt
            for tc in range(8):
                p.emit("dve", lambda e, tc=tc: e.tensor_mul(
                    rstat[:, 0:1], rtok[:, tc:tc + 1], mtok[:, tc:tc + 1]),
                    wait=[(S.dve, S.dve.cnt)], inc=[(S.dve, 1)])
                p.emit("dve", lambda e, tc=tc: e.tensor_scalar_mul(
                    lg[:], raw[:, tc * 8:(tc + 1) * 8], rtok[:, tc:tc + 1]),
                    wait=[(S.dve, S.dve.cnt)], inc=[(S.dve, 1)])
                p.emit("dve", lambda e: e.tensor_scalar_mul(
                    lgt[:], env.wgsum_b[:], rstat[:, 0:1]),
                    wait=[(S.dve, S.dve.cnt)], inc=[(S.dve, 1)])
                p.emit("dve", lambda e: e.tensor_sub(lg[:], lg[:], lgt[:]),
                       wait=[(S.dve, S.dve.cnt)], inc=[(S.dve, 1)])
                p.emit("dve", lambda e: e.reduce_max(
                    rstat[:, 1:2], lg[:], axis=AX.X, negate=True),
                    wait=[(S.dve, S.dve.cnt)], inc=[(S.dve, 1)])
                l_cnt = S.dve.cnt
                p.emit("act", lambda e: e.activation(
                    p8[:], lg[:], AF.Exp, bias=rstat[:, 1:2],
                    accum_out=rstat[:, 2:3]),
                    wait=[(S.dve, l_cnt)], inc=[(S.act, 1)])
                e_cnt = S.act.cnt
                p.emit("dve", lambda e: e.tensor_scalar(
                    lgt[:], lg[:], rstat[:, 1:2], None, ALU.add),
                    wait=[(S.dve, S.dve.cnt)], inc=[(S.dve, 1)])
                p.emit("dve", lambda e: e.tensor_single_scalar(
                    msk[:], lgt[:], 0.0, ALU.is_equal),
                    wait=[(S.dve, S.dve.cnt)], inc=[(S.dve, 1)])
                p.emit("dve", lambda e: e.reciprocal(rstat[:, 0:1], rstat[:, 2:3]),
                       wait=[(S.act, e_cnt), (S.dve, S.dve.cnt)], inc=[(S.dve, 1)])
                p.emit("dve", lambda e: e.tensor_mul(p8[:], p8[:], msk[:]),
                       wait=[(S.dve, S.dve.cnt)], inc=[(S.dve, 1)])
                p.emit("dve", lambda e, tc=tc: e.tensor_scalar(
                    c_sb[:, tc * 8:(tc + 1) * 8], p8[:], rstat[:, 0:1], 0.5,
                    ALU.mult, ALU.mult),
                    wait=[(S.dve, S.dve.cnt)], inc=[(S.dve, 1)])
                c_cnt = S.dve.cnt
                p.emit("pe", lambda e, tc=tc: e.transpose(
                    psT8[:], c_sb[:, tc * 8:(tc + 1) * 8], env.idn_f[:]),
                    wait=[(S.dve, c_cnt)], inc=[(S.pe, 1)])
                t_cnt = S.pe.cnt
                p.emit("dve", lambda e, tc=tc: e.tensor_copy(
                    cT[0:8, tc * 128:(tc + 1) * 128], psT8[0:8, :]),
                    wait=[(S.pe, t_cnt)], inc=[(S.dve, 1)])
            cT_done = S.dve.cnt

            og_done = [0, 0]
            psy_free = [0, 0]
            cb_ready = [0, 0]
            hs_ready = 0
            ce_g = [(None, 0)] * 2
            cb_used = [0, 0]
            cb_used_pool = [0, 0]
            ps_cb_guard = [0]

            def prep_expert(ee):
                sl = ee % 2
                gs, gv = ce_g[sl]
                p.emit("sp", lambda e, ee=ee, sl=sl: e.dma_start(
                    ce0[sl][:], cT[ee:ee + 1, :]),
                    wait=[(S.dve, cT_done)] + ([(gs, gv)] if gs else []),
                    inc=[(sce[sl], 16)])
                ccnt = sce[sl].cnt
                for half2 in range(2):
                    p.emit("pe", lambda e, sl=sl, half2=half2: e.matmul(
                        ps_cb[:, half2 * 512:(half2 + 1) * 512], env.onesr_f[:],
                        ce0[sl][0:1, half2 * 512:(half2 + 1) * 512],
                        start=True, stop=True),
                        wait=([(sce[sl], ccnt), (S.dve, cb_used[sl]),
                               (S.pool, cb_used_pool[sl]),
                               (S.dve, ps_cb_guard[0])] if half2 == 0 else []),
                        inc=[(S.pe, 1)])
                bcnt = S.pe.cnt
                p.emit("dve", lambda e, sl=sl: e.tensor_copy(cb[sl][:], ps_cb[:]),
                       wait=[(S.pe, bcnt)], inc=[(S.dve, 1)])
                ce_g[sl] = (S.pe, bcnt)
                cb_ready[sl] = S.dve.cnt
                ps_cb_guard[0] = S.dve.cnt

            prep_expert(0)
            for ee in range(8):
                sl = ee % 2
                p.wait("dve", (S.dve, cb_ready[sl]), (S.pe, hs_ready))
                p.wait("pool", (S.dve, cb_ready[sl]), (S.pe, hs_ready))
                for c in range(NXC):
                    p.emit("dve", lambda e, c=c, sl=sl: e.tensor_mul(
                        hs[c][:], h[c][:], cb[sl][:]),
                        wait=[(S.dve, h_done)] if c == 0 else [], inc=[(S.dve, 1)])
                hs_dve, hs_pool = S.dve.cnt, S.pool.cnt
                cb_used[sl] = S.dve.cnt
                cb_used_pool[sl] = S.pool.cnt
                if ee + 1 < 8:
                    prep_expert(ee + 1)
                p.wait("pe", (S.dve, hs_dve), (S.pool, hs_pool))
                for og2 in range(8):
                    unit = ee * 8 + og2
                    wsl = unit % 2
                    if unit + 1 < 64 and unit + 1 >= 2:
                        we_ready[1 - wsl] = load_we(1 - wsl, (unit + 1) // 8,
                                                    (unit + 1) % 8,
                                                    (S.pe, og_done[1 - wsl]))
                    p.wait("pe", (swe[wsl], we_ready[wsl]))
                    for oi in range(2):
                        oc = og2 * 2 + oi
                        for ts in range(2):
                            bank = (oi * 2 + ts) % 2
                            p.wait("pe", (S.dve, psy_free[bank]))
                            for c in range(NXC):
                                p.emit("pe", lambda e, c=c, wsl=wsl, oi=oi, ts=ts, bank=bank: e.matmul(
                                    psy[bank][:],
                                    weF[:, wsl * 4096 + c * 256 + oi * 128:
                                        wsl * 4096 + c * 256 + (oi + 1) * 128],
                                    hs[c][:, ts * 512:(ts + 1) * 512],
                                    start=(c == 0), stop=(c == NXC - 1)),
                                    inc=[(S.pe, 1)] if c == NXC - 1 else [])
                            mile = S.pe.cnt
                            if ee == 0:
                                p.emit("dve", lambda e, oc=oc, ts=ts, bank=bank: e.tensor_copy(
                                    y[oc][:, ts * 512:(ts + 1) * 512], psy[bank][:]),
                                    wait=[(S.pe, mile)], inc=[(S.dve, 1)])
                            else:
                                p.emit("dve", lambda e, oc=oc, ts=ts, bank=bank: e.tensor_add(
                                    y[oc][:, ts * 512:(ts + 1) * 512],
                                    y[oc][:, ts * 512:(ts + 1) * 512], psy[bank][:]),
                                    wait=[(S.pe, mile)], inc=[(S.dve, 1)])
                            psy_free[bank] = S.dve.cnt
                    og_done[wsl] = S.pe.cnt
                hs_ready = S.pe.cnt
            for oc in range(NXC):
                for ts in range(2):
                    bank = (oc * 2 + ts) % 2
                    p.wait("pe", (S.dve, psy_free[bank]))
                    p.emit("pe", lambda e, oc=oc, ts=ts, bank=bank: e.matmul(
                        psy[bank][:], b_sb[0:8, oc * 128:(oc + 1) * 128],
                        cT[0:8, ts * 512:(ts + 1) * 512], start=True, stop=True),
                        wait=[(S.dact, bex_cnt)] if oc == 0 and ts == 0 else [],
                        inc=[(S.pe, 1)])
                    mile = S.pe.cnt
                    p.emit("dve", lambda e, oc=oc, ts=ts, bank=bank: e.tensor_add(
                        y[oc][:, ts * 512:(ts + 1) * 512],
                        y[oc][:, ts * 512:(ts + 1) * 512], psy[bank][:]),
                        wait=[(S.pe, mile)], inc=[(S.dve, 1)])
                    psy_free[bank] = S.dve.cnt
            xg = env.rg.setdefault("epixg", [(None, 0)] * 4)
            for oc in range(NXC):
                for ts in range(2):
                    sl = (oc * 2 + ts) % 3
                    gs, gv = xg[sl]
                    p.emit("sp", lambda e, oc=oc, ts=ts, sl=sl, t0=t0: e.dma_start(
                        xfr[sl][:], x_in[oc * 128:(oc + 1) * 128,
                                         t0 + ts * 512:t0 + (ts + 1) * 512]),
                        wait=([(gs, gv)] if gs else []), inc=[(sxr[sl], 16)])
                    xcnt = sxr[sl].cnt
                    p.emit("dve", lambda e, oc=oc, ts=ts, sl=sl: e.tensor_add(
                        xnw[sl][:], y[oc][:, ts * 512:(ts + 1) * 512], xfr[sl][:]),
                        wait=[(sxr[sl], xcnt), (sxn[sl], sxn[sl].cnt)],
                        inc=[(S.dve, 1)])
                    xg[sl] = (S.dve, S.dve.cnt)
                    dcnt = S.dve.cnt
                    p.emit("act", lambda e, oc=oc, ts=ts, sl=sl, t0=t0: e.dma_start(
                        x_out[oc * 128:(oc + 1) * 128,
                              t0 + ts * 512:t0 + (ts + 1) * 512], xnw[sl][:]),
                        wait=[(S.dve, dcnt)], inc=[(sxn[sl], 16)])
            barrier(env)

    _half(0)
    _half(1)


def build_group(sublayers):
    nc = bass.Bass("TRN2", target_bir_lowering=False)
    env = Env()
    env.nc = nc
    env.p = Prog(nc)
    env.uid = 0
    env.bar_snap = {}
    env.rg = {}
    p = env.p

    class Sems:
        pass
    S = Sems()
    for nm in ("pe", "act", "dve", "pool", "dsp", "dpool", "dact"):
        setattr(S, nm, p.sem(nm))
    env.s = S

    need = set(sublayers)
    x_in = nc.dram_tensor("xin", [D, T], F32, kind="ExternalInput").ap()
    x_out = nc.dram_tensor("xout", [D, T], F32, kind="ExternalOutput").ap()
    names = ["xin"]

    def din(name, shape, dt):
        names.append(name)
        return nc.dram_tensor(name, shape, dt, kind="ExternalInput").ap()

    onesc_bf_d = din("onescb", [128, 1], BF16)
    onesr_f_d = din("onesrf", [1, 128], F32)
    onesc_f_d = din("onescf", [128, 1], F32)

    w = {}
    if need & {0, 2}:
        w["wg"] = din("wg", [D, INTER], I8)
        w["wu"] = din("wu", [D, INTER], I8)
        w["wd"] = din("wd", [INTER, D], I8)
        w["sg"] = din("sgv", [128, 64], F32)
        w["su"] = din("suv", [128, 64], F32)
        w["sd"] = din("sdv", [128, 16], F32)
    if 1 in need:
        w["wqkv0"] = din("wqkv0", [D, 3 * D], F32)
        w["bqkv0"] = din("bqkv0", [128, 48], F32)
        w["wout0"] = din("wout0", [D, D], F32)
        w["bout0"] = din("bout0", [128, 16], F32)
        w["anot"] = din("anot", [4 * 128, 512], U8)
        w["n10k"] = din("n10k", [128, 512], F32)
    if 4 in need:
        w["wqkv1"] = din("wqkv1", [D, 3 * D], F32)
        w["wout1"] = din("wout1", [D, D], F32)
    if need & {3, 5}:
        w["wgr"] = din("wgr", [128, 128], F32)
        w["wt"] = din("wt", [8, D, D], F32)
        w["bex"] = din("bex", [8, D], F32)
        w["idnf"] = din("idnf", [128, 128], F32)
    if need & {1, 4}:
        w["negc"] = din("negc", [128, 1], F32)
        env.negc_d = w["negc"]
        env.Ks = nc.dram_tensor("ks_scr", [D, T], BF16, kind="Internal").ap()
        env.Vs = nc.dram_tensor("vs_scr", [T, D], BF16, kind="Internal").ap()

    nseq = len(sublayers)
    xs = [x_in]
    for i in range(nseq - 1):
        xs.append(nc.dram_tensor(f"xscr{i}", [D, T], F32, kind="Internal").ap())
    xs.append(x_out)

    stack = ExitStack()
    sb = lambda n, shp, dt: stack.enter_context(nc.sbuf_tensor(n, shp, dt))

    env.onesc_bf = sb("onescb_s", [128, 1], BF16)
    env.onesr_f = sb("onesrf_s", [1, 128], F32)
    env.onesc_f = sb("onescf_s", [128, 1], F32)
    p.emit("act", lambda e: e.dma_start(env.onesc_bf[:], onesc_bf_d), inc=[(S.dact, 16)])
    p.emit("act", lambda e: e.dma_start(env.onesr_f[:], onesr_f_d), inc=[(S.dact, 16)])
    p.emit("act", lambda e: e.dma_start(env.onesc_f[:], onesc_f_d), inc=[(S.dact, 16)])

    if need & {0, 2}:
        env.sg_sb = sb("sg_s", [128, 64], F32)
        env.su_sb = sb("su_s", [128, 64], F32)
        env.sdr_sb = sb("sdr_s", [128, 16], F32)
        env.sdh_sb = sb("sdh_s", [128, 16], F32)
        p.emit("act", lambda e: e.dma_start(env.sg_sb[:], w["sg"]), inc=[(S.dact, 16)])
        p.emit("act", lambda e: e.dma_start(env.su_sb[:], w["su"]), inc=[(S.dact, 16)])
        p.emit("act", lambda e: e.dma_start(env.sdr_sb[:], w["sd"]), inc=[(S.dact, 16)])
        sd_cnt = S.dact.cnt
        p.emit("dve", lambda e: e.tensor_scalar_mul(env.sdh_sb[:], env.sdr_sb[:], 0.5),
               wait=[(S.dact, sd_cnt)], inc=[(S.dve, 1)])
    if need & {3, 5}:
        env.wg_sb = sb("wgr_s", [128, 128], F32)
        env.idn_f = sb("idnf_s", [128, 128], F32)
        env.wgsum_b = sb("wgsb_s", [128, 8], F32)
        wgsum_r = sb("wgsr_s", [1, 8], F32)
        p.emit("act", lambda e: e.dma_start(env.wg_sb[:], w["wgr"]), inc=[(S.dact, 16)])
        p.emit("act", lambda e: e.dma_start(env.idn_f[:], w["idnf"]), inc=[(S.dact, 16)])
        wg_cnt = S.dact.cnt
        with ExitStack() as st0:
            ps_ws = st0.enter_context(nc.psum_tensor("ps_ws", [1, 8], F32))
            ps_wb = st0.enter_context(nc.psum_tensor("ps_wb", [128, 8], F32))
            for c in range(16):
                p.emit("pe", lambda e, c=c: e.matmul(
                    ps_ws[:], env.onesc_f[:], env.wg_sb[:, c * 8:(c + 1) * 8],
                    start=(c == 0), stop=(c == 15)),
                    wait=[(S.dact, wg_cnt)] if c == 0 else [], inc=[(S.pe, 1)])
            p.emit("dve", lambda e: e.tensor_copy(wgsum_r[:], ps_ws[:]),
                   wait=[(S.pe, S.pe.cnt)], inc=[(S.dve, 1)])
            wr_cnt = S.dve.cnt
            p.emit("pe", lambda e: e.matmul(ps_wb[:], env.onesr_f[:], wgsum_r[0:1, :],
                                            start=True, stop=True),
                   wait=[(S.dve, wr_cnt)], inc=[(S.pe, 1)])
            p.emit("dve", lambda e: e.tensor_copy(env.wgsum_b[:], ps_wb[:]),
                   wait=[(S.pe, S.pe.cnt)], inc=[(S.dve, 1)])
    if 1 in need:
        env.anot_d = w["anot"]
        env.n10k_d = w["n10k"]
    env.const_dact = S.dact.cnt

    cfg0 = {
        "wqkv": w.get("wqkv0"), "bqkv": w.get("bqkv0"),
        "wout": w.get("wout0"), "bout": w.get("bout0"),
        "causal": True, "l0_bias": True,
        "qk_slices": lambda hh: [(384 * hh, 256, 0)],
        "v_slices": lambda g: [(384 * (4 * g + i) + 256, 128, 128 * i)
                               for i in range(4)],
    }
    cfg1 = {
        "wqkv": w.get("wqkv1"), "wout": w.get("wout1"),
        "causal": False, "l0_bias": False,
        "qk_slices": lambda hh: [(128 * hh, 128, 0), (2048 + 128 * hh, 128, 128)],
        "v_slices": lambda g: [(4096 + 512 * g, 512, 0)],
    }

    barrier(env)
    for i, sl_id in enumerate(sublayers):
        xi, xo = xs[i], xs[i + 1]
        if sl_id in (0, 2):
            emit_ffn0(env, xi, xo, w)
        elif sl_id == 1:
            emit_attn(env, xi, xo, cfg0)
        elif sl_id == 4:
            emit_attn(env, xi, xo, cfg1)
        else:
            emit_moe(env, xi, xo, {"wt": w["wt"], "bex": w["bex"]})

    p.wait("sp", *[(s, s.cnt) for s in p.sems if s.cnt > 0])

    with stack:
        p.alloc_sems(stack)
        with nc.Block() as block:
            p.replay(block)
    return nc, names


_BUILT = {}


def _get_group(key, sublayers):
    if key not in _BUILT:
        _BUILT[key] = build_group(sublayers)
    return _BUILT[key]


def kernel(L0_attn_x, L0_attn_w_qkv, L0_attn_b_qkv, L0_attn_w_out, L0_attn_b_out,
           L0_ffn_w_gate, L0_ffn_w_up, L0_ffn_w_down, L0_ffn_s_gate, L0_ffn_s_up,
           L0_ffn_s_down, L1_attn_Wqkv, L1_attn_Wout, L1_ffn_wg, L1_ffn_W,
           L1_ffn_b):
    x = np.asarray(L0_attn_x, dtype=np.float32)
    n_cores = 8

    pp, cc = np.meshgrid(np.arange(128), np.arange(512), indexing="ij")
    anot = np.stack([(128 * i + pp > cc) for i in range(4)], axis=0)
    anot = anot.reshape(4 * 128, 512).astype(np.uint8)

    weights = {
        "wg": np.ascontiguousarray(L0_ffn_w_gate),
        "wu": np.ascontiguousarray(L0_ffn_w_up),
        "wd": np.ascontiguousarray(L0_ffn_w_down),
        "sgv": np.ascontiguousarray(np.asarray(L0_ffn_s_gate, np.float32).reshape(64, 128).T),
        "suv": np.ascontiguousarray(np.asarray(L0_ffn_s_up, np.float32).reshape(64, 128).T),
        "sdv": np.ascontiguousarray(np.asarray(L0_ffn_s_down, np.float32).reshape(16, 128).T),
        "wqkv0": np.ascontiguousarray(np.asarray(L0_attn_w_qkv, np.float32)),
        "bqkv0": np.ascontiguousarray(np.asarray(L0_attn_b_qkv, np.float32).reshape(48, 128).T),
        "wout0": np.ascontiguousarray(np.asarray(L0_attn_w_out, np.float32)),
        "bout0": np.ascontiguousarray(np.asarray(L0_attn_b_out, np.float32).reshape(16, 128).T),
        "anot": anot,
        "n10k": np.full((128, 512), NEGMASK, dtype=np.float32),
        "wqkv1": np.ascontiguousarray(np.asarray(L1_attn_Wqkv, np.float32).T),
        "wout1": np.ascontiguousarray(np.asarray(L1_attn_Wout, np.float32).T),
        "wgr": np.ascontiguousarray(np.asarray(L1_ffn_wg, np.float32)
                                    .reshape(16, 128, 8).transpose(1, 0, 2).reshape(128, 128)),
        "wt": np.ascontiguousarray(np.asarray(L1_ffn_W, np.float32).transpose(0, 2, 1)),
        "bex": np.ascontiguousarray(np.asarray(L1_ffn_b, np.float32)),
        "idnf": np.eye(128, dtype=np.float32),
        "negc": np.full((128, 1), -30.0, dtype=np.float32),
        "onescb": np.ones((128, 1), dtype=ml_dtypes.bfloat16),
        "onesrf": np.ones((1, 128), dtype=np.float32),
        "onescf": np.ones((128, 1), dtype=np.float32),
    }

    xc = [np.ascontiguousarray(x[c].T) for c in range(n_cores)]
    for gi, group in enumerate(GROUPS):
        nc, names = _get_group(tuple(group), group)
        in_maps = []
        for c in range(n_cores):
            m = {"xin": xc[c]}
            for nm in names:
                if nm != "xin":
                    m[nm] = weights[nm]
            in_maps.append(m)
        res = run_bass_kernel_spmd(nc, in_maps, core_ids=list(range(n_cores)))
        xc = [np.asarray(res.results[c]["xout"]) for c in range(n_cores)]

    out = np.stack([np.asarray(xc[c], dtype=np.float32).T for c in range(n_cores)],
                   axis=0)
    return out


# version 14
# speedup vs baseline: 1.1653x; 1.1653x over previous
import numpy as np
import ml_dtypes
import concourse.bass as bass
import concourse.mybir as mybir
from concourse.bass_utils import run_bass_kernel_spmd
from contextlib import ExitStack

F32 = mybir.dt.float32
BF16 = mybir.dt.bfloat16
I8 = mybir.dt.int8
U8 = mybir.dt.uint8
AF = mybir.ActivationFunctionType
ALU = mybir.AluOpType
AX = mybir.AxisListType

ENGINES = ("pe", "act", "dve", "pool", "sp")
ENGINE_ATTR = {"pe": "tensor", "act": "scalar", "dve": "vector",
               "pool": "gpsimd", "sp": "sync"}

D = 2048
T = 2048
NXC = D // 128
INTER = 8192
NH = 16
HD = 128
ISQ = float(1.0 / np.sqrt(np.float32(128.0)))
EPS = 1e-5
NEGMASK = -10000.0

GROUPS = [[0, 1, 2, 3, 4, 5]]


class Sem:
    __slots__ = ("name", "cnt", "h")

    def __init__(self, name):
        self.name = name
        self.cnt = 0
        self.h = None


class Prog:
    def __init__(self, nc):
        self.nc = nc
        self.q = {e: [] for e in ENGINES}
        self.sems = []
        self._named = {}

    def sem(self, name):
        s = Sem(name)
        self.sems.append(s)
        return s

    def dsem(self, name):
        if name not in self._named:
            self._named[name] = self.sem(name)
        return self._named[name]

    def emit(self, eng, fn, wait=(), inc=()):
        wait = tuple((s, v) for (s, v) in wait if v > 0)
        inc = tuple(inc)
        for s, n in inc:
            s.cnt += n

        def run(h):
            for s, v in wait:
                h.wait_ge(s.h, v)
            instr = fn(h)
            for s, n in inc:
                instr = instr.then_inc(s.h, n)
        self.q[eng].append(run)

    def wait(self, eng, *pairs):
        pairs = tuple((s, v) for (s, v) in pairs if v > 0)
        if not pairs:
            return

        def run(h):
            for s, v in pairs:
                h.wait_ge(s.h, v)
        self.q[eng].append(run)

    def alloc_sems(self, stack):
        for s in self.sems:
            s.h = stack.enter_context(self.nc.semaphore(s.name))

    def replay(self, block):
        for e in ENGINES:
            lst = self.q[e]
            if not lst:
                continue
            getattr(block, ENGINE_ATTR[e])(lambda h, lst=lst: [f(h) for f in lst])


class Env:
    pass


def barrier(env, engines=ENGINES):
    p = env.p
    snap = []
    for s in p.sems:
        if s.cnt != env.bar_snap.get(s.name, 0):
            snap.append((s, s.cnt))
            env.bar_snap[s.name] = s.cnt
    if not snap:
        return
    for e in engines:
        p.wait(e, *snap)


def emit_ln(env, x_dram, t0, tlen, h_tiles, hcol0, bigxb, router=None):
    nc, p = env.nc, env.p
    S = env.s
    sub_done = []
    nsub = tlen // 512
    sxh = [p.dsem(f"xbh{i}") for i in range(4)]
    sxf = [p.dsem(f"xf{i}") for i in range(2)]
    with ExitStack() as st:
        sb = lambda n, shp, dt: st.enter_context(nc.sbuf_tensor(n, shp, dt))
        psa = lambda n, shp, dt: st.enter_context(nc.psum_tensor(n, shp, dt))
        uid = f"ln{env.uid}"
        env.uid += 1
        xsq = [sb(f"{uid}xq{i}", [128, 512], BF16) for i in range(2)]
        t1 = [sb(f"{uid}t1{i}", [128, 512], F32) for i in range(2)]
        if router is not None:
            xf = [sb(f"{uid}xf{i}", [128, 512], F32) for i in range(2)]
        st_mean = sb(f"{uid}mean", [1, 512], F32)
        st_var = sb(f"{uid}var", [1, 512], F32)
        st_tmp = sb(f"{uid}tmp", [1, 512], F32)
        st_r = sb(f"{uid}r", [1, 512], F32)
        st_sd = st_tmp
        st_nmr = st_var
        ps_sum = psa(f"{uid}psum", [1, 512], F32)
        ps_sq = psa(f"{uid}psq", [1, 512], F32)
        ps_nmrb = psa(f"{uid}psnm", [128, 512], F32)
        rb_sb = sb(f"{uid}rbs", [128, 512], F32)
        if router is not None:
            ps_lg = [psa(f"{uid}pslg{i}", [128, 8], F32) for i in range(4)]

        bh_g = env.rg.setdefault("bigxb", [[(None, 0)] for _ in range(4)])
        xsq_g = env.rg.setdefault("xsq", [(None, 0)] * 2)
        xf_g = env.rg.setdefault("xf", [(None, 0)] * 2)
        t1_g = env.rg.setdefault("t1", [(None, 0)] * 2)
        xt = x_dram.tensor
        for sub in range(nsub):
            ts0 = t0 + sub * 512
            bdma = [0, 0, 0, 0]
            for bh in range(4):
                p.emit("pool", lambda e, bh=bh, ts0=ts0: e.dma_start(
                    bass.AP(bigxb, bh * 2048, [[8192, 128], [512, 4], [1, 512]]),
                    bass.AP(xt, bh * 4 * 128 * T + ts0,
                            [[T, 128], [128 * T, 4], [1, 512]])),
                    wait=[(gs, gv) for (gs, gv) in bh_g[bh] if gs],
                    inc=[(sxh[bh], 16)])
                bdma[bh] = sxh[bh].cnt
            for xc in range(NXC):
                bh = xc // 4
                xbv = bigxb[:, xc * 512:(xc + 1) * 512]
                sl = xc % 2
                gs, gv = xsq_g[sl]
                p.emit("act", lambda e, sl=sl, xbv=xbv: e.square(xsq[sl][:], xbv),
                       wait=[(sxh[bh], bdma[bh])] + ([(gs, gv)] if gs else []),
                       inc=[(S.act, 1)])
                sq_cnt = S.act.cnt
                if router is not None:
                    gs, gv = xf_g[sl]
                    p.emit("sp", lambda e, xc=xc, sl=sl, ts0=ts0: e.dma_start(
                        xf[sl][:], x_dram[xc * 128:(xc + 1) * 128, ts0:ts0 + 512]),
                        wait=([(gs, gv)] if gs else []), inc=[(sxf[sl], 16)])
                    f_cnt = sxf[sl].cnt
                w = [(sxh[bh], bdma[bh])]
                if xc == 0 and sub == 0:
                    w.extend(env.const_w)
                p.emit("pe", lambda e, xbv=xbv, xc=xc: e.matmul(
                    ps_sum[:], env.onesc_bf[:], xbv,
                    start=(xc == 0), stop=(xc == NXC - 1)), wait=w, inc=[(S.pe, 1)])
                p.emit("pe", lambda e, sl=sl, xc=xc: e.matmul(
                    ps_sq[:], env.onesc_bf[:], xsq[sl][:],
                    start=(xc == 0), stop=(xc == NXC - 1)),
                    wait=[(S.act, sq_cnt)], inc=[(S.pe, 1)])
                xsq_g[sl] = (S.pe, S.pe.cnt)
                if router is not None:
                    for tc4 in range(4):
                        p.emit("pe", lambda e, sl=sl, xc=xc, tc4=tc4: e.matmul(
                            ps_lg[tc4][:],
                            xf[sl][:, tc4 * 128:(tc4 + 1) * 128],
                            router["wg_sb"][:, xc * 8:(xc + 1) * 8],
                            start=(xc == 0), stop=(xc == NXC - 1)),
                            wait=[(sxf[sl], f_cnt)], inc=[(S.pe, 1)])
                    xf_g[sl] = (S.pe, S.pe.cnt)
            stats_pe = S.pe.cnt
            p.emit("dve", lambda e: e.tensor_scalar_mul(st_mean[:], ps_sum[:], 1.0 / D),
                   wait=[(S.pe, stats_pe)], inc=[(S.dve, 1)])
            p.emit("dve", lambda e: e.tensor_scalar_mul(st_var[:], ps_sq[:], 1.0 / D),
                   inc=[(S.dve, 1)])
            p.emit("dve", lambda e: e.tensor_mul(st_tmp[:], st_mean[:], st_mean[:]),
                   wait=[(S.dve, S.dve.cnt)], inc=[(S.dve, 1)])
            p.emit("dve", lambda e: e.scalar_tensor_tensor(
                st_var[:], st_var[:], EPS, st_tmp[:], ALU.add, ALU.subtract),
                wait=[(S.dve, S.dve.cnt)], inc=[(S.dve, 1)])
            var_cnt = S.dve.cnt
            p.emit("act", lambda e: e.sqrt(st_sd[:], st_var[:]),
                   wait=[(S.dve, var_cnt)], inc=[(S.act, 1)])
            sd_cnt = S.act.cnt
            p.emit("dve", lambda e: e.reciprocal(st_r[:], st_sd[:]),
                   wait=[(S.act, sd_cnt)], inc=[(S.dve, 1)])
            p.emit("dve", lambda e: e.scalar_tensor_tensor(
                st_nmr[:], st_mean[:], -1.0, st_r[:], ALU.mult, ALU.mult),
                wait=[(S.dve, S.dve.cnt)], inc=[(S.dve, 1)])
            fin_cnt = S.dve.cnt
            p.emit("pe", lambda e: e.matmul(ps_nmrb[:], env.onesr_f[:], st_r[0:1, :],
                                            start=True, stop=True),
                   wait=[(S.dve, fin_cnt)], inc=[(S.pe, 1)])
            rbmm = S.pe.cnt
            p.emit("dve", lambda e: e.tensor_copy(rb_sb[:], ps_nmrb[:]),
                   wait=[(S.pe, rbmm)], inc=[(S.dve, 1)])
            rbev = S.dve.cnt
            p.emit("pe", lambda e: e.matmul(ps_nmrb[:], env.onesr_f[:], st_nmr[0:1, :],
                                            start=True, stop=True),
                   wait=[(S.dve, rbev)], inc=[(S.pe, 1)])
            bc_cnt = S.pe.cnt
            if router is not None:
                for tc4 in range(4):
                    tc = (t0 - router["t0"]) // 128 + sub * 4 + tc4
                    p.emit("dve", lambda e, tc4=tc4, tc=tc: e.tensor_copy(
                        router["raw"][:, tc * 8:(tc + 1) * 8], ps_lg[tc4][:]),
                        wait=[(S.pe, stats_pe)] if tc4 == 0 else [],
                        inc=[(S.dve, 1)])
                raw_ev = S.dve.cnt
                for tc4 in range(4):
                    tc = (t0 - router["t0"]) // 128 + sub * 4 + tc4
                    p.emit("pe", lambda e, tc4=tc4: e.matmul(
                        ps_lg[0][:, 0:1], st_mean[0:1, tc4 * 128:(tc4 + 1) * 128],
                        env.onesr_f[0:1, 0:1], start=True, stop=True),
                        wait=[(S.dve, max(raw_ev, fin_cnt))] if tc4 == 0 else
                             [(S.dve, S.dve.cnt)],
                        inc=[(S.pe, 1)])
                    mcnt = S.pe.cnt
                    p.emit("dve", lambda e, tc=tc: e.tensor_copy(
                        router["mtok"][:, tc:tc + 1], ps_lg[0][:, 0:1]),
                        wait=[(S.pe, mcnt)], inc=[(S.dve, 1)])
                    p.emit("pe", lambda e, tc4=tc4: e.matmul(
                        ps_lg[0][:, 1:2], st_r[0:1, tc4 * 128:(tc4 + 1) * 128],
                        env.onesr_f[0:1, 0:1], start=True, stop=True),
                        wait=[(S.dve, S.dve.cnt)], inc=[(S.pe, 1)])
                    rcnt = S.pe.cnt
                    p.emit("dve", lambda e, tc=tc: e.tensor_copy(
                        router["rtok"][:, tc:tc + 1], ps_lg[0][:, 1:2]),
                        wait=[(S.pe, rcnt)], inc=[(S.dve, 1)])
            p.wait("dve", (S.pe, bc_cnt))
            half_apply = [0, 0, 0, 0]
            for xc in range(NXC):
                xbv = bigxb[:, xc * 512:(xc + 1) * 512]
                sl1 = xc % 2
                gs, gv = t1_g[sl1]
                p.emit("dve", lambda e, xbv=xbv, sl1=sl1: e.tensor_mul(
                    t1[sl1][:], xbv, rb_sb[:]),
                    wait=([(gs, gv)] if gs else []), inc=[(S.dve, 1)])
                p.emit("dve", lambda e, sl1=sl1, xc=xc, col=hcol0 + sub * 512: e.tensor_add(
                    h_tiles[xc][:, col:col + 512], t1[sl1][:], ps_nmrb[:]),
                    wait=[(S.dve, S.dve.cnt)], inc=[(S.dve, 1)])
                t1_g[sl1] = (S.dve, S.dve.cnt)
                half_apply[xc // 4] = S.dve.cnt
            for bh in range(4):
                bh_g[bh] = [(S.dve, half_apply[bh]), (S.pe, stats_pe)]
            sub_done.append(S.dve.cnt)
    return sub_done


def emit_ffn0(env, x_in, x_out, W):
    nc, p = env.nc, env.p
    S = env.s
    NOG = 16
    swgu = [p.dsem(f"wgu{i}") for i in range(2)]
    swd = [p.dsem(f"wd{i}") for i in range(2)]
    sxr = [p.dsem(f"xr{i}") for i in range(4)]
    sxn = [p.dsem(f"xn{i}") for i in range(4)]

    def _half(half):
        t0 = half * 1024
        with ExitStack() as st:
            sb = lambda n, shp, dt: st.enter_context(nc.sbuf_tensor(n, shp, dt))
            psa = lambda n, shp, dt: st.enter_context(nc.psum_tensor(n, shp, dt))
            uid = f"f{env.uid}"
            env.uid += 1
            h = [sb(f"{uid}h{i}", [128, 1024], BF16) for i in range(NXC)]
            wgu = [sb(f"{uid}wgu{r}", [128, NXC * 512], BF16) for r in range(2)]
            wd = [sb(f"{uid}wd{r}", [128, 4 * 1024], BF16) for r in range(2)]
            gf = sb(f"{uid}gf", [128, 8192], BF16)
            y = [sb(f"{uid}y{i}", [128, 1024], F32) for i in range(NXC)]
            xnw = [sb(f"{uid}xn{i}", [128, 512], F32) for i in range(4)]
            xfr = [sb(f"{uid}xr{i}", [128, 512], F32) for i in range(4)]
            psg = [psa(f"{uid}psg{r}", [128, 512], F32) for r in range(3)]
            psy = [psa(f"{uid}psy{r}", [128, 512], F32) for r in range(2)]

            def load_wgu(slot, Wm, og, guard):
                p.emit("pool", lambda e, slot=slot, og=og, Wm=Wm: e.dma_start(
                    bass.AP(wgu[slot], 0, [[NXC * 512, 128], [512, NXC], [1, 512]]),
                    bass.AP(Wm.tensor, og * 512,
                            [[INTER, 128], [128 * INTER, NXC], [1, 512]])),
                    wait=[guard] if guard[1] > 0 else [],
                    inc=[(swgu[slot], 16)])
                return swgu[slot].cnt

            def load_wd(slot, og, o2h, guard):
                p.emit("pool", lambda e, slot=slot, og=og, o2h=o2h: e.dma_start(
                    bass.AP(wd[slot], 0, [[4 * 1024, 128], [1024, 4], [1, 1024]]),
                    bass.AP(W["wd"].tensor, og * 512 * D + o2h * 1024,
                            [[D, 128], [128 * D, 4], [1, 1024]])),
                    wait=[guard] if guard[1] > 0 else [],
                    inc=[(swd[slot], 16)])
                return swd[slot].cnt

            wg_ready = [0, 0]
            wu_ready = [0, 0]
            wd_ready = [0, 0]
            gate_done = [0, 0]
            up_done = [0, 0]
            wd_mm_done = [0, 0]
            psg_free = [0, 0, 0]
            psg_free_eng = [None, None, None]
            psy_free = [0, 0]
            wg_ready[0] = load_wgu(0, W["wg"], 0, (S.pe, 0))
            wu_ready[1] = load_wgu(1, W["wu"], 0, (S.pe, 0))
            wd_ready[0] = load_wd(0, 0, 0, (S.pe, 0))
            h_sub = emit_ln(env, x_in, t0, 1024, h, 0, gf)
            h_all = h_sub[-1]
            f_done_prev = 0
            for og in range(NOG):
                s_g = og % 2
                s_u = 1 - s_g
                s_d = og % 2
                if og > 0:
                    p.wait("act", (S.dve, f_done_prev))
                p.wait("pe", (swgu[s_g], wg_ready[s_g]))
                for ts in range(2):
                    p.wait("pe", (S.dve, h_sub[ts]))
                    for j in range(4):
                        oc = og * 4 + j
                        bank = (ts * 4 + j) % 3
                        if psg_free_eng[bank] is not None:
                            p.wait("pe", (psg_free_eng[bank], psg_free[bank]))
                        for c in range(NXC):
                            p.emit("pe", lambda e, c=c, j=j, ts=ts, bank=bank, s_g=s_g: e.matmul(
                                psg[bank][:],
                                wgu[s_g][:, c * 512 + j * 128:c * 512 + (j + 1) * 128],
                                h[c][:, ts * 512:(ts + 1) * 512],
                                start=(c == 0), stop=(c == NXC - 1)),
                                inc=[(S.pe, 1)] if c == NXC - 1 else [])
                        mile = S.pe.cnt
                        p.emit("act", lambda e, j=j, ts=ts, oc=oc, bank=bank: e.activation(
                            gf[:, j * 1024 + ts * 512:j * 1024 + (ts + 1) * 512],
                            psg[bank][:], AF.Silu, scale=env.sg_sb[:, oc:oc + 1]),
                            wait=[(S.pe, mile)] +
                                 ([(S.dve, h_all)] if og == 0 and ts == 0 else []),
                            inc=[(S.act, 1)])
                        psg_free[bank] = S.act.cnt
                        psg_free_eng[bank] = S.act
                gate_done[s_g] = S.pe.cnt
                p.wait("pe", (swgu[s_u], wu_ready[s_u]))
                for ts in range(2):
                    for j in range(4):
                        oc = og * 4 + j
                        bank = (ts * 4 + j) % 3
                        p.wait("pe", (psg_free_eng[bank], psg_free[bank]))
                        for c in range(NXC):
                            p.emit("pe", lambda e, c=c, j=j, ts=ts, bank=bank, s_u=s_u: e.matmul(
                                psg[bank][:],
                                wgu[s_u][:, c * 512 + j * 128:c * 512 + (j + 1) * 128],
                                h[c][:, ts * 512:(ts + 1) * 512],
                                start=(c == 0), stop=(c == NXC - 1)),
                                inc=[(S.pe, 1)] if c == NXC - 1 else [])
                        mile = S.pe.cnt
                        acnt = S.act.cnt
                        p.emit("dve", lambda e, j=j, ts=ts, oc=oc, bank=bank: e.scalar_tensor_tensor(
                            gf[:, 4096 + j * 1024 + ts * 512:4096 + j * 1024 + (ts + 1) * 512],
                            psg[bank][:], env.su_sb[:, oc:oc + 1],
                            gf[:, j * 1024 + ts * 512:j * 1024 + (ts + 1) * 512],
                            ALU.mult, ALU.mult),
                            wait=[(S.pe, mile), (S.act, acnt)] +
                                 ([(S.dve, h_all)] if og == 0 and ts == 0 else []),
                            inc=[(S.dve, 1)])
                        psg_free[bank] = S.dve.cnt
                        psg_free_eng[bank] = S.dve
                up_done[s_u] = S.pe.cnt
                f_ready = S.dve.cnt
                f_done_prev = f_ready
                if og + 1 < NOG:
                    wg_ready[1 - s_g] = load_wgu(1 - s_g, W["wg"], og + 1,
                                                 (S.pe, up_done[s_u]))
                    wu_ready[1 - s_u] = load_wgu(1 - s_u, W["wu"], og + 1,
                                                 (S.pe, gate_done[s_g]))
                p.wait("pe", (S.dve, f_ready))
                for o2h in range(2):
                    u = og * 2 + o2h
                    wsl = u % 2
                    if u + 1 < 2 * NOG:
                        wd_ready[1 - wsl] = load_wd(
                            1 - wsl, (u + 1) // 2, (u + 1) % 2,
                            (S.pe, wd_mm_done[1 - wsl]))
                    p.wait("pe", (swd[wsl], wd_ready[wsl]))
                    for o2i in range(8):
                        o2c = o2h * 8 + o2i
                        for ts in range(2):
                            bank = (o2i * 2 + ts) % 2
                            p.wait("pe", (S.dve, psy_free[bank]))
                            for j in range(4):
                                p.emit("pe", lambda e, j=j, o2i=o2i, ts=ts, bank=bank, wsl=wsl: e.matmul(
                                    psy[bank][:],
                                    wd[wsl][:, j * 1024 + o2i * 128:j * 1024 + (o2i + 1) * 128],
                                    gf[:, 4096 + j * 1024 + ts * 512:4096 + j * 1024 + (ts + 1) * 512],
                                    start=(j == 0), stop=(j == 3)),
                                    inc=[(S.pe, 1)] if j == 3 else [])
                            mile = S.pe.cnt
                            if og == 0:
                                p.emit("dve", lambda e, o2c=o2c, ts=ts, bank=bank: e.tensor_copy(
                                    y[o2c][:, ts * 512:(ts + 1) * 512], psy[bank][:]),
                                    wait=[(S.pe, mile)], inc=[(S.dve, 1)])
                            else:
                                p.emit("dve", lambda e, o2c=o2c, ts=ts, bank=bank: e.tensor_add(
                                    y[o2c][:, ts * 512:(ts + 1) * 512],
                                    y[o2c][:, ts * 512:(ts + 1) * 512], psy[bank][:]),
                                    wait=[(S.pe, mile)], inc=[(S.dve, 1)])
                            psy_free[bank] = S.dve.cnt
                    wd_mm_done[wsl] = S.pe.cnt
            xg = env.rg.setdefault("epixg", [(None, 0)] * 4)
            for o2c in range(NXC):
                for ts in range(2):
                    sl = (o2c * 2 + ts) % 4
                    gs, gv = xg[sl]
                    p.emit("sp", lambda e, o2c=o2c, ts=ts, sl=sl, t0=t0: e.dma_start(
                        xfr[sl][:], x_in[o2c * 128:(o2c + 1) * 128,
                                         t0 + ts * 512:t0 + (ts + 1) * 512]),
                        wait=([(gs, gv)] if gs else []), inc=[(sxr[sl], 16)])
                    xcnt = sxr[sl].cnt
                    p.emit("dve", lambda e, o2c=o2c, ts=ts, sl=sl: e.scalar_tensor_tensor(
                        xnw[sl][:], y[o2c][:, ts * 512:(ts + 1) * 512],
                        env.sdh_sb[:, o2c:o2c + 1], xfr[sl][:], ALU.mult, ALU.add),
                        wait=[(sxr[sl], xcnt), (sxn[sl], sxn[sl].cnt)],
                        inc=[(S.dve, 1)])
                    xg[sl] = (S.dve, S.dve.cnt)
                    dcnt = S.dve.cnt
                    p.emit("act", lambda e, o2c=o2c, ts=ts, sl=sl, t0=t0: e.dma_start(
                        x_out[o2c * 128:(o2c + 1) * 128,
                              t0 + ts * 512:t0 + (ts + 1) * 512], xnw[sl][:]),
                        wait=[(S.dve, dcnt)], inc=[(sxn[sl], 16)])
            barrier(env)

    _half(0)
    _half(1)


def emit_attn(env, x_in, x_out, cfg):
    nc, p = env.nc, env.p
    S = env.s
    swqk = [p.dsem(f"wqk{i}") for i in range(2)]
    swv = [p.dsem(f"wv{i}") for i in range(2)]
    sksp = [p.dsem(f"ksp{i}") for i in range(2)]
    svsp = [p.dsem(f"vsp{i}") for i in range(2)]
    skb = [p.dsem(f"kb{i}") for i in range(2)]
    svb = [p.dsem(f"vb{i}") for i in range(2)]
    sqb = [p.dsem(f"qb{i}") for i in range(2)]
    swo = [p.dsem(f"wo{i}") for i in range(2)]
    sxr = [p.dsem(f"xr{i}") for i in range(4)]
    sxn = [p.dsem(f"xn{i}") for i in range(4)]
    with ExitStack() as slst:
        sbl = lambda n, shp, dt: slst.enter_context(nc.sbuf_tensor(n, shp, dt))
        uid = f"a{env.uid}"
        env.uid += 1
        qres = [sbl(f"{uid}q{i}", [128, T], BF16) for i in range(NH)]
        if cfg["l0_bias"]:
            bqkv = sbl(f"{uid}bqkv", [128, 48], F32)
            bqs = sbl(f"{uid}bqs", [128, 16], F32)
            bout = sbl(f"{uid}bout", [128, 16], F32)
            sbq = p.dsem(f"{uid}bq")
            p.emit("act", lambda e: e.dma_start(bqkv[:], cfg["bqkv"]),
                   inc=[(sbq, 16)])
            p.emit("act", lambda e: e.dma_start(bout[:], cfg["bout"]),
                   inc=[(sbq, 16)])
            bv_cnt = 32
            for j in range(16):
                p.emit("dve", lambda e, j=j: e.tensor_scalar_mul(
                    bqs[:, j:j + 1], bqkv[:, 3 * j:3 * j + 1], ISQ),
                    wait=[(sbq, bv_cnt)] if j == 0 else [], inc=[(S.dve, 1)])
            bqs_done = S.dve.cnt

        with ExitStack() as st:
            sb = lambda n, shp, dt: st.enter_context(nc.sbuf_tensor(n, shp, dt))
            psa = lambda n, shp, dt: st.enter_context(nc.psum_tensor(n, shp, dt))
            h = [sb(f"{uid}h{i}", [128, T], BF16) for i in range(NXC)]
            wqk = [sb(f"{uid}wqk{r}", [128, NXC * 256], BF16) for r in range(2)]
            wv = [sb(f"{uid}wv{r}", [128, NXC * 512], BF16) for r in range(2)]
            ksb = [sb(f"{uid}ksb0", [128, T], BF16)]
            vsb = [sb(f"{uid}vsb{r}", [128, 512], BF16) for r in range(2)]
            psq = [psa(f"{uid}psq{r}", [128, 512], F32) for r in range(2)]

            def load_wqk(slot, hh, guard):
                for (c0, wd_, d0) in cfg["qk_slices"](hh):
                    p.emit("pool", lambda e, slot=slot, c0=c0, wd_=wd_, d0=d0: e.dma_start(
                        bass.AP(wqk[slot], d0,
                                [[NXC * 256, 128], [256, NXC], [1, wd_]]),
                        bass.AP(cfg["wqkv"].tensor, c0,
                                [[3 * D, 128], [128 * 3 * D, NXC], [1, wd_]])),
                        wait=[guard] if (d0 == 0 and guard[1] > 0) else [],
                        inc=[(swqk[slot], 16)])
                return swqk[slot].cnt

            def load_wv(slot, g, guard):
                for (c0, wd_, d0) in cfg["v_slices"](g):
                    p.emit("pool", lambda e, slot=slot, c0=c0, wd_=wd_, d0=d0: e.dma_start(
                        bass.AP(wv[slot], d0,
                                [[NXC * 512, 128], [512, NXC], [1, wd_]]),
                        bass.AP(cfg["wqkv"].tensor, c0,
                                [[3 * D, 128], [128 * 3 * D, NXC], [1, wd_]])),
                        wait=[guard] if (d0 == 0 and guard[1] > 0) else [],
                        inc=[(swv[slot], 16)])
                return swv[slot].cnt

            wqk_ready = [0, 0]
            qk_done = [0, 0]
            psq_free = [0, 0]
            ksp_g = [0, 0]
            vsp_g = [0, 0]
            wqk_ready[0] = load_wqk(0, 0, (S.pe, 0))
            h_sub = emit_ln(env, x_in, 0, 1024, h, 0, wv[0])
            h_sub += emit_ln(env, x_in, 1024, 1024, h, 1024, wv[0])
            h_all = h_sub[-1]
            for hh in range(NH):
                sl = hh % 2
                if hh + 1 < NH:
                    wqk_ready[1 - sl] = load_wqk(1 - sl, hh + 1, (S.pe, qk_done[1 - sl]))
                p.wait("pe", (swqk[sl], wqk_ready[sl]))
                for part in range(2):
                    for ts in range(4):
                        bank = ts % 2
                        p.wait("pe", (S.dve, psq_free[bank]), (S.dve, h_sub[ts]))
                        for c in range(NXC):
                            p.emit("pe", lambda e, c=c, sl=sl, part=part, ts=ts, bank=bank: e.matmul(
                                psq[bank][:],
                                wqk[sl][:, c * 256 + part * 128:c * 256 + (part + 1) * 128],
                                h[c][:, ts * 512:(ts + 1) * 512],
                                start=(c == 0), stop=(c == NXC - 1)),
                                inc=[(S.pe, 1)] if c == NXC - 1 else [])
                        mile = S.pe.cnt
                        if part == 0:
                            if cfg["l0_bias"]:
                                p.emit("dve", lambda e, hh=hh, ts=ts, bank=bank: e.tensor_scalar(
                                    qres[hh][:, ts * 512:(ts + 1) * 512], psq[bank][:],
                                    ISQ, bqs[:, hh:hh + 1], ALU.mult, ALU.add),
                                    wait=[(S.pe, mile), (S.dve, bqs_done)],
                                    inc=[(S.dve, 1)])
                            else:
                                p.emit("dve", lambda e, hh=hh, ts=ts, bank=bank: e.tensor_scalar_mul(
                                    qres[hh][:, ts * 512:(ts + 1) * 512], psq[bank][:],
                                    ISQ), wait=[(S.pe, mile)], inc=[(S.dve, 1)])
                        else:
                            wk = [(S.pe, mile)]
                            if ts == 0:
                                wk.append((sksp[0], ksp_g[0]))
                            if cfg["l0_bias"]:
                                p.emit("dve", lambda e, sl=sl, ts=ts, bank=bank, hh=hh: e.tensor_scalar_add(
                                    ksb[0][:, ts * 512:(ts + 1) * 512], psq[bank][:],
                                    bqkv[:, 3 * hh + 1:3 * hh + 2]),
                                    wait=wk, inc=[(S.dve, 1)])
                            else:
                                p.emit("dve", lambda e, sl=sl, ts=ts, bank=bank: e.tensor_copy(
                                    ksb[0][:, ts * 512:(ts + 1) * 512], psq[bank][:]),
                                    wait=wk, inc=[(S.dve, 1)])
                        psq_free[bank] = S.dve.cnt
                qk_done[sl] = S.pe.cnt
                kcnt = S.dve.cnt
                p.emit("sp", lambda e, hh=hh, sl=sl: e.dma_start(
                    env.Ks[hh * 128:(hh + 1) * 128, :], ksb[0][:]),
                    wait=[(S.dve, kcnt)], inc=[(sksp[0], 16)])
                ksp_g[0] = sksp[0].cnt
            wv_ready = [0, 0]
            v_done = [0, 0]
            p.wait("pool", (S.dve, h_all))
            wv_ready[0] = load_wv(0, 0, (S.pe, 0))
            for g in range(4):
                sl = g % 2
                if g + 1 < 4:
                    wv_ready[1 - sl] = load_wv(1 - sl, g + 1, (S.pe, v_done[1 - sl]))
                p.wait("pe", (swv[sl], wv_ready[sl]))
                for tc in range(NXC):
                    bank = tc % 2
                    p.wait("pe", (S.dve, psq_free[bank]), (S.dve, h_sub[tc // 4]))
                    for c in range(NXC):
                        p.emit("pe", lambda e, c=c, sl=sl, tc=tc, bank=bank: e.matmul(
                            psq[bank][:], h[c][:, tc * 128:(tc + 1) * 128],
                            wv[sl][:, c * 512:(c + 1) * 512],
                            start=(c == 0), stop=(c == NXC - 1)),
                            inc=[(S.pe, 1)] if c == NXC - 1 else [])
                    mile = S.pe.cnt
                    vsl = tc % 2
                    p.emit("dve", lambda e, vsl=vsl, bank=bank: e.tensor_copy(
                        vsb[vsl][:], psq[bank][:]),
                        wait=[(S.pe, mile), (svsp[vsl], vsp_g[vsl])],
                        inc=[(S.dve, 1)])
                    psq_free[bank] = S.dve.cnt
                    vcnt = S.dve.cnt
                    p.emit("sp", lambda e, tc=tc, g=g, vsl=vsl: e.dma_start(
                        env.Vs[tc * 128:(tc + 1) * 128, g * 512:(g + 1) * 512],
                        vsb[vsl][:]), wait=[(S.dve, vcnt)], inc=[(svsp[vsl], 16)])
                    vsp_g[vsl] = svsp[vsl].cnt
                v_done[sl] = S.pe.cnt
            barrier(env)

        st23 = slst.enter_context(ExitStack())
        ctx = [st23.enter_context(nc.sbuf_tensor(f"{uid}c{i}", [128, T], BF16))
               for i in range(NH)]

        with ExitStack() as st:
            sb = lambda n, shp, dt: st.enter_context(nc.sbuf_tensor(n, shp, dt))
            psa = lambda n, shp, dt: st.enter_context(nc.psum_tensor(n, shp, dt))
            kbuf = [sb(f"{uid}kb{r}", [128, T], BF16) for r in range(2)]
            vbuf = [sb(f"{uid}vb{r}", [128, T], BF16) for r in range(2)]
            eTg = sb(f"{uid}eT", [128, NXC * 512], BF16)
            negc = sb(f"{uid}negc", [128, 1], F32)
            rzrow = sb(f"{uid}rz", [1, 512], F32)
            zb = sb(f"{uid}zb", [128, 512], F32)
            tx = sb(f"{uid}tx", [128, 512], F32)
            psS = [psa(f"{uid}psS{r}", [128, 1024], F32) for r in range(2)]
            psZ = psa(f"{uid}psZ", [1, 512], F32)
            psB = psa(f"{uid}psB", [128, 512], F32)
            psC = psa(f"{uid}psC", [128, 512], F32)
            sneg = p.dsem(f"{uid}neg")
            p.emit("act", lambda e: e.dma_start(negc[:], env.negc_d),
                   inc=[(sneg, 16)])
            negc_cnt = sneg.cnt
            if cfg["causal"]:
                anot = sb(f"{uid}anot", [128, 4 * 512], U8)
                n10k = sb(f"{uid}n10k", [128, 512], F32)
                smsk = p.dsem(f"{uid}msk")
                for i in range(4):
                    p.emit("act", lambda e, i=i: e.dma_start(
                        anot[:, i * 512:(i + 1) * 512],
                        env.anot_d[i * 128:(i + 1) * 128, :]), inc=[(smsk, 16)])
                p.emit("act", lambda e: e.dma_start(n10k[:], env.n10k_d),
                       inc=[(smsk, 16)])
                p.wait("dve", (smsk, 80))

            kv_ready = [0, 0]
            head_done = [0, 0]

            def load_kv(slot, hh, guard):
                p.emit("act", lambda e, hh=hh, slot=slot: e.dma_start(
                    kbuf[slot][:], env.Ks[hh * 128:(hh + 1) * 128, :]),
                    wait=[guard] if guard[1] > 0 else [], inc=[(skb[slot], 16)])
                vc0 = (hh // 4) * 512 + (hh % 4) * 128
                for tc in range(NXC):
                    p.emit("act", lambda e, tc=tc, slot=slot, vc0=vc0: e.dma_start(
                        vbuf[slot][:, tc * 128:(tc + 1) * 128],
                        env.Vs[tc * 128:(tc + 1) * 128, vc0:vc0 + 128]),
                        inc=[(svb[slot], 16)])
                return (skb[slot].cnt, svb[slot].cnt)

            kv_ready[0] = load_kv(0, 0, (S.pe, 0))
            exp_free = [0, 0]
            eT_war = 0
            z_guard = 0
            b_guard = 0
            c_guard = 0
            for hh in range(NH):
                sl = hh % 2
                if hh + 1 < NH:
                    kv_ready[1 - sl] = load_kv(1 - sl, hh + 1,
                                               (S.pe, head_done[1 - sl]))
                p.wait("pe", (skb[sl], kv_ready[sl][0]), (svb[sl], kv_ready[sl][1]))
                for g in range(4):
                    nk = 4 * (g + 1) if cfg["causal"] else 16
                    for kp in range(nk // 2):
                        slot = kp % 2
                        p.wait("pe", (S.act, exp_free[slot]))
                        if kp == 0:
                            p.wait("pe", (S.dve, z_guard), (S.dve, c_guard))
                        for half2 in range(2):
                            kc = kp * 2 + half2
                            p.emit("pe", lambda e, hh=hh, g=g, kc=kc, sl=sl, slot=slot, half2=half2: e.matmul(
                                psS[slot][:, half2 * 512:(half2 + 1) * 512],
                                kbuf[sl][:, kc * 128:(kc + 1) * 128],
                                qres[hh][:, g * 512:(g + 1) * 512],
                                start=True, stop=True),
                                inc=[(S.pe, 1)] if half2 == 1 else [])
                        mileb = S.pe.cnt
                        wexp = [(S.pe, mileb)]
                        if hh == 0 and g == 0 and kp == 0:
                            wexp.append((sneg, negc_cnt))
                        if kp == 0:
                            wexp.append((S.pe, eT_war))
                        if cfg["causal"] and kp * 2 + 1 >= 4 * g:
                            for half2 in range(2):
                                kc = kp * 2 + half2
                                if kc >= 4 * g:
                                    ci = kc - 4 * g
                                    p.emit("dve", lambda e, ci=ci, slot=slot, half2=half2: e.copy_predicated(
                                        psS[slot][:, half2 * 512:(half2 + 1) * 512],
                                        anot[:, ci * 512:(ci + 1) * 512], n10k[:]),
                                        wait=[(S.pe, mileb), (S.dve, S.dve.cnt)],
                                        inc=[(S.dve, 1)])
                            wexp.append((S.dve, S.dve.cnt))
                        p.emit("act", lambda e, kp=kp, slot=slot: e.activation(
                            eTg[:, kp * 1024:(kp + 1) * 1024], psS[slot][:],
                            AF.Exp, bias=negc[:, 0:1]),
                            wait=wexp, inc=[(S.act, 1)])
                        exp_free[slot] = S.act.cnt
                        p.wait("pe", (S.act, exp_free[slot]))
                        for half2 in range(2):
                            kc = kp * 2 + half2
                            p.emit("pe", lambda e, kc=kc, nk=nk: e.matmul(
                                psZ[:], env.onesc_bf[:],
                                eTg[:, kc * 512:(kc + 1) * 512],
                                start=(kc == 0), stop=(kc == nk - 1)),
                                inc=[(S.pe, 1)] if kc == nk - 1 else [])
                            p.emit("pe", lambda e, kc=kc, nk=nk, sl=sl: e.matmul(
                                psC[:], vbuf[sl][:, kc * 128:(kc + 1) * 128],
                                eTg[:, kc * 512:(kc + 1) * 512],
                                start=(kc == 0), stop=(kc == nk - 1)),
                                inc=[(S.pe, 1)] if kc == nk - 1 else [])
                    zc_mile = S.pe.cnt
                    eT_war = zc_mile
                    p.emit("dve", lambda e: e.reciprocal(rzrow[:], psZ[0:1, :]),
                           wait=[(S.pe, zc_mile), (S.dve, S.dve.cnt)],
                           inc=[(S.dve, 1)])
                    z_guard = S.dve.cnt
                    p.emit("pe", lambda e: e.matmul(psB[:], env.onesr_f[:],
                                                    rzrow[0:1, :],
                                                    start=True, stop=True),
                           wait=[(S.dve, z_guard), (S.dve, b_guard)],
                           inc=[(S.pe, 1)])
                    bmile = S.pe.cnt
                    p.emit("dve", lambda e: e.tensor_copy(zb[:], psB[:]),
                           wait=[(S.pe, bmile)], inc=[(S.dve, 1)])
                    b_guard = S.dve.cnt
                    if cfg["l0_bias"]:
                        p.emit("dve", lambda e: e.tensor_mul(tx[:], psC[:], zb[:]),
                               wait=[(S.dve, S.dve.cnt)], inc=[(S.dve, 1)])
                        p.emit("dve", lambda e, hh=hh, g=g: e.tensor_scalar_add(
                            ctx[hh][:, g * 512:(g + 1) * 512], tx[:],
                            bqkv[:, 3 * hh + 2:3 * hh + 3]),
                            wait=[(S.dve, S.dve.cnt)], inc=[(S.dve, 1)])
                    else:
                        p.emit("dve", lambda e, hh=hh, g=g: e.tensor_mul(
                            ctx[hh][:, g * 512:(g + 1) * 512], psC[:], zb[:]),
                            wait=[(S.dve, S.dve.cnt)], inc=[(S.dve, 1)])
                    c_guard = S.dve.cnt
                head_done[sl] = S.pe.cnt
            barrier(env)

        with ExitStack() as st:
            sb = lambda n, shp, dt: st.enter_context(nc.sbuf_tensor(n, shp, dt))
            psa = lambda n, shp, dt: st.enter_context(nc.psum_tensor(n, shp, dt))
            wo = [sb(f"{uid}wo{r}", [128, NXC * 1024], BF16) for r in range(2)]
            xfr = [sb(f"{uid}xr{i}", [128, 512], F32) for i in range(2)]
            xnw = [sb(f"{uid}xn{i}", [128, 512], F32) for i in range(2)]
            pso = [psa(f"{uid}pso{r}", [128, 512], F32) for r in range(2)]

            def load_wo(slot, ocg):
                p.emit("pool", lambda e, slot=slot, ocg=ocg: e.dma_start(
                    bass.AP(wo[slot], 0, [[NXC * 1024, 128], [1024, NXC], [1, 1024]]),
                    bass.AP(cfg["wout"].tensor, ocg * 1024,
                            [[D, 128], [128 * D, NXC], [1, 1024]])),
                    inc=[(swo[slot], 16)])
                return swo[slot].cnt

            wo_ready = [load_wo(0, 0), load_wo(1, 1)]
            pso_free = [0, 0]
            xg = [(None, 0)] * 2
            for ocg in range(2):
                sl = ocg
                p.wait("pe", (swo[sl], wo_ready[sl]))
                for oi in range(8):
                    oc = ocg * 8 + oi
                    for ts in range(4):
                        bank = (oi * 4 + ts) % 2
                        p.wait("pe", (S.dve, pso_free[bank]))
                        for c in range(NXC):
                            p.emit("pe", lambda e, c=c, sl=sl, oi=oi, ts=ts, bank=bank: e.matmul(
                                pso[bank][:],
                                wo[sl][:, c * 1024 + oi * 128:c * 1024 + (oi + 1) * 128],
                                ctx[c][:, ts * 512:(ts + 1) * 512],
                                start=(c == 0), stop=(c == NXC - 1)),
                                inc=[(S.pe, 1)] if c == NXC - 1 else [])
                        mile = S.pe.cnt
                        sl2 = (oi * 4 + ts) % 2
                        gs, gv = xg[sl2]
                        p.emit("sp", lambda e, oc=oc, ts=ts, sl2=sl2: e.dma_start(
                            xfr[sl2][:], x_in[oc * 128:(oc + 1) * 128,
                                             ts * 512:(ts + 1) * 512]),
                            wait=([(gs, gv)] if gs else []), inc=[(sxr[sl2], 16)])
                        xcnt = sxr[sl2].cnt
                        if cfg["l0_bias"]:
                            p.emit("dve", lambda e, oc=oc, bank=bank, sl2=sl2: e.scalar_tensor_tensor(
                                xnw[sl2][:], pso[bank][:], bout[:, oc:oc + 1],
                                xfr[sl2][:], ALU.add, ALU.add),
                                wait=[(S.pe, mile), (sxr[sl2], xcnt),
                                      (sxn[sl2], sxn[sl2].cnt)],
                                inc=[(S.dve, 1)])
                        else:
                            p.emit("dve", lambda e, bank=bank, sl2=sl2: e.tensor_add(
                                xnw[sl2][:], pso[bank][:], xfr[sl2][:]),
                                wait=[(S.pe, mile), (sxr[sl2], xcnt),
                                      (sxn[sl2], sxn[sl2].cnt)],
                                inc=[(S.dve, 1)])
                        xg[sl2] = (S.dve, S.dve.cnt)
                        pso_free[bank] = S.dve.cnt
                        dcnt = S.dve.cnt
                        p.emit("sp", lambda e, oc=oc, ts=ts, sl2=sl2: e.dma_start(
                            x_out[oc * 128:(oc + 1) * 128, ts * 512:(ts + 1) * 512],
                            xnw[sl2][:]), wait=[(S.dve, dcnt)], inc=[(sxn[sl2], 16)])
            barrier(env)


def emit_moe(env, x_in, x_out, W):
    nc, p = env.nc, env.p
    S = env.s
    swe = [p.dsem(f"we{i}") for i in range(2)]
    sce = [p.dsem(f"ce{i}") for i in range(2)]
    sxr = [p.dsem(f"xr{i}") for i in range(4)]
    sxn = [p.dsem(f"xn{i}") for i in range(4)]

    def _half(half):
        t0 = half * 1024
        with ExitStack() as st:
            sb = lambda n, shp, dt: st.enter_context(nc.sbuf_tensor(n, shp, dt))
            psa = lambda n, shp, dt: st.enter_context(nc.psum_tensor(n, shp, dt))
            uid = f"m{env.uid}"
            env.uid += 1
            h = [sb(f"{uid}h{i}", [128, 1024], BF16) for i in range(NXC)]
            b_sb = sb(f"{uid}bex", [8, D], F32)
            sbex = p.dsem(f"{uid}bex")
            p.emit("act", lambda e: e.dma_start(b_sb[:], W["bex"]), inc=[(sbex, 16)])
            raw = sb(f"{uid}raw", [128, 64], F32)
            mtok = sb(f"{uid}mt", [128, 8], F32)
            rtok = sb(f"{uid}rt", [128, 8], F32)
            lg = sb(f"{uid}lg", [128, 8], F32)
            lgt = sb(f"{uid}lgt", [128, 8], F32)
            p8 = sb(f"{uid}p8", [128, 8], F32)
            msk = sb(f"{uid}msk", [128, 8], F32)
            rstat = sb(f"{uid}rs", [128, 3], F32)
            c_sb = sb(f"{uid}csb", [128, 64], F32)
            cT = sb(f"{uid}cT", [8, 1024], F32)
            ce0 = [sb(f"{uid}ce{r}", [1, 1024], F32) for r in range(2)]
            cb = [sb(f"{uid}cb{r}", [128, 1024], F32) for r in range(2)]
            hs = [sb(f"{uid}hs{i}", [128, 1024], BF16) for i in range(NXC)]
            y = [sb(f"{uid}y{i}", [128, 1024], F32) for i in range(NXC)]
            weF = sb(f"{uid}weF", [128, 8192], BF16)
            xnw = [sb(f"{uid}xn{i}", [128, 512], F32) for i in range(3)]
            xfr = [sb(f"{uid}xr{i}", [128, 512], F32) for i in range(3)]
            def load_we(slot, ee, og2, guard):
                p.emit("pool", lambda e, slot=slot, ee=ee, og2=og2: e.dma_start(
                    bass.AP(weF, slot * 4096,
                            [[8192, 128], [256, NXC], [1, 256]]),
                    bass.AP(W["wt"].tensor, ee * D * D + og2 * 256,
                            [[D, 128], [128 * D, NXC], [1, 256]])),
                    wait=[guard] if guard[1] > 0 else [],
                    inc=[(swe[slot], 16)])
                return swe[slot].cnt

            router = {"wg_sb": env.wg_sb, "raw": raw, "mtok": mtok,
                      "rtok": rtok, "t0": t0}
            h_subm = emit_ln(env, x_in, t0, 1024, h, 0, weF, router=router)
            h_done = S.dve.cnt
            we_ready = [0, 0]
            we_ready[0] = load_we(0, 0, 0, (S.dve, h_subm[-1]))
            we_ready[1] = load_we(1, 0, 1, (S.dve, h_subm[-1]))

            psT8 = psa(f"{uid}psT8", [8, 128], F32)
            ps_cb = psa(f"{uid}pscb", [128, 1024], F32)
            psy = [psa(f"{uid}psy{r}", [128, 512], F32) for r in range(2)]

            raw_done = S.dve.cnt
            for tc in range(8):
                p.emit("dve", lambda e, tc=tc: e.tensor_mul(
                    rstat[:, 0:1], rtok[:, tc:tc + 1], mtok[:, tc:tc + 1]),
                    wait=[(S.dve, S.dve.cnt)], inc=[(S.dve, 1)])
                p.emit("dve", lambda e, tc=tc: e.tensor_scalar_mul(
                    lg[:], raw[:, tc * 8:(tc + 1) * 8], rtok[:, tc:tc + 1]),
                    wait=[(S.dve, S.dve.cnt)], inc=[(S.dve, 1)])
                p.emit("dve", lambda e: e.tensor_scalar_mul(
                    lgt[:], env.wgsum_b[:], rstat[:, 0:1]),
                    wait=[(S.dve, S.dve.cnt)], inc=[(S.dve, 1)])
                p.emit("dve", lambda e: e.tensor_sub(lg[:], lg[:], lgt[:]),
                       wait=[(S.dve, S.dve.cnt)], inc=[(S.dve, 1)])
                p.emit("dve", lambda e: e.reduce_max(
                    rstat[:, 1:2], lg[:], axis=AX.X, negate=True),
                    wait=[(S.dve, S.dve.cnt)], inc=[(S.dve, 1)])
                l_cnt = S.dve.cnt
                p.emit("act", lambda e: e.activation(
                    p8[:], lg[:], AF.Exp, bias=rstat[:, 1:2],
                    accum_out=rstat[:, 2:3]),
                    wait=[(S.dve, l_cnt)], inc=[(S.act, 1)])
                e_cnt = S.act.cnt
                p.emit("dve", lambda e: e.tensor_scalar(
                    lgt[:], lg[:], rstat[:, 1:2], None, ALU.add),
                    wait=[(S.dve, S.dve.cnt)], inc=[(S.dve, 1)])
                p.emit("dve", lambda e: e.tensor_single_scalar(
                    msk[:], lgt[:], 0.0, ALU.is_equal),
                    wait=[(S.dve, S.dve.cnt)], inc=[(S.dve, 1)])
                p.emit("dve", lambda e: e.reciprocal(rstat[:, 0:1], rstat[:, 2:3]),
                       wait=[(S.act, e_cnt), (S.dve, S.dve.cnt)], inc=[(S.dve, 1)])
                p.emit("dve", lambda e: e.tensor_mul(p8[:], p8[:], msk[:]),
                       wait=[(S.dve, S.dve.cnt)], inc=[(S.dve, 1)])
                p.emit("dve", lambda e, tc=tc: e.tensor_scalar(
                    c_sb[:, tc * 8:(tc + 1) * 8], p8[:], rstat[:, 0:1], 0.5,
                    ALU.mult, ALU.mult),
                    wait=[(S.dve, S.dve.cnt)], inc=[(S.dve, 1)])
                c_cnt = S.dve.cnt
                p.emit("pe", lambda e, tc=tc: e.transpose(
                    psT8[:], c_sb[:, tc * 8:(tc + 1) * 8], env.idn_f[:]),
                    wait=[(S.dve, c_cnt)], inc=[(S.pe, 1)])
                t_cnt = S.pe.cnt
                p.emit("dve", lambda e, tc=tc: e.tensor_copy(
                    cT[0:8, tc * 128:(tc + 1) * 128], psT8[0:8, :]),
                    wait=[(S.pe, t_cnt)], inc=[(S.dve, 1)])
            cT_done = S.dve.cnt

            og_done = [0, 0]
            psy_free = [0, 0]
            cb_ready = [0, 0]
            hs_ready = 0
            ce_g = [(None, 0)] * 2
            cb_used = [0, 0]
            cb_used_pool = [0, 0]
            ps_cb_guard = [0]

            def prep_expert(ee):
                sl = ee % 2
                gs, gv = ce_g[sl]
                p.emit("sp", lambda e, ee=ee, sl=sl: e.dma_start(
                    ce0[sl][:], cT[ee:ee + 1, :]),
                    wait=[(S.dve, cT_done)] + ([(gs, gv)] if gs else []),
                    inc=[(sce[sl], 16)])
                ccnt = sce[sl].cnt
                for half2 in range(2):
                    p.emit("pe", lambda e, sl=sl, half2=half2: e.matmul(
                        ps_cb[:, half2 * 512:(half2 + 1) * 512], env.onesr_f[:],
                        ce0[sl][0:1, half2 * 512:(half2 + 1) * 512],
                        start=True, stop=True),
                        wait=([(sce[sl], ccnt), (S.dve, cb_used[sl]),
                               (S.pool, cb_used_pool[sl]),
                               (S.dve, ps_cb_guard[0])] if half2 == 0 else []),
                        inc=[(S.pe, 1)])
                bcnt = S.pe.cnt
                p.emit("dve", lambda e, sl=sl: e.tensor_copy(cb[sl][:], ps_cb[:]),
                       wait=[(S.pe, bcnt)], inc=[(S.dve, 1)])
                ce_g[sl] = (S.pe, bcnt)
                cb_ready[sl] = S.dve.cnt
                ps_cb_guard[0] = S.dve.cnt

            prep_expert(0)
            for ee in range(8):
                sl = ee % 2
                p.wait("dve", (S.dve, cb_ready[sl]), (S.pe, hs_ready))
                p.wait("pool", (S.dve, cb_ready[sl]), (S.pe, hs_ready))
                for c in range(NXC):
                    p.emit("dve", lambda e, c=c, sl=sl: e.tensor_mul(
                        hs[c][:], h[c][:], cb[sl][:]),
                        wait=[(S.dve, h_done)] if c == 0 else [], inc=[(S.dve, 1)])
                hs_dve, hs_pool = S.dve.cnt, S.pool.cnt
                cb_used[sl] = S.dve.cnt
                cb_used_pool[sl] = S.pool.cnt
                if ee + 1 < 8:
                    prep_expert(ee + 1)
                p.wait("pe", (S.dve, hs_dve), (S.pool, hs_pool))
                for og2 in range(8):
                    unit = ee * 8 + og2
                    wsl = unit % 2
                    if unit + 1 < 64 and unit + 1 >= 2:
                        we_ready[1 - wsl] = load_we(1 - wsl, (unit + 1) // 8,
                                                    (unit + 1) % 8,
                                                    (S.pe, og_done[1 - wsl]))
                    p.wait("pe", (swe[wsl], we_ready[wsl]))
                    for oi in range(2):
                        oc = og2 * 2 + oi
                        for ts in range(2):
                            bank = (oi * 2 + ts) % 2
                            p.wait("pe", (S.dve, psy_free[bank]))
                            for c in range(NXC):
                                p.emit("pe", lambda e, c=c, wsl=wsl, oi=oi, ts=ts, bank=bank: e.matmul(
                                    psy[bank][:],
                                    weF[:, wsl * 4096 + c * 256 + oi * 128:
                                        wsl * 4096 + c * 256 + (oi + 1) * 128],
                                    hs[c][:, ts * 512:(ts + 1) * 512],
                                    start=(c == 0), stop=(c == NXC - 1)),
                                    inc=[(S.pe, 1)] if c == NXC - 1 else [])
                            mile = S.pe.cnt
                            if ee == 0:
                                p.emit("dve", lambda e, oc=oc, ts=ts, bank=bank: e.tensor_copy(
                                    y[oc][:, ts * 512:(ts + 1) * 512], psy[bank][:]),
                                    wait=[(S.pe, mile)], inc=[(S.dve, 1)])
                            else:
                                p.emit("dve", lambda e, oc=oc, ts=ts, bank=bank: e.tensor_add(
                                    y[oc][:, ts * 512:(ts + 1) * 512],
                                    y[oc][:, ts * 512:(ts + 1) * 512], psy[bank][:]),
                                    wait=[(S.pe, mile)], inc=[(S.dve, 1)])
                            psy_free[bank] = S.dve.cnt
                    og_done[wsl] = S.pe.cnt
                hs_ready = S.pe.cnt
            for oc in range(NXC):
                for ts in range(2):
                    bank = (oc * 2 + ts) % 2
                    p.wait("pe", (S.dve, psy_free[bank]))
                    p.emit("pe", lambda e, oc=oc, ts=ts, bank=bank: e.matmul(
                        psy[bank][:], b_sb[0:8, oc * 128:(oc + 1) * 128],
                        cT[0:8, ts * 512:(ts + 1) * 512], start=True, stop=True),
                        wait=[(sbex, 16)] if oc == 0 and ts == 0 else [],
                        inc=[(S.pe, 1)])
                    mile = S.pe.cnt
                    p.emit("dve", lambda e, oc=oc, ts=ts, bank=bank: e.tensor_add(
                        y[oc][:, ts * 512:(ts + 1) * 512],
                        y[oc][:, ts * 512:(ts + 1) * 512], psy[bank][:]),
                        wait=[(S.pe, mile)], inc=[(S.dve, 1)])
                    psy_free[bank] = S.dve.cnt
            xg = env.rg.setdefault("epixg", [(None, 0)] * 4)
            for oc in range(NXC):
                for ts in range(2):
                    sl = (oc * 2 + ts) % 3
                    gs, gv = xg[sl]
                    p.emit("sp", lambda e, oc=oc, ts=ts, sl=sl, t0=t0: e.dma_start(
                        xfr[sl][:], x_in[oc * 128:(oc + 1) * 128,
                                         t0 + ts * 512:t0 + (ts + 1) * 512]),
                        wait=([(gs, gv)] if gs else []), inc=[(sxr[sl], 16)])
                    xcnt = sxr[sl].cnt
                    p.emit("dve", lambda e, oc=oc, ts=ts, sl=sl: e.tensor_add(
                        xnw[sl][:], y[oc][:, ts * 512:(ts + 1) * 512], xfr[sl][:]),
                        wait=[(sxr[sl], xcnt), (sxn[sl], sxn[sl].cnt)],
                        inc=[(S.dve, 1)])
                    xg[sl] = (S.dve, S.dve.cnt)
                    dcnt = S.dve.cnt
                    p.emit("act", lambda e, oc=oc, ts=ts, sl=sl, t0=t0: e.dma_start(
                        x_out[oc * 128:(oc + 1) * 128,
                              t0 + ts * 512:t0 + (ts + 1) * 512], xnw[sl][:]),
                        wait=[(S.dve, dcnt)], inc=[(sxn[sl], 16)])
            barrier(env)

    _half(0)
    _half(1)


def build_group(sublayers):
    nc = bass.Bass("TRN2", target_bir_lowering=False)
    env = Env()
    env.nc = nc
    env.p = Prog(nc)
    env.uid = 0
    env.bar_snap = {}
    env.rg = {}
    p = env.p

    class Sems:
        pass
    S = Sems()
    for nm in ("pe", "act", "dve", "pool", "dsp", "dpool", "dact"):
        setattr(S, nm, p.sem(nm))
    env.s = S

    need = set(sublayers)
    x_in = nc.dram_tensor("xin", [D, T], F32, kind="ExternalInput").ap()
    x_out = nc.dram_tensor("xout", [D, T], F32, kind="ExternalOutput").ap()
    names = ["xin"]

    def din(name, shape, dt):
        names.append(name)
        return nc.dram_tensor(name, shape, dt, kind="ExternalInput").ap()

    onesc_bf_d = din("onescb", [128, 1], BF16)
    onesr_f_d = din("onesrf", [1, 128], F32)
    onesc_f_d = din("onescf", [128, 1], F32)

    w = {}
    if need & {0, 2}:
        w["wg"] = din("wg", [D, INTER], I8)
        w["wu"] = din("wu", [D, INTER], I8)
        w["wd"] = din("wd", [INTER, D], I8)
        w["sg"] = din("sgv", [128, 64], F32)
        w["su"] = din("suv", [128, 64], F32)
        w["sd"] = din("sdv", [128, 16], F32)
    if 1 in need:
        w["wqkv0"] = din("wqkv0", [D, 3 * D], F32)
        w["bqkv0"] = din("bqkv0", [128, 48], F32)
        w["wout0"] = din("wout0", [D, D], F32)
        w["bout0"] = din("bout0", [128, 16], F32)
        w["anot"] = din("anot", [4 * 128, 512], U8)
        w["n10k"] = din("n10k", [128, 512], F32)
    if 4 in need:
        w["wqkv1"] = din("wqkv1", [D, 3 * D], F32)
        w["wout1"] = din("wout1", [D, D], F32)
    if need & {3, 5}:
        w["wgr"] = din("wgr", [128, 128], F32)
        w["wt"] = din("wt", [8, D, D], F32)
        w["bex"] = din("bex", [8, D], F32)
        w["idnf"] = din("idnf", [128, 128], F32)
    if need & {1, 4}:
        w["negc"] = din("negc", [128, 1], F32)
        env.negc_d = w["negc"]
        env.Ks = nc.dram_tensor("ks_scr", [D, T], BF16, kind="Internal").ap()
        env.Vs = nc.dram_tensor("vs_scr", [T, D], BF16, kind="Internal").ap()

    nseq = len(sublayers)
    xs = [x_in]
    for i in range(nseq - 1):
        xs.append(nc.dram_tensor(f"xscr{i}", [D, T], F32, kind="Internal").ap())
    xs.append(x_out)

    stack = ExitStack()
    sb = lambda n, shp, dt: stack.enter_context(nc.sbuf_tensor(n, shp, dt))

    env.onesc_bf = sb("onescb_s", [128, 1], BF16)
    env.onesr_f = sb("onesrf_s", [1, 128], F32)
    env.onesc_f = sb("onescf_s", [128, 1], F32)
    ds_ones = p.dsem("csto")
    p.emit("act", lambda e: e.dma_start(env.onesc_bf[:], onesc_bf_d), inc=[(ds_ones, 16)])
    p.emit("act", lambda e: e.dma_start(env.onesr_f[:], onesr_f_d), inc=[(ds_ones, 16)])
    p.emit("act", lambda e: e.dma_start(env.onesc_f[:], onesc_f_d), inc=[(ds_ones, 16)])
    env.const_w = [(ds_ones, 48)]

    if need & {0, 2}:
        env.sg_sb = sb("sg_s", [128, 64], F32)
        env.su_sb = sb("su_s", [128, 64], F32)
        env.sdr_sb = sb("sdr_s", [128, 16], F32)
        env.sdh_sb = sb("sdh_s", [128, 16], F32)
        ds_ffn = p.dsem("cstf")
        p.emit("act", lambda e: e.dma_start(env.sg_sb[:], w["sg"]), inc=[(ds_ffn, 16)])
        p.emit("act", lambda e: e.dma_start(env.su_sb[:], w["su"]), inc=[(ds_ffn, 16)])
        p.emit("act", lambda e: e.dma_start(env.sdr_sb[:], w["sd"]), inc=[(ds_ffn, 16)])
        p.emit("dve", lambda e: e.tensor_scalar_mul(env.sdh_sb[:], env.sdr_sb[:], 0.5),
               wait=[(ds_ffn, 48)], inc=[(S.dve, 1)])
        env.const_w.append((ds_ffn, 48))
    if need & {3, 5}:
        env.wg_sb = sb("wgr_s", [128, 128], F32)
        env.idn_f = sb("idnf_s", [128, 128], F32)
        env.wgsum_b = sb("wgsb_s", [128, 8], F32)
        wgsum_r = sb("wgsr_s", [1, 8], F32)
        ds_moe = p.dsem("cstm")
        p.emit("act", lambda e: e.dma_start(env.wg_sb[:], w["wgr"]), inc=[(ds_moe, 16)])
        p.emit("act", lambda e: e.dma_start(env.idn_f[:], w["idnf"]), inc=[(ds_moe, 16)])
        env.const_w.append((ds_moe, 32))
        with ExitStack() as st0:
            ps_ws = st0.enter_context(nc.psum_tensor("ps_ws", [1, 8], F32))
            ps_wb = st0.enter_context(nc.psum_tensor("ps_wb", [128, 8], F32))
            for c in range(16):
                p.emit("pe", lambda e, c=c: e.matmul(
                    ps_ws[:], env.onesc_f[:], env.wg_sb[:, c * 8:(c + 1) * 8],
                    start=(c == 0), stop=(c == 15)),
                    wait=[(ds_moe, 32), (ds_ones, 48)] if c == 0 else [],
                    inc=[(S.pe, 1)])
            p.emit("dve", lambda e: e.tensor_copy(wgsum_r[:], ps_ws[:]),
                   wait=[(S.pe, S.pe.cnt)], inc=[(S.dve, 1)])
            wr_cnt = S.dve.cnt
            p.emit("pe", lambda e: e.matmul(ps_wb[:], env.onesr_f[:], wgsum_r[0:1, :],
                                            start=True, stop=True),
                   wait=[(S.dve, wr_cnt)], inc=[(S.pe, 1)])
            p.emit("dve", lambda e: e.tensor_copy(env.wgsum_b[:], ps_wb[:]),
                   wait=[(S.pe, S.pe.cnt)], inc=[(S.dve, 1)])
    if 1 in need:
        env.anot_d = w["anot"]
        env.n10k_d = w["n10k"]

    cfg0 = {
        "wqkv": w.get("wqkv0"), "bqkv": w.get("bqkv0"),
        "wout": w.get("wout0"), "bout": w.get("bout0"),
        "causal": True, "l0_bias": True,
        "qk_slices": lambda hh: [(384 * hh, 256, 0)],
        "v_slices": lambda g: [(384 * (4 * g + i) + 256, 128, 128 * i)
                               for i in range(4)],
    }
    cfg1 = {
        "wqkv": w.get("wqkv1"), "wout": w.get("wout1"),
        "causal": False, "l0_bias": False,
        "qk_slices": lambda hh: [(128 * hh, 128, 0), (2048 + 128 * hh, 128, 128)],
        "v_slices": lambda g: [(4096 + 512 * g, 512, 0)],
    }

    barrier(env)
    for i, sl_id in enumerate(sublayers):
        xi, xo = xs[i], xs[i + 1]
        if sl_id in (0, 2):
            emit_ffn0(env, xi, xo, w)
        elif sl_id == 1:
            emit_attn(env, xi, xo, cfg0)
        elif sl_id == 4:
            emit_attn(env, xi, xo, cfg1)
        else:
            emit_moe(env, xi, xo, {"wt": w["wt"], "bex": w["bex"]})

    p.wait("sp", *[(s, s.cnt) for s in p.sems if s.cnt > 0])

    with stack:
        p.alloc_sems(stack)
        with nc.Block() as block:
            p.replay(block)
    return nc, names


_BUILT = {}


def _get_group(key, sublayers):
    if key not in _BUILT:
        _BUILT[key] = build_group(sublayers)
    return _BUILT[key]


def kernel(L0_attn_x, L0_attn_w_qkv, L0_attn_b_qkv, L0_attn_w_out, L0_attn_b_out,
           L0_ffn_w_gate, L0_ffn_w_up, L0_ffn_w_down, L0_ffn_s_gate, L0_ffn_s_up,
           L0_ffn_s_down, L1_attn_Wqkv, L1_attn_Wout, L1_ffn_wg, L1_ffn_W,
           L1_ffn_b):
    x = np.asarray(L0_attn_x, dtype=np.float32)
    n_cores = 8

    pp, cc = np.meshgrid(np.arange(128), np.arange(512), indexing="ij")
    anot = np.stack([(128 * i + pp > cc) for i in range(4)], axis=0)
    anot = anot.reshape(4 * 128, 512).astype(np.uint8)

    weights = {
        "wg": np.ascontiguousarray(L0_ffn_w_gate),
        "wu": np.ascontiguousarray(L0_ffn_w_up),
        "wd": np.ascontiguousarray(L0_ffn_w_down),
        "sgv": np.ascontiguousarray(np.asarray(L0_ffn_s_gate, np.float32).reshape(64, 128).T),
        "suv": np.ascontiguousarray(np.asarray(L0_ffn_s_up, np.float32).reshape(64, 128).T),
        "sdv": np.ascontiguousarray(np.asarray(L0_ffn_s_down, np.float32).reshape(16, 128).T),
        "wqkv0": np.ascontiguousarray(np.asarray(L0_attn_w_qkv, np.float32)),
        "bqkv0": np.ascontiguousarray(np.asarray(L0_attn_b_qkv, np.float32).reshape(48, 128).T),
        "wout0": np.ascontiguousarray(np.asarray(L0_attn_w_out, np.float32)),
        "bout0": np.ascontiguousarray(np.asarray(L0_attn_b_out, np.float32).reshape(16, 128).T),
        "anot": anot,
        "n10k": np.full((128, 512), NEGMASK, dtype=np.float32),
        "wqkv1": np.ascontiguousarray(np.asarray(L1_attn_Wqkv, np.float32).T),
        "wout1": np.ascontiguousarray(np.asarray(L1_attn_Wout, np.float32).T),
        "wgr": np.ascontiguousarray(np.asarray(L1_ffn_wg, np.float32)
                                    .reshape(16, 128, 8).transpose(1, 0, 2).reshape(128, 128)),
        "wt": np.ascontiguousarray(np.asarray(L1_ffn_W, np.float32).transpose(0, 2, 1)),
        "bex": np.ascontiguousarray(np.asarray(L1_ffn_b, np.float32)),
        "idnf": np.eye(128, dtype=np.float32),
        "negc": np.full((128, 1), -30.0, dtype=np.float32),
        "onescb": np.ones((128, 1), dtype=ml_dtypes.bfloat16),
        "onesrf": np.ones((1, 128), dtype=np.float32),
        "onescf": np.ones((128, 1), dtype=np.float32),
    }

    xc = [np.ascontiguousarray(x[c].T) for c in range(n_cores)]
    for gi, group in enumerate(GROUPS):
        nc, names = _get_group(tuple(group), group)
        in_maps = []
        for c in range(n_cores):
            m = {"xin": xc[c]}
            for nm in names:
                if nm != "xin":
                    m[nm] = weights[nm]
            in_maps.append(m)
        res = run_bass_kernel_spmd(nc, in_maps, core_ids=list(range(n_cores)))
        xc = [np.asarray(res.results[c]["xout"]) for c in range(n_cores)]

    out = np.stack([np.asarray(xc[c], dtype=np.float32).T for c in range(n_cores)],
                   axis=0)
    return out


# version 15 (final)
# speedup vs baseline: 1.3013x; 1.1167x over previous
import numpy as np
import ml_dtypes
import concourse.bass as bass
import concourse.mybir as mybir
from concourse.bass_utils import run_bass_kernel_spmd
from contextlib import ExitStack

F32 = mybir.dt.float32
BF16 = mybir.dt.bfloat16
I8 = mybir.dt.int8
U8 = mybir.dt.uint8
AF = mybir.ActivationFunctionType
ALU = mybir.AluOpType
AX = mybir.AxisListType

ENGINES = ("pe", "act", "dve", "pool", "sp")
ENGINE_ATTR = {"pe": "tensor", "act": "scalar", "dve": "vector",
               "pool": "gpsimd", "sp": "sync"}

D = 2048
T = 2048
NXC = D // 128
INTER = 8192
NH = 16
HD = 128
ISQ = float(1.0 / np.sqrt(np.float32(128.0)))
EPS = 1e-5
NEGMASK = -10000.0

GROUPS = [[0, 1, 2, 3, 4, 5]]


class Sem:
    __slots__ = ("name", "cnt", "h")

    def __init__(self, name):
        self.name = name
        self.cnt = 0
        self.h = None


class Prog:
    def __init__(self, nc):
        self.nc = nc
        self.q = {e: [] for e in ENGINES}
        self.sems = []
        self._named = {}

    def sem(self, name):
        s = Sem(name)
        self.sems.append(s)
        return s

    def dsem(self, name):
        if name not in self._named:
            self._named[name] = self.sem(name)
        return self._named[name]

    def emit(self, eng, fn, wait=(), inc=()):
        wait = tuple((s, v) for (s, v) in wait if v > 0)
        inc = tuple(inc)
        for s, n in inc:
            s.cnt += n

        def run(h):
            for s, v in wait:
                h.wait_ge(s.h, v)
            instr = fn(h)
            for s, n in inc:
                instr = instr.then_inc(s.h, n)
        self.q[eng].append(run)

    def wait(self, eng, *pairs):
        pairs = tuple((s, v) for (s, v) in pairs if v > 0)
        if not pairs:
            return

        def run(h):
            for s, v in pairs:
                h.wait_ge(s.h, v)
        self.q[eng].append(run)

    def alloc_sems(self, stack):
        for s in self.sems:
            s.h = stack.enter_context(self.nc.semaphore(s.name))

    def replay(self, block):
        for e in ENGINES:
            lst = self.q[e]
            if not lst:
                continue
            getattr(block, ENGINE_ATTR[e])(lambda h, lst=lst: [f(h) for f in lst])


class Env:
    pass


def barrier(env, engines=ENGINES):
    p = env.p
    snap = []
    for s in p.sems:
        if s.cnt != env.bar_snap.get(s.name, 0):
            snap.append((s, s.cnt))
            env.bar_snap[s.name] = s.cnt
    if not snap:
        return
    for e in engines:
        p.wait(e, *snap)


def emit_ln(env, x_dram, t0, tlen, h_tiles, hcol0, bigxb, router=None):
    nc, p = env.nc, env.p
    S = env.s
    sub_done = []
    nsub = tlen // 512
    sxh = [p.dsem(f"xbh{i}") for i in range(4)]
    sxf = [p.dsem(f"xf{i}") for i in range(2)]
    with ExitStack() as st:
        sb = lambda n, shp, dt: st.enter_context(nc.sbuf_tensor(n, shp, dt))
        psa = lambda n, shp, dt: st.enter_context(nc.psum_tensor(n, shp, dt))
        uid = f"ln{env.uid}"
        env.uid += 1
        xsq = [sb(f"{uid}xq{i}", [128, 512], BF16) for i in range(2)]
        t1 = [sb(f"{uid}t1{i}", [128, 512], F32) for i in range(2)]
        if router is not None:
            xf = [sb(f"{uid}xf{i}", [128, 512], F32) for i in range(2)]
        st_mean = sb(f"{uid}mean", [1, 512], F32)
        st_var = sb(f"{uid}var", [1, 512], F32)
        st_tmp = sb(f"{uid}tmp", [1, 512], F32)
        st_r = sb(f"{uid}r", [1, 512], F32)
        st_sd = st_tmp
        st_nmr = st_var
        ps_sum = psa(f"{uid}psum", [1, 512], F32)
        ps_sq = psa(f"{uid}psq", [1, 512], F32)
        ps_nmrb = psa(f"{uid}psnm", [128, 512], F32)
        rb_sb = sb(f"{uid}rbs", [128, 512], F32)
        if router is not None:
            ps_lg = [psa(f"{uid}pslg{i}", [128, 8], F32) for i in range(4)]

        bh_g = env.rg.setdefault("bigxb", [[(None, 0)] for _ in range(4)])
        xsq_g = env.rg.setdefault("xsq", [(None, 0)] * 2)
        xf_g = env.rg.setdefault("xf", [(None, 0)] * 2)
        t1_g = env.rg.setdefault("t1", [(None, 0)] * 2)
        xt = x_dram.tensor
        for sub in range(nsub):
            ts0 = t0 + sub * 512
            bdma = [0, 0, 0, 0]
            for bh in range(4):
                p.emit("pool", lambda e, bh=bh, ts0=ts0: e.dma_start(
                    bass.AP(bigxb, bh * 2048, [[8192, 128], [512, 4], [1, 512]]),
                    bass.AP(xt, bh * 4 * 128 * T + ts0,
                            [[T, 128], [128 * T, 4], [1, 512]])),
                    wait=[(gs, gv) for (gs, gv) in bh_g[bh] if gs],
                    inc=[(sxh[bh], 16)])
                bdma[bh] = sxh[bh].cnt
            for xc in range(NXC):
                bh = xc // 4
                xbv = bigxb[:, xc * 512:(xc + 1) * 512]
                sl = xc % 2
                gs, gv = xsq_g[sl]
                p.emit("act", lambda e, sl=sl, xbv=xbv: e.square(xsq[sl][:], xbv),
                       wait=[(sxh[bh], bdma[bh])] + ([(gs, gv)] if gs else []),
                       inc=[(S.act, 1)])
                sq_cnt = S.act.cnt
                if router is not None:
                    gs, gv = xf_g[sl]
                    p.emit("sp", lambda e, xc=xc, sl=sl, ts0=ts0: e.dma_start(
                        xf[sl][:], x_dram[xc * 128:(xc + 1) * 128, ts0:ts0 + 512]),
                        wait=([(gs, gv)] if gs else []), inc=[(sxf[sl], 16)])
                    f_cnt = sxf[sl].cnt
                w = [(sxh[bh], bdma[bh])]
                if xc == 0 and sub == 0:
                    w.extend(env.const_w)
                p.emit("pe", lambda e, xbv=xbv, xc=xc: e.matmul(
                    ps_sum[:], env.onesc_bf[:], xbv,
                    start=(xc == 0), stop=(xc == NXC - 1)), wait=w, inc=[(S.pe, 1)])
                p.emit("pe", lambda e, sl=sl, xc=xc: e.matmul(
                    ps_sq[:], env.onesc_bf[:], xsq[sl][:],
                    start=(xc == 0), stop=(xc == NXC - 1)),
                    wait=[(S.act, sq_cnt)], inc=[(S.pe, 1)])
                xsq_g[sl] = (S.pe, S.pe.cnt)
                if router is not None:
                    for tc4 in range(4):
                        p.emit("pe", lambda e, sl=sl, xc=xc, tc4=tc4: e.matmul(
                            ps_lg[tc4][:],
                            xf[sl][:, tc4 * 128:(tc4 + 1) * 128],
                            router["wg_sb"][:, xc * 8:(xc + 1) * 8],
                            start=(xc == 0), stop=(xc == NXC - 1)),
                            wait=[(sxf[sl], f_cnt)], inc=[(S.pe, 1)])
                    xf_g[sl] = (S.pe, S.pe.cnt)
            stats_pe = S.pe.cnt
            p.emit("dve", lambda e: e.tensor_scalar_mul(st_mean[:], ps_sum[:], 1.0 / D),
                   wait=[(S.pe, stats_pe)], inc=[(S.dve, 1)])
            p.emit("dve", lambda e: e.tensor_scalar_mul(st_var[:], ps_sq[:], 1.0 / D),
                   inc=[(S.dve, 1)])
            p.emit("dve", lambda e: e.tensor_mul(st_tmp[:], st_mean[:], st_mean[:]),
                   wait=[(S.dve, S.dve.cnt)], inc=[(S.dve, 1)])
            p.emit("dve", lambda e: e.scalar_tensor_tensor(
                st_var[:], st_var[:], EPS, st_tmp[:], ALU.add, ALU.subtract),
                wait=[(S.dve, S.dve.cnt)], inc=[(S.dve, 1)])
            var_cnt = S.dve.cnt
            p.emit("act", lambda e: e.sqrt(st_sd[:], st_var[:]),
                   wait=[(S.dve, var_cnt)], inc=[(S.act, 1)])
            sd_cnt = S.act.cnt
            p.emit("dve", lambda e: e.reciprocal(st_r[:], st_sd[:]),
                   wait=[(S.act, sd_cnt)], inc=[(S.dve, 1)])
            p.emit("dve", lambda e: e.scalar_tensor_tensor(
                st_nmr[:], st_mean[:], -1.0, st_r[:], ALU.mult, ALU.mult),
                wait=[(S.dve, S.dve.cnt)], inc=[(S.dve, 1)])
            fin_cnt = S.dve.cnt
            p.emit("pe", lambda e: e.matmul(ps_nmrb[:], env.onesr_f[:], st_r[0:1, :],
                                            start=True, stop=True),
                   wait=[(S.dve, fin_cnt)], inc=[(S.pe, 1)])
            rbmm = S.pe.cnt
            p.emit("dve", lambda e: e.tensor_copy(rb_sb[:], ps_nmrb[:]),
                   wait=[(S.pe, rbmm)], inc=[(S.dve, 1)])
            rbev = S.dve.cnt
            p.emit("pe", lambda e: e.matmul(ps_nmrb[:], env.onesr_f[:], st_nmr[0:1, :],
                                            start=True, stop=True),
                   wait=[(S.dve, rbev)], inc=[(S.pe, 1)])
            bc_cnt = S.pe.cnt
            if router is not None:
                for tc4 in range(4):
                    tc = (t0 - router["t0"]) // 128 + sub * 4 + tc4
                    p.emit("dve", lambda e, tc4=tc4, tc=tc: e.tensor_copy(
                        router["raw"][:, tc * 8:(tc + 1) * 8], ps_lg[tc4][:]),
                        wait=[(S.pe, stats_pe)] if tc4 == 0 else [],
                        inc=[(S.dve, 1)])
                raw_ev = S.dve.cnt
                for tc4 in range(4):
                    tc = (t0 - router["t0"]) // 128 + sub * 4 + tc4
                    p.emit("pe", lambda e, tc4=tc4: e.matmul(
                        ps_lg[0][:, 0:1], st_mean[0:1, tc4 * 128:(tc4 + 1) * 128],
                        env.onesr_f[0:1, 0:1], start=True, stop=True),
                        wait=[(S.dve, max(raw_ev, fin_cnt))] if tc4 == 0 else
                             [(S.dve, S.dve.cnt)],
                        inc=[(S.pe, 1)])
                    mcnt = S.pe.cnt
                    p.emit("dve", lambda e, tc=tc: e.tensor_copy(
                        router["mtok"][:, tc:tc + 1], ps_lg[0][:, 0:1]),
                        wait=[(S.pe, mcnt)], inc=[(S.dve, 1)])
                    p.emit("pe", lambda e, tc4=tc4: e.matmul(
                        ps_lg[0][:, 1:2], st_r[0:1, tc4 * 128:(tc4 + 1) * 128],
                        env.onesr_f[0:1, 0:1], start=True, stop=True),
                        wait=[(S.dve, S.dve.cnt)], inc=[(S.pe, 1)])
                    rcnt = S.pe.cnt
                    p.emit("dve", lambda e, tc=tc: e.tensor_copy(
                        router["rtok"][:, tc:tc + 1], ps_lg[0][:, 1:2]),
                        wait=[(S.pe, rcnt)], inc=[(S.dve, 1)])
            p.wait("dve", (S.pe, bc_cnt))
            half_apply = [0, 0, 0, 0]
            for xc in range(NXC):
                xbv = bigxb[:, xc * 512:(xc + 1) * 512]
                sl1 = xc % 2
                gs, gv = t1_g[sl1]
                p.emit("dve", lambda e, xbv=xbv, sl1=sl1: e.tensor_mul(
                    t1[sl1][:], xbv, rb_sb[:]),
                    wait=([(gs, gv)] if gs else []), inc=[(S.dve, 1)])
                p.emit("dve", lambda e, sl1=sl1, xc=xc, col=hcol0 + sub * 512: e.tensor_add(
                    h_tiles[xc][:, col:col + 512], t1[sl1][:], ps_nmrb[:]),
                    wait=[(S.dve, S.dve.cnt)], inc=[(S.dve, 1)])
                t1_g[sl1] = (S.dve, S.dve.cnt)
                half_apply[xc // 4] = S.dve.cnt
            for bh in range(4):
                bh_g[bh] = [(S.dve, half_apply[bh]), (S.pe, stats_pe)]
            sub_done.append(S.dve.cnt)
    return sub_done


def emit_ffn0(env, x_in, x_out, W):
    nc, p = env.nc, env.p
    S = env.s
    NOG = 16
    swgu = [p.dsem(f"wgu{i}") for i in range(2)]
    swd = [p.dsem(f"wd{i}") for i in range(2)]
    sxr = [p.dsem(f"xr{i}") for i in range(4)]
    sxn = [p.dsem(f"xn{i}") for i in range(4)]

    def _half(half):
        t0 = half * 1024
        with ExitStack() as st:
            sb = lambda n, shp, dt: st.enter_context(nc.sbuf_tensor(n, shp, dt))
            psa = lambda n, shp, dt: st.enter_context(nc.psum_tensor(n, shp, dt))
            uid = f"f{env.uid}"
            env.uid += 1
            h = [sb(f"{uid}h{i}", [128, 1024], BF16) for i in range(NXC)]
            wgu = [sb(f"{uid}wgu{r}", [128, NXC * 512], BF16) for r in range(2)]
            wd = [sb(f"{uid}wd{r}", [128, 4 * 1024], BF16) for r in range(2)]
            gf = sb(f"{uid}gf", [128, 8192], BF16)
            y = [sb(f"{uid}y{i}", [128, 1024], F32) for i in range(NXC)]
            xnw = [sb(f"{uid}xn{i}", [128, 512], F32) for i in range(4)]
            xfr = [sb(f"{uid}xr{i}", [128, 512], F32) for i in range(4)]
            psg = [psa(f"{uid}psg{r}", [128, 512], F32) for r in range(3)]
            psy = [psa(f"{uid}psy{r}", [128, 512], F32) for r in range(2)]

            def load_wgu(slot, Wm, og, guard):
                p.emit("pool", lambda e, slot=slot, og=og, Wm=Wm: e.dma_start(
                    bass.AP(wgu[slot], 0, [[NXC * 512, 128], [512, NXC], [1, 512]]),
                    bass.AP(Wm.tensor, og * 512,
                            [[INTER, 128], [128 * INTER, NXC], [1, 512]])),
                    wait=[guard] if guard[1] > 0 else [],
                    inc=[(swgu[slot], 16)])
                return swgu[slot].cnt

            def load_wd(slot, og, o2h, guard):
                p.emit("pool", lambda e, slot=slot, og=og, o2h=o2h: e.dma_start(
                    bass.AP(wd[slot], 0, [[4 * 1024, 128], [1024, 4], [1, 1024]]),
                    bass.AP(W["wd"].tensor, og * 512 * D + o2h * 1024,
                            [[D, 128], [128 * D, 4], [1, 1024]])),
                    wait=[guard] if guard[1] > 0 else [],
                    inc=[(swd[slot], 16)])
                return swd[slot].cnt

            wg_ready = [0, 0]
            wu_ready = [0, 0]
            wd_ready = [0, 0]
            gate_done = [0, 0]
            up_done = [0, 0]
            wd_mm_done = [0, 0]
            psg_free = [0, 0, 0]
            psg_free_eng = [None, None, None]
            psy_free = [0, 0]
            wg_ready[0] = load_wgu(0, W["wg"], 0, (S.pe, 0))
            wu_ready[1] = load_wgu(1, W["wu"], 0, (S.pe, 0))
            wd_ready[0] = load_wd(0, 0, 0, (S.pe, 0))
            h_sub = emit_ln(env, x_in, t0, 1024, h, 0, gf)
            h_all = h_sub[-1]
            f_done_prev = 0
            for og in range(NOG):
                s_g = og % 2
                s_u = 1 - s_g
                s_d = og % 2
                if og > 0:
                    p.wait("act", (S.dve, f_done_prev))
                p.wait("pe", (swgu[s_g], wg_ready[s_g]))
                for ts in range(2):
                    p.wait("pe", (S.dve, h_sub[ts]))
                    for j in range(4):
                        oc = og * 4 + j
                        bank = (ts * 4 + j) % 3
                        if psg_free_eng[bank] is not None:
                            p.wait("pe", (psg_free_eng[bank], psg_free[bank]))
                        for c in range(NXC):
                            p.emit("pe", lambda e, c=c, j=j, ts=ts, bank=bank, s_g=s_g: e.matmul(
                                psg[bank][:],
                                wgu[s_g][:, c * 512 + j * 128:c * 512 + (j + 1) * 128],
                                h[c][:, ts * 512:(ts + 1) * 512],
                                start=(c == 0), stop=(c == NXC - 1)),
                                inc=[(S.pe, 1)] if c == NXC - 1 else [])
                        mile = S.pe.cnt
                        p.emit("act", lambda e, j=j, ts=ts, oc=oc, bank=bank: e.activation(
                            gf[:, j * 1024 + ts * 512:j * 1024 + (ts + 1) * 512],
                            psg[bank][:], AF.Silu, scale=env.sg_sb[:, oc:oc + 1]),
                            wait=[(S.pe, mile)] +
                                 ([(S.dve, h_all)] if og == 0 and ts == 0 else []),
                            inc=[(S.act, 1)])
                        psg_free[bank] = S.act.cnt
                        psg_free_eng[bank] = S.act
                gate_done[s_g] = S.pe.cnt
                p.wait("pe", (swgu[s_u], wu_ready[s_u]))
                for ts in range(2):
                    for j in range(4):
                        oc = og * 4 + j
                        bank = (ts * 4 + j) % 3
                        p.wait("pe", (psg_free_eng[bank], psg_free[bank]))
                        for c in range(NXC):
                            p.emit("pe", lambda e, c=c, j=j, ts=ts, bank=bank, s_u=s_u: e.matmul(
                                psg[bank][:],
                                wgu[s_u][:, c * 512 + j * 128:c * 512 + (j + 1) * 128],
                                h[c][:, ts * 512:(ts + 1) * 512],
                                start=(c == 0), stop=(c == NXC - 1)),
                                inc=[(S.pe, 1)] if c == NXC - 1 else [])
                        mile = S.pe.cnt
                        acnt = S.act.cnt
                        p.emit("dve", lambda e, j=j, ts=ts, oc=oc, bank=bank: e.scalar_tensor_tensor(
                            gf[:, 4096 + j * 1024 + ts * 512:4096 + j * 1024 + (ts + 1) * 512],
                            psg[bank][:], env.su_sb[:, oc:oc + 1],
                            gf[:, j * 1024 + ts * 512:j * 1024 + (ts + 1) * 512],
                            ALU.mult, ALU.mult),
                            wait=[(S.pe, mile), (S.act, acnt)] +
                                 ([(S.dve, h_all)] if og == 0 and ts == 0 else []),
                            inc=[(S.dve, 1)])
                        psg_free[bank] = S.dve.cnt
                        psg_free_eng[bank] = S.dve
                up_done[s_u] = S.pe.cnt
                f_ready = S.dve.cnt
                f_done_prev = f_ready
                if og + 1 < NOG:
                    wg_ready[1 - s_g] = load_wgu(1 - s_g, W["wg"], og + 1,
                                                 (S.pe, up_done[s_u]))
                    wu_ready[1 - s_u] = load_wgu(1 - s_u, W["wu"], og + 1,
                                                 (S.pe, gate_done[s_g]))
                p.wait("pe", (S.dve, f_ready))
                for o2h in range(2):
                    u = og * 2 + o2h
                    wsl = u % 2
                    if u + 1 < 2 * NOG:
                        wd_ready[1 - wsl] = load_wd(
                            1 - wsl, (u + 1) // 2, (u + 1) % 2,
                            (S.pe, wd_mm_done[1 - wsl]))
                    p.wait("pe", (swd[wsl], wd_ready[wsl]))
                    for o2i in range(8):
                        o2c = o2h * 8 + o2i
                        for ts in range(2):
                            bank = (o2i * 2 + ts) % 2
                            p.wait("pe", (S.dve, psy_free[bank]))
                            for j in range(4):
                                p.emit("pe", lambda e, j=j, o2i=o2i, ts=ts, bank=bank, wsl=wsl: e.matmul(
                                    psy[bank][:],
                                    wd[wsl][:, j * 1024 + o2i * 128:j * 1024 + (o2i + 1) * 128],
                                    gf[:, 4096 + j * 1024 + ts * 512:4096 + j * 1024 + (ts + 1) * 512],
                                    start=(j == 0), stop=(j == 3)),
                                    inc=[(S.pe, 1)] if j == 3 else [])
                            mile = S.pe.cnt
                            if og == 0:
                                p.emit("dve", lambda e, o2c=o2c, ts=ts, bank=bank: e.tensor_copy(
                                    y[o2c][:, ts * 512:(ts + 1) * 512], psy[bank][:]),
                                    wait=[(S.pe, mile)], inc=[(S.dve, 1)])
                            else:
                                p.emit("dve", lambda e, o2c=o2c, ts=ts, bank=bank: e.tensor_add(
                                    y[o2c][:, ts * 512:(ts + 1) * 512],
                                    y[o2c][:, ts * 512:(ts + 1) * 512], psy[bank][:]),
                                    wait=[(S.pe, mile)], inc=[(S.dve, 1)])
                            psy_free[bank] = S.dve.cnt
                    wd_mm_done[wsl] = S.pe.cnt
            xg = env.rg.setdefault("epixg", [(None, 0)] * 4)
            for o2c in range(NXC):
                for ts in range(2):
                    sl = (o2c * 2 + ts) % 4
                    gs, gv = xg[sl]
                    p.emit("sp", lambda e, o2c=o2c, ts=ts, sl=sl, t0=t0: e.dma_start(
                        xfr[sl][:], x_in[o2c * 128:(o2c + 1) * 128,
                                         t0 + ts * 512:t0 + (ts + 1) * 512]),
                        wait=([(gs, gv)] if gs else []), inc=[(sxr[sl], 16)])
                    xcnt = sxr[sl].cnt
                    p.emit("dve", lambda e, o2c=o2c, ts=ts, sl=sl: e.scalar_tensor_tensor(
                        xnw[sl][:], y[o2c][:, ts * 512:(ts + 1) * 512],
                        env.sdh_sb[:, o2c:o2c + 1], xfr[sl][:], ALU.mult, ALU.add),
                        wait=[(sxr[sl], xcnt), (sxn[sl], sxn[sl].cnt)],
                        inc=[(S.dve, 1)])
                    xg[sl] = (S.dve, S.dve.cnt)
                    dcnt = S.dve.cnt
                    p.emit("act", lambda e, o2c=o2c, ts=ts, sl=sl, t0=t0: e.dma_start(
                        x_out[o2c * 128:(o2c + 1) * 128,
                              t0 + ts * 512:t0 + (ts + 1) * 512], xnw[sl][:]),
                        wait=[(S.dve, dcnt)], inc=[(sxn[sl], 16)])
            barrier(env)

    _half(0)
    _half(1)


def emit_attn(env, x_in, x_out, cfg):
    nc, p = env.nc, env.p
    S = env.s
    swqk = [p.dsem(f"wqk{i}") for i in range(2)]
    swv = [p.dsem(f"wv{i}") for i in range(2)]
    sksp = [p.dsem(f"ksp{i}") for i in range(2)]
    svsp = [p.dsem(f"vsp{i}") for i in range(2)]
    skb = [p.dsem(f"kb{i}") for i in range(2)]
    svb = [p.dsem(f"vb{i}") for i in range(2)]
    sqb = [p.dsem(f"qb{i}") for i in range(2)]
    swo = [p.dsem(f"wo{i}") for i in range(2)]
    sxr = [p.dsem(f"xr{i}") for i in range(4)]
    sxn = [p.dsem(f"xn{i}") for i in range(4)]
    with ExitStack() as slst:
        sbl = lambda n, shp, dt: slst.enter_context(nc.sbuf_tensor(n, shp, dt))
        uid = f"a{env.uid}"
        env.uid += 1
        qres = [sbl(f"{uid}q{i}", [128, T], BF16) for i in range(NH)]
        if cfg["l0_bias"]:
            bqkv = sbl(f"{uid}bqkv", [128, 48], F32)
            bqs = sbl(f"{uid}bqs", [128, 16], F32)
            bout = sbl(f"{uid}bout", [128, 16], F32)
            sbq = p.dsem(f"{uid}bq")
            p.emit("act", lambda e: e.dma_start(bqkv[:], cfg["bqkv"]),
                   inc=[(sbq, 16)])
            p.emit("act", lambda e: e.dma_start(bout[:], cfg["bout"]),
                   inc=[(sbq, 16)])
            bv_cnt = 32
            for j in range(16):
                p.emit("dve", lambda e, j=j: e.tensor_scalar_mul(
                    bqs[:, j:j + 1], bqkv[:, 3 * j:3 * j + 1], ISQ),
                    wait=[(sbq, bv_cnt)] if j == 0 else [], inc=[(S.dve, 1)])
            bqs_done = S.dve.cnt

        with ExitStack() as st:
            sb = lambda n, shp, dt: st.enter_context(nc.sbuf_tensor(n, shp, dt))
            psa = lambda n, shp, dt: st.enter_context(nc.psum_tensor(n, shp, dt))
            h = [sb(f"{uid}h{i}", [128, T], BF16) for i in range(NXC)]
            wqk = [sb(f"{uid}wqk{r}", [128, NXC * 256], BF16) for r in range(2)]
            wv = [sb(f"{uid}wv{r}", [128, NXC * 512], BF16) for r in range(2)]
            ksb = [sb(f"{uid}ksb0", [128, T], BF16)]
            vsb = [sb(f"{uid}vsb{r}", [128, 512], BF16) for r in range(2)]
            psq = [psa(f"{uid}psq{r}", [128, 512], F32) for r in range(2)]

            def load_wqk(slot, hh, guard):
                for (c0, wd_, d0) in cfg["qk_slices"](hh):
                    p.emit("pool", lambda e, slot=slot, c0=c0, wd_=wd_, d0=d0: e.dma_start(
                        bass.AP(wqk[slot], d0,
                                [[NXC * 256, 128], [256, NXC], [1, wd_]]),
                        bass.AP(cfg["wqkv"].tensor, c0,
                                [[3 * D, 128], [128 * 3 * D, NXC], [1, wd_]])),
                        wait=[guard] if (d0 == 0 and guard[1] > 0) else [],
                        inc=[(swqk[slot], 16)])
                return swqk[slot].cnt

            def load_wv(slot, g, guard):
                for (c0, wd_, d0) in cfg["v_slices"](g):
                    p.emit("pool", lambda e, slot=slot, c0=c0, wd_=wd_, d0=d0: e.dma_start(
                        bass.AP(wv[slot], d0,
                                [[NXC * 512, 128], [512, NXC], [1, wd_]]),
                        bass.AP(cfg["wqkv"].tensor, c0,
                                [[3 * D, 128], [128 * 3 * D, NXC], [1, wd_]])),
                        wait=[guard] if (d0 == 0 and guard[1] > 0) else [],
                        inc=[(swv[slot], 16)])
                return swv[slot].cnt

            wqk_ready = [0, 0]
            qk_done = [0, 0]
            psq_free = [0, 0]
            ksp_g = [0, 0]
            vsp_g = [0, 0]
            wqk_ready[0] = load_wqk(0, 0, (S.pe, 0))
            h_sub = emit_ln(env, x_in, 0, 1024, h, 0, wv[0])
            h_sub += emit_ln(env, x_in, 1024, 1024, h, 1024, wv[0])
            h_all = h_sub[-1]
            for hh in range(NH):
                sl = hh % 2
                if hh + 1 < NH:
                    wqk_ready[1 - sl] = load_wqk(1 - sl, hh + 1, (S.pe, qk_done[1 - sl]))
                p.wait("pe", (swqk[sl], wqk_ready[sl]))
                for part in range(2):
                    for ts in range(4):
                        bank = ts % 2
                        p.wait("pe", (S.dve, psq_free[bank]), (S.dve, h_sub[ts]))
                        for c in range(NXC):
                            p.emit("pe", lambda e, c=c, sl=sl, part=part, ts=ts, bank=bank: e.matmul(
                                psq[bank][:],
                                wqk[sl][:, c * 256 + part * 128:c * 256 + (part + 1) * 128],
                                h[c][:, ts * 512:(ts + 1) * 512],
                                start=(c == 0), stop=(c == NXC - 1)),
                                inc=[(S.pe, 1)] if c == NXC - 1 else [])
                        mile = S.pe.cnt
                        if part == 0:
                            if cfg["l0_bias"]:
                                p.emit("dve", lambda e, hh=hh, ts=ts, bank=bank: e.tensor_scalar(
                                    qres[hh][:, ts * 512:(ts + 1) * 512], psq[bank][:],
                                    ISQ, bqs[:, hh:hh + 1], ALU.mult, ALU.add),
                                    wait=[(S.pe, mile), (S.dve, bqs_done)],
                                    inc=[(S.dve, 1)])
                            else:
                                p.emit("dve", lambda e, hh=hh, ts=ts, bank=bank: e.tensor_scalar_mul(
                                    qres[hh][:, ts * 512:(ts + 1) * 512], psq[bank][:],
                                    ISQ), wait=[(S.pe, mile)], inc=[(S.dve, 1)])
                        else:
                            wk = [(S.pe, mile)]
                            if ts == 0:
                                wk.append((sksp[0], ksp_g[0]))
                            if cfg["l0_bias"]:
                                p.emit("dve", lambda e, sl=sl, ts=ts, bank=bank, hh=hh: e.tensor_scalar_add(
                                    ksb[0][:, ts * 512:(ts + 1) * 512], psq[bank][:],
                                    bqkv[:, 3 * hh + 1:3 * hh + 2]),
                                    wait=wk, inc=[(S.dve, 1)])
                            else:
                                p.emit("dve", lambda e, sl=sl, ts=ts, bank=bank: e.tensor_copy(
                                    ksb[0][:, ts * 512:(ts + 1) * 512], psq[bank][:]),
                                    wait=wk, inc=[(S.dve, 1)])
                        psq_free[bank] = S.dve.cnt
                qk_done[sl] = S.pe.cnt
                kcnt = S.dve.cnt
                p.emit("sp", lambda e, hh=hh, sl=sl: e.dma_start(
                    env.Ks[hh * 128:(hh + 1) * 128, :], ksb[0][:]),
                    wait=[(S.dve, kcnt)], inc=[(sksp[0], 16)])
                ksp_g[0] = sksp[0].cnt
            wv_ready = [0, 0]
            v_done = [0, 0]
            p.wait("pool", (S.dve, h_all))
            wv_ready[0] = load_wv(0, 0, (S.pe, 0))
            for g in range(4):
                sl = g % 2
                if g + 1 < 4:
                    wv_ready[1 - sl] = load_wv(1 - sl, g + 1, (S.pe, v_done[1 - sl]))
                p.wait("pe", (swv[sl], wv_ready[sl]))
                for tc in range(NXC):
                    bank = tc % 2
                    p.wait("pe", (S.dve, psq_free[bank]), (S.dve, h_sub[tc // 4]))
                    for c in range(NXC):
                        p.emit("pe", lambda e, c=c, sl=sl, tc=tc, bank=bank: e.matmul(
                            psq[bank][:], h[c][:, tc * 128:(tc + 1) * 128],
                            wv[sl][:, c * 512:(c + 1) * 512],
                            start=(c == 0), stop=(c == NXC - 1)),
                            inc=[(S.pe, 1)] if c == NXC - 1 else [])
                    mile = S.pe.cnt
                    vsl = tc % 2
                    p.emit("dve", lambda e, vsl=vsl, bank=bank: e.tensor_copy(
                        vsb[vsl][:], psq[bank][:]),
                        wait=[(S.pe, mile), (svsp[vsl], vsp_g[vsl])],
                        inc=[(S.dve, 1)])
                    psq_free[bank] = S.dve.cnt
                    vcnt = S.dve.cnt
                    p.emit("sp", lambda e, tc=tc, g=g, vsl=vsl: e.dma_start(
                        env.Vs[tc * 128:(tc + 1) * 128, g * 512:(g + 1) * 512],
                        vsb[vsl][:]), wait=[(S.dve, vcnt)], inc=[(svsp[vsl], 16)])
                    vsp_g[vsl] = svsp[vsl].cnt
                v_done[sl] = S.pe.cnt
            barrier(env)

        st23 = slst.enter_context(ExitStack())
        ctx = [st23.enter_context(nc.sbuf_tensor(f"{uid}c{i}", [128, T], BF16))
               for i in range(NH)]

        with ExitStack() as st:
            sb = lambda n, shp, dt: st.enter_context(nc.sbuf_tensor(n, shp, dt))
            psa = lambda n, shp, dt: st.enter_context(nc.psum_tensor(n, shp, dt))
            kbuf = [sb(f"{uid}kb{r}", [128, T], BF16) for r in range(2)]
            vbuf = [sb(f"{uid}vb{r}", [128, T], BF16) for r in range(2)]
            eTg = [sb(f"{uid}eT{r}", [128, NXC * 512], BF16) for r in range(2)]
            negc = sb(f"{uid}negc", [128, 1], F32)
            zrow = sb(f"{uid}zr", [1, 512], F32)
            zb = sb(f"{uid}zb", [128, 512], F32)
            tx = sb(f"{uid}tx", [128, 512], F32)
            psS = [psa(f"{uid}psS{r}", [128, 1024], F32) for r in range(2)]
            psZ = psa(f"{uid}psZ", [1, 512], F32)
            psB = psa(f"{uid}psB", [128, 512], F32)
            psC = psa(f"{uid}psC", [128, 512], F32)
            sneg = p.dsem(f"{uid}neg")
            p.emit("act", lambda e: e.dma_start(negc[:], env.negc_d),
                   inc=[(sneg, 16)])
            negc_cnt = sneg.cnt
            if cfg["causal"]:
                anot = sb(f"{uid}anot", [128, 4 * 512], U8)
                n10k = sb(f"{uid}n10k", [128, 512], F32)
                smsk = p.dsem(f"{uid}msk")
                for i in range(4):
                    p.emit("act", lambda e, i=i: e.dma_start(
                        anot[:, i * 512:(i + 1) * 512],
                        env.anot_d[i * 128:(i + 1) * 128, :]), inc=[(smsk, 16)])
                p.emit("act", lambda e: e.dma_start(n10k[:], env.n10k_d),
                       inc=[(smsk, 16)])
                p.wait("dve", (smsk, 80))

            kv_ready = [0, 0]
            head_done = [0, 0]

            def load_kv(slot, hh, guard):
                p.emit("act", lambda e, hh=hh, slot=slot: e.dma_start(
                    kbuf[slot][:], env.Ks[hh * 128:(hh + 1) * 128, :]),
                    wait=[guard] if guard[1] > 0 else [], inc=[(skb[slot], 16)])
                vc0 = (hh // 4) * 512 + (hh % 4) * 128
                for tc in range(NXC):
                    p.emit("act", lambda e, tc=tc, slot=slot, vc0=vc0: e.dma_start(
                        vbuf[slot][:, tc * 128:(tc + 1) * 128],
                        env.Vs[tc * 128:(tc + 1) * 128, vc0:vc0 + 128]),
                        inc=[(svb[slot], 16)])
                return (skb[slot].cnt, svb[slot].cnt)

            kv_ready[0] = load_kv(0, 0, (S.pe, 0))
            exp_free = [0, 0]
            eT_free = [0, 0]
            war = {"z": 0, "b": 0, "c": 0}
            pend = [None]
            first_exp = [True]

            def emit_bc(pd):
                hh2, g2, sl2, esl2, lastexp, nk2 = pd
                for kc in range(nk2):
                    wz = [(S.act, lastexp), (S.dve, war["z"])] if kc == 0 else []
                    p.emit("pe", lambda e, kc=kc, nk2=nk2, esl2=esl2: e.matmul(
                        psZ[:], env.onesc_bf[:],
                        eTg[esl2][:, kc * 512:(kc + 1) * 512],
                        start=(kc == 0), stop=(kc == nk2 - 1)),
                        wait=wz, inc=[(S.pe, 1)] if kc == nk2 - 1 else [])
                bz = S.pe.cnt
                for kc in range(nk2):
                    wc = [(S.dve, war["c"])] if kc == 0 else []
                    p.emit("pe", lambda e, kc=kc, nk2=nk2, sl2=sl2, esl2=esl2: e.matmul(
                        psC[:], vbuf[sl2][:, kc * 128:(kc + 1) * 128],
                        eTg[esl2][:, kc * 512:(kc + 1) * 512],
                        start=(kc == 0), stop=(kc == nk2 - 1)),
                        wait=wc, inc=[(S.pe, 1)] if kc == nk2 - 1 else [])
                cm = S.pe.cnt
                eT_free[esl2] = cm
                p.emit("dve", lambda e: e.tensor_copy(zrow[:], psZ[0:1, :]),
                       wait=[(S.pe, bz), (S.dve, S.dve.cnt)], inc=[(S.dve, 1)])
                war["z"] = S.dve.cnt
                p.emit("pe", lambda e: e.matmul(psB[:], env.onesr_f[:],
                                                zrow[0:1, :],
                                                start=True, stop=True),
                       wait=[(S.dve, war["z"]), (S.dve, war["b"])],
                       inc=[(S.pe, 1)])
                bm = S.pe.cnt
                p.emit("dve", lambda e: e.reciprocal(zb[:], psB[:]),
                       wait=[(S.pe, bm), (S.dve, S.dve.cnt)], inc=[(S.dve, 1)])
                war["b"] = S.dve.cnt
                if cfg["l0_bias"]:
                    p.emit("dve", lambda e: e.tensor_mul(tx[:], psC[:], zb[:]),
                           wait=[(S.pe, cm), (S.dve, S.dve.cnt)],
                           inc=[(S.dve, 1)])
                    p.emit("dve", lambda e, hh2=hh2, g2=g2: e.tensor_scalar_add(
                        ctx[hh2][:, g2 * 512:(g2 + 1) * 512], tx[:],
                        bqkv[:, 3 * hh2 + 2:3 * hh2 + 3]),
                        wait=[(S.dve, S.dve.cnt)], inc=[(S.dve, 1)])
                else:
                    p.emit("dve", lambda e, hh2=hh2, g2=g2: e.tensor_mul(
                        ctx[hh2][:, g2 * 512:(g2 + 1) * 512], psC[:], zb[:]),
                        wait=[(S.pe, cm), (S.dve, S.dve.cnt)],
                        inc=[(S.dve, 1)])
                war["c"] = S.dve.cnt
                if g2 == 3:
                    head_done[sl2] = cm

            for hh in range(NH):
                sl = hh % 2
                p.wait("pe", (skb[sl], kv_ready[sl][0]), (svb[sl], kv_ready[sl][1]))
                for g in range(4):
                    nk = 4 * (g + 1) if cfg["causal"] else 16
                    esl = g % 2
                    for kp in range(nk // 2):
                        slot = kp % 2
                        p.wait("pe", (S.act, exp_free[slot]))
                        for half2 in range(2):
                            kc = kp * 2 + half2
                            p.emit("pe", lambda e, hh=hh, g=g, kc=kc, sl=sl, slot=slot, half2=half2: e.matmul(
                                psS[slot][:, half2 * 512:(half2 + 1) * 512],
                                kbuf[sl][:, kc * 128:(kc + 1) * 128],
                                qres[hh][:, g * 512:(g + 1) * 512],
                                start=True, stop=True),
                                inc=[(S.pe, 1)] if half2 == 1 else [])
                        mileb = S.pe.cnt
                        wexp = [(S.pe, mileb)]
                        if first_exp[0]:
                            wexp.append((sneg, negc_cnt))
                            first_exp[0] = False
                        if kp == 0:
                            wexp.append((S.pe, eT_free[esl]))
                        if cfg["causal"] and kp * 2 + 1 >= 4 * g:
                            for half2 in range(2):
                                kc = kp * 2 + half2
                                if kc >= 4 * g:
                                    ci = kc - 4 * g
                                    p.emit("dve", lambda e, ci=ci, slot=slot, half2=half2: e.copy_predicated(
                                        psS[slot][:, half2 * 512:(half2 + 1) * 512],
                                        anot[:, ci * 512:(ci + 1) * 512], n10k[:]),
                                        wait=[(S.pe, mileb), (S.dve, S.dve.cnt)],
                                        inc=[(S.dve, 1)])
                            wexp.append((S.dve, S.dve.cnt))
                        p.emit("act", lambda e, kp=kp, slot=slot, esl=esl: e.activation(
                            eTg[esl][:, kp * 1024:(kp + 1) * 1024], psS[slot][:],
                            AF.Exp, bias=negc[:, 0:1]),
                            wait=wexp, inc=[(S.act, 1)])
                        exp_free[slot] = S.act.cnt
                    newpend = (hh, g, sl, esl, S.act.cnt, nk)
                    if pend[0] is not None:
                        emit_bc(pend[0])
                    pend[0] = newpend
                    if g == 0 and hh + 1 < NH:
                        kv_ready[1 - sl] = load_kv(1 - sl, hh + 1,
                                                   (S.pe, head_done[1 - sl]))
            emit_bc(pend[0])
            barrier(env)

        with ExitStack() as st:
            sb = lambda n, shp, dt: st.enter_context(nc.sbuf_tensor(n, shp, dt))
            psa = lambda n, shp, dt: st.enter_context(nc.psum_tensor(n, shp, dt))
            wo = [sb(f"{uid}wo{r}", [128, NXC * 1024], BF16) for r in range(2)]
            xfr = [sb(f"{uid}xr{i}", [128, 512], F32) for i in range(2)]
            xnw = [sb(f"{uid}xn{i}", [128, 512], F32) for i in range(2)]
            pso = [psa(f"{uid}pso{r}", [128, 512], F32) for r in range(2)]

            def load_wo(slot, ocg):
                p.emit("pool", lambda e, slot=slot, ocg=ocg: e.dma_start(
                    bass.AP(wo[slot], 0, [[NXC * 1024, 128], [1024, NXC], [1, 1024]]),
                    bass.AP(cfg["wout"].tensor, ocg * 1024,
                            [[D, 128], [128 * D, NXC], [1, 1024]])),
                    inc=[(swo[slot], 16)])
                return swo[slot].cnt

            wo_ready = [load_wo(0, 0), load_wo(1, 1)]
            pso_free = [0, 0]
            xg = [(None, 0)] * 2
            for ocg in range(2):
                sl = ocg
                p.wait("pe", (swo[sl], wo_ready[sl]))
                for oi in range(8):
                    oc = ocg * 8 + oi
                    for ts in range(4):
                        bank = (oi * 4 + ts) % 2
                        p.wait("pe", (S.dve, pso_free[bank]))
                        for c in range(NXC):
                            p.emit("pe", lambda e, c=c, sl=sl, oi=oi, ts=ts, bank=bank: e.matmul(
                                pso[bank][:],
                                wo[sl][:, c * 1024 + oi * 128:c * 1024 + (oi + 1) * 128],
                                ctx[c][:, ts * 512:(ts + 1) * 512],
                                start=(c == 0), stop=(c == NXC - 1)),
                                inc=[(S.pe, 1)] if c == NXC - 1 else [])
                        mile = S.pe.cnt
                        sl2 = (oi * 4 + ts) % 2
                        gs, gv = xg[sl2]
                        p.emit("sp", lambda e, oc=oc, ts=ts, sl2=sl2: e.dma_start(
                            xfr[sl2][:], x_in[oc * 128:(oc + 1) * 128,
                                             ts * 512:(ts + 1) * 512]),
                            wait=([(gs, gv)] if gs else []), inc=[(sxr[sl2], 16)])
                        xcnt = sxr[sl2].cnt
                        if cfg["l0_bias"]:
                            p.emit("dve", lambda e, oc=oc, bank=bank, sl2=sl2: e.scalar_tensor_tensor(
                                xnw[sl2][:], pso[bank][:], bout[:, oc:oc + 1],
                                xfr[sl2][:], ALU.add, ALU.add),
                                wait=[(S.pe, mile), (sxr[sl2], xcnt),
                                      (sxn[sl2], sxn[sl2].cnt)],
                                inc=[(S.dve, 1)])
                        else:
                            p.emit("dve", lambda e, bank=bank, sl2=sl2: e.tensor_add(
                                xnw[sl2][:], pso[bank][:], xfr[sl2][:]),
                                wait=[(S.pe, mile), (sxr[sl2], xcnt),
                                      (sxn[sl2], sxn[sl2].cnt)],
                                inc=[(S.dve, 1)])
                        xg[sl2] = (S.dve, S.dve.cnt)
                        pso_free[bank] = S.dve.cnt
                        dcnt = S.dve.cnt
                        p.emit("sp", lambda e, oc=oc, ts=ts, sl2=sl2: e.dma_start(
                            x_out[oc * 128:(oc + 1) * 128, ts * 512:(ts + 1) * 512],
                            xnw[sl2][:]), wait=[(S.dve, dcnt)], inc=[(sxn[sl2], 16)])
            barrier(env)


def emit_moe(env, x_in, x_out, W):
    nc, p = env.nc, env.p
    S = env.s
    swe = [p.dsem(f"we{i}") for i in range(2)]
    sce = [p.dsem(f"ce{i}") for i in range(2)]
    sxr = [p.dsem(f"xr{i}") for i in range(4)]
    sxn = [p.dsem(f"xn{i}") for i in range(4)]

    def _half(half):
        t0 = half * 1024
        with ExitStack() as st:
            sb = lambda n, shp, dt: st.enter_context(nc.sbuf_tensor(n, shp, dt))
            psa = lambda n, shp, dt: st.enter_context(nc.psum_tensor(n, shp, dt))
            uid = f"m{env.uid}"
            env.uid += 1
            h = [sb(f"{uid}h{i}", [128, 1024], BF16) for i in range(NXC)]
            b_sb = sb(f"{uid}bex", [8, D], F32)
            sbex = p.dsem(f"{uid}bex")
            p.emit("act", lambda e: e.dma_start(b_sb[:], W["bex"]), inc=[(sbex, 16)])
            raw = sb(f"{uid}raw", [128, 64], F32)
            mtok = sb(f"{uid}mt", [128, 8], F32)
            rtok = sb(f"{uid}rt", [128, 8], F32)
            lg = sb(f"{uid}lg", [128, 8], F32)
            lgt = sb(f"{uid}lgt", [128, 8], F32)
            p8 = sb(f"{uid}p8", [128, 8], F32)
            msk = sb(f"{uid}msk", [128, 8], F32)
            rstat = sb(f"{uid}rs", [128, 3], F32)
            c_sb = sb(f"{uid}csb", [128, 64], F32)
            cT = sb(f"{uid}cT", [8, 1024], F32)
            ce0 = [sb(f"{uid}ce{r}", [1, 1024], F32) for r in range(2)]
            cb = [sb(f"{uid}cb{r}", [128, 1024], F32) for r in range(2)]
            hs = [sb(f"{uid}hs{i}", [128, 1024], BF16) for i in range(NXC)]
            y = [sb(f"{uid}y{i}", [128, 1024], F32) for i in range(NXC)]
            weF = sb(f"{uid}weF", [128, 8192], BF16)
            xnw = [sb(f"{uid}xn{i}", [128, 512], F32) for i in range(3)]
            xfr = [sb(f"{uid}xr{i}", [128, 512], F32) for i in range(3)]
            def load_we(slot, ee, og2, guard):
                p.emit("pool", lambda e, slot=slot, ee=ee, og2=og2: e.dma_start(
                    bass.AP(weF, slot * 4096,
                            [[8192, 128], [256, NXC], [1, 256]]),
                    bass.AP(W["wt"].tensor, ee * D * D + og2 * 256,
                            [[D, 128], [128 * D, NXC], [1, 256]])),
                    wait=[guard] if guard[1] > 0 else [],
                    inc=[(swe[slot], 16)])
                return swe[slot].cnt

            router = {"wg_sb": env.wg_sb, "raw": raw, "mtok": mtok,
                      "rtok": rtok, "t0": t0}
            h_subm = emit_ln(env, x_in, t0, 1024, h, 0, weF, router=router)
            h_done = S.dve.cnt
            we_ready = [0, 0]
            we_ready[0] = load_we(0, 0, 0, (S.dve, h_subm[-1]))
            we_ready[1] = load_we(1, 0, 1, (S.dve, h_subm[-1]))

            psT8 = psa(f"{uid}psT8", [8, 128], F32)
            ps_cb = psa(f"{uid}pscb", [128, 1024], F32)
            psy = [psa(f"{uid}psy{r}", [128, 512], F32) for r in range(2)]

            raw_done = S.dve.cnt
            for tc in range(8):
                p.emit("dve", lambda e, tc=tc: e.tensor_mul(
                    rstat[:, 0:1], rtok[:, tc:tc + 1], mtok[:, tc:tc + 1]),
                    wait=[(S.dve, S.dve.cnt)], inc=[(S.dve, 1)])
                p.emit("dve", lambda e, tc=tc: e.tensor_scalar_mul(
                    lg[:], raw[:, tc * 8:(tc + 1) * 8], rtok[:, tc:tc + 1]),
                    wait=[(S.dve, S.dve.cnt)], inc=[(S.dve, 1)])
                p.emit("dve", lambda e: e.tensor_scalar_mul(
                    lgt[:], env.wgsum_b[:], rstat[:, 0:1]),
                    wait=[(S.dve, S.dve.cnt)], inc=[(S.dve, 1)])
                p.emit("dve", lambda e: e.tensor_sub(lg[:], lg[:], lgt[:]),
                       wait=[(S.dve, S.dve.cnt)], inc=[(S.dve, 1)])
                p.emit("dve", lambda e: e.reduce_max(
                    rstat[:, 1:2], lg[:], axis=AX.X, negate=True),
                    wait=[(S.dve, S.dve.cnt)], inc=[(S.dve, 1)])
                l_cnt = S.dve.cnt
                p.emit("act", lambda e: e.activation(
                    p8[:], lg[:], AF.Exp, bias=rstat[:, 1:2],
                    accum_out=rstat[:, 2:3]),
                    wait=[(S.dve, l_cnt)], inc=[(S.act, 1)])
                e_cnt = S.act.cnt
                p.emit("dve", lambda e: e.tensor_scalar(
                    lgt[:], lg[:], rstat[:, 1:2], None, ALU.add),
                    wait=[(S.dve, S.dve.cnt)], inc=[(S.dve, 1)])
                p.emit("dve", lambda e: e.tensor_single_scalar(
                    msk[:], lgt[:], 0.0, ALU.is_equal),
                    wait=[(S.dve, S.dve.cnt)], inc=[(S.dve, 1)])
                p.emit("dve", lambda e: e.reciprocal(rstat[:, 0:1], rstat[:, 2:3]),
                       wait=[(S.act, e_cnt), (S.dve, S.dve.cnt)], inc=[(S.dve, 1)])
                p.emit("dve", lambda e: e.tensor_mul(p8[:], p8[:], msk[:]),
                       wait=[(S.dve, S.dve.cnt)], inc=[(S.dve, 1)])
                p.emit("dve", lambda e, tc=tc: e.tensor_scalar(
                    c_sb[:, tc * 8:(tc + 1) * 8], p8[:], rstat[:, 0:1], 0.5,
                    ALU.mult, ALU.mult),
                    wait=[(S.dve, S.dve.cnt)], inc=[(S.dve, 1)])
                c_cnt = S.dve.cnt
                p.emit("pe", lambda e, tc=tc: e.transpose(
                    psT8[:], c_sb[:, tc * 8:(tc + 1) * 8], env.idn_f[:]),
                    wait=[(S.dve, c_cnt)], inc=[(S.pe, 1)])
                t_cnt = S.pe.cnt
                p.emit("dve", lambda e, tc=tc: e.tensor_copy(
                    cT[0:8, tc * 128:(tc + 1) * 128], psT8[0:8, :]),
                    wait=[(S.pe, t_cnt)], inc=[(S.dve, 1)])
            cT_done = S.dve.cnt

            og_done = [0, 0]
            psy_free = [0, 0]
            cb_ready = [0, 0]
            hs_ready = 0
            ce_g = [(None, 0)] * 2
            cb_used = [0, 0]
            cb_used_pool = [0, 0]
            ps_cb_guard = [0]

            def prep_expert(ee):
                sl = ee % 2
                gs, gv = ce_g[sl]
                p.emit("sp", lambda e, ee=ee, sl=sl: e.dma_start(
                    ce0[sl][:], cT[ee:ee + 1, :]),
                    wait=[(S.dve, cT_done)] + ([(gs, gv)] if gs else []),
                    inc=[(sce[sl], 16)])
                ccnt = sce[sl].cnt
                for half2 in range(2):
                    p.emit("pe", lambda e, sl=sl, half2=half2: e.matmul(
                        ps_cb[:, half2 * 512:(half2 + 1) * 512], env.onesr_f[:],
                        ce0[sl][0:1, half2 * 512:(half2 + 1) * 512],
                        start=True, stop=True),
                        wait=([(sce[sl], ccnt), (S.dve, cb_used[sl]),
                               (S.pool, cb_used_pool[sl]),
                               (S.dve, ps_cb_guard[0])] if half2 == 0 else []),
                        inc=[(S.pe, 1)])
                bcnt = S.pe.cnt
                p.emit("dve", lambda e, sl=sl: e.tensor_copy(cb[sl][:], ps_cb[:]),
                       wait=[(S.pe, bcnt)], inc=[(S.dve, 1)])
                ce_g[sl] = (S.pe, bcnt)
                cb_ready[sl] = S.dve.cnt
                ps_cb_guard[0] = S.dve.cnt

            prep_expert(0)
            for ee in range(8):
                sl = ee % 2
                p.wait("dve", (S.dve, cb_ready[sl]), (S.pe, hs_ready))
                p.wait("pool", (S.dve, cb_ready[sl]), (S.pe, hs_ready))
                for c in range(NXC):
                    p.emit("dve", lambda e, c=c, sl=sl: e.tensor_mul(
                        hs[c][:], h[c][:], cb[sl][:]),
                        wait=[(S.dve, h_done)] if c == 0 else [], inc=[(S.dve, 1)])
                hs_dve, hs_pool = S.dve.cnt, S.pool.cnt
                cb_used[sl] = S.dve.cnt
                cb_used_pool[sl] = S.pool.cnt
                if ee + 1 < 8:
                    prep_expert(ee + 1)
                p.wait("pe", (S.dve, hs_dve), (S.pool, hs_pool))
                for og2 in range(8):
                    unit = ee * 8 + og2
                    wsl = unit % 2
                    if unit + 1 < 64 and unit + 1 >= 2:
                        we_ready[1 - wsl] = load_we(1 - wsl, (unit + 1) // 8,
                                                    (unit + 1) % 8,
                                                    (S.pe, og_done[1 - wsl]))
                    p.wait("pe", (swe[wsl], we_ready[wsl]))
                    for oi in range(2):
                        oc = og2 * 2 + oi
                        for ts in range(2):
                            bank = (oi * 2 + ts) % 2
                            p.wait("pe", (S.dve, psy_free[bank]))
                            for c in range(NXC):
                                p.emit("pe", lambda e, c=c, wsl=wsl, oi=oi, ts=ts, bank=bank: e.matmul(
                                    psy[bank][:],
                                    weF[:, wsl * 4096 + c * 256 + oi * 128:
                                        wsl * 4096 + c * 256 + (oi + 1) * 128],
                                    hs[c][:, ts * 512:(ts + 1) * 512],
                                    start=(c == 0), stop=(c == NXC - 1)),
                                    inc=[(S.pe, 1)] if c == NXC - 1 else [])
                            mile = S.pe.cnt
                            if ee == 0:
                                p.emit("dve", lambda e, oc=oc, ts=ts, bank=bank: e.tensor_copy(
                                    y[oc][:, ts * 512:(ts + 1) * 512], psy[bank][:]),
                                    wait=[(S.pe, mile)], inc=[(S.dve, 1)])
                            else:
                                p.emit("dve", lambda e, oc=oc, ts=ts, bank=bank: e.tensor_add(
                                    y[oc][:, ts * 512:(ts + 1) * 512],
                                    y[oc][:, ts * 512:(ts + 1) * 512], psy[bank][:]),
                                    wait=[(S.pe, mile)], inc=[(S.dve, 1)])
                            psy_free[bank] = S.dve.cnt
                    og_done[wsl] = S.pe.cnt
                hs_ready = S.pe.cnt
            for oc in range(NXC):
                for ts in range(2):
                    bank = (oc * 2 + ts) % 2
                    p.wait("pe", (S.dve, psy_free[bank]))
                    p.emit("pe", lambda e, oc=oc, ts=ts, bank=bank: e.matmul(
                        psy[bank][:], b_sb[0:8, oc * 128:(oc + 1) * 128],
                        cT[0:8, ts * 512:(ts + 1) * 512], start=True, stop=True),
                        wait=[(sbex, 16)] if oc == 0 and ts == 0 else [],
                        inc=[(S.pe, 1)])
                    mile = S.pe.cnt
                    p.emit("dve", lambda e, oc=oc, ts=ts, bank=bank: e.tensor_add(
                        y[oc][:, ts * 512:(ts + 1) * 512],
                        y[oc][:, ts * 512:(ts + 1) * 512], psy[bank][:]),
                        wait=[(S.pe, mile)], inc=[(S.dve, 1)])
                    psy_free[bank] = S.dve.cnt
            xg = env.rg.setdefault("epixg", [(None, 0)] * 4)
            for oc in range(NXC):
                for ts in range(2):
                    sl = (oc * 2 + ts) % 3
                    gs, gv = xg[sl]
                    p.emit("sp", lambda e, oc=oc, ts=ts, sl=sl, t0=t0: e.dma_start(
                        xfr[sl][:], x_in[oc * 128:(oc + 1) * 128,
                                         t0 + ts * 512:t0 + (ts + 1) * 512]),
                        wait=([(gs, gv)] if gs else []), inc=[(sxr[sl], 16)])
                    xcnt = sxr[sl].cnt
                    p.emit("dve", lambda e, oc=oc, ts=ts, sl=sl: e.tensor_add(
                        xnw[sl][:], y[oc][:, ts * 512:(ts + 1) * 512], xfr[sl][:]),
                        wait=[(sxr[sl], xcnt), (sxn[sl], sxn[sl].cnt)],
                        inc=[(S.dve, 1)])
                    xg[sl] = (S.dve, S.dve.cnt)
                    dcnt = S.dve.cnt
                    p.emit("act", lambda e, oc=oc, ts=ts, sl=sl, t0=t0: e.dma_start(
                        x_out[oc * 128:(oc + 1) * 128,
                              t0 + ts * 512:t0 + (ts + 1) * 512], xnw[sl][:]),
                        wait=[(S.dve, dcnt)], inc=[(sxn[sl], 16)])
            barrier(env)

    _half(0)
    _half(1)


def build_group(sublayers):
    nc = bass.Bass("TRN2", target_bir_lowering=False)
    env = Env()
    env.nc = nc
    env.p = Prog(nc)
    env.uid = 0
    env.bar_snap = {}
    env.rg = {}
    p = env.p

    class Sems:
        pass
    S = Sems()
    for nm in ("pe", "act", "dve", "pool", "dsp", "dpool", "dact"):
        setattr(S, nm, p.sem(nm))
    env.s = S

    need = set(sublayers)
    x_in = nc.dram_tensor("xin", [D, T], F32, kind="ExternalInput").ap()
    x_out = nc.dram_tensor("xout", [D, T], F32, kind="ExternalOutput").ap()
    names = ["xin"]

    def din(name, shape, dt):
        names.append(name)
        return nc.dram_tensor(name, shape, dt, kind="ExternalInput").ap()

    onesc_bf_d = din("onescb", [128, 1], BF16)
    onesr_f_d = din("onesrf", [1, 128], F32)
    onesc_f_d = din("onescf", [128, 1], F32)

    w = {}
    if need & {0, 2}:
        w["wg"] = din("wg", [D, INTER], I8)
        w["wu"] = din("wu", [D, INTER], I8)
        w["wd"] = din("wd", [INTER, D], I8)
        w["sg"] = din("sgv", [128, 64], F32)
        w["su"] = din("suv", [128, 64], F32)
        w["sd"] = din("sdv", [128, 16], F32)
    if 1 in need:
        w["wqkv0"] = din("wqkv0", [D, 3 * D], F32)
        w["bqkv0"] = din("bqkv0", [128, 48], F32)
        w["wout0"] = din("wout0", [D, D], F32)
        w["bout0"] = din("bout0", [128, 16], F32)
        w["anot"] = din("anot", [4 * 128, 512], U8)
        w["n10k"] = din("n10k", [128, 512], F32)
    if 4 in need:
        w["wqkv1"] = din("wqkv1", [D, 3 * D], F32)
        w["wout1"] = din("wout1", [D, D], F32)
    if need & {3, 5}:
        w["wgr"] = din("wgr", [128, 128], F32)
        w["wt"] = din("wt", [8, D, D], F32)
        w["bex"] = din("bex", [8, D], F32)
        w["idnf"] = din("idnf", [128, 128], F32)
    if need & {1, 4}:
        w["negc"] = din("negc", [128, 1], F32)
        env.negc_d = w["negc"]
        env.Ks = nc.dram_tensor("ks_scr", [D, T], BF16, kind="Internal").ap()
        env.Vs = nc.dram_tensor("vs_scr", [T, D], BF16, kind="Internal").ap()

    nseq = len(sublayers)
    xs = [x_in]
    for i in range(nseq - 1):
        xs.append(nc.dram_tensor(f"xscr{i}", [D, T], F32, kind="Internal").ap())
    xs.append(x_out)

    stack = ExitStack()
    sb = lambda n, shp, dt: stack.enter_context(nc.sbuf_tensor(n, shp, dt))

    env.onesc_bf = sb("onescb_s", [128, 1], BF16)
    env.onesr_f = sb("onesrf_s", [1, 128], F32)
    env.onesc_f = sb("onescf_s", [128, 1], F32)
    ds_ones = p.dsem("csto")
    p.emit("act", lambda e: e.dma_start(env.onesc_bf[:], onesc_bf_d), inc=[(ds_ones, 16)])
    p.emit("act", lambda e: e.dma_start(env.onesr_f[:], onesr_f_d), inc=[(ds_ones, 16)])
    p.emit("act", lambda e: e.dma_start(env.onesc_f[:], onesc_f_d), inc=[(ds_ones, 16)])
    env.const_w = [(ds_ones, 48)]

    if need & {0, 2}:
        env.sg_sb = sb("sg_s", [128, 64], F32)
        env.su_sb = sb("su_s", [128, 64], F32)
        env.sdr_sb = sb("sdr_s", [128, 16], F32)
        env.sdh_sb = sb("sdh_s", [128, 16], F32)
        ds_ffn = p.dsem("cstf")
        p.emit("act", lambda e: e.dma_start(env.sg_sb[:], w["sg"]), inc=[(ds_ffn, 16)])
        p.emit("act", lambda e: e.dma_start(env.su_sb[:], w["su"]), inc=[(ds_ffn, 16)])
        p.emit("act", lambda e: e.dma_start(env.sdr_sb[:], w["sd"]), inc=[(ds_ffn, 16)])
        p.emit("dve", lambda e: e.tensor_scalar_mul(env.sdh_sb[:], env.sdr_sb[:], 0.5),
               wait=[(ds_ffn, 48)], inc=[(S.dve, 1)])
        env.const_w.append((ds_ffn, 48))
    if need & {3, 5}:
        env.wg_sb = sb("wgr_s", [128, 128], F32)
        env.idn_f = sb("idnf_s", [128, 128], F32)
        env.wgsum_b = sb("wgsb_s", [128, 8], F32)
        wgsum_r = sb("wgsr_s", [1, 8], F32)
        ds_moe = p.dsem("cstm")
        p.emit("act", lambda e: e.dma_start(env.wg_sb[:], w["wgr"]), inc=[(ds_moe, 16)])
        p.emit("act", lambda e: e.dma_start(env.idn_f[:], w["idnf"]), inc=[(ds_moe, 16)])
        env.const_w.append((ds_moe, 32))
        with ExitStack() as st0:
            ps_ws = st0.enter_context(nc.psum_tensor("ps_ws", [1, 8], F32))
            ps_wb = st0.enter_context(nc.psum_tensor("ps_wb", [128, 8], F32))
            for c in range(16):
                p.emit("pe", lambda e, c=c: e.matmul(
                    ps_ws[:], env.onesc_f[:], env.wg_sb[:, c * 8:(c + 1) * 8],
                    start=(c == 0), stop=(c == 15)),
                    wait=[(ds_moe, 32), (ds_ones, 48)] if c == 0 else [],
                    inc=[(S.pe, 1)])
            p.emit("dve", lambda e: e.tensor_copy(wgsum_r[:], ps_ws[:]),
                   wait=[(S.pe, S.pe.cnt)], inc=[(S.dve, 1)])
            wr_cnt = S.dve.cnt
            p.emit("pe", lambda e: e.matmul(ps_wb[:], env.onesr_f[:], wgsum_r[0:1, :],
                                            start=True, stop=True),
                   wait=[(S.dve, wr_cnt)], inc=[(S.pe, 1)])
            p.emit("dve", lambda e: e.tensor_copy(env.wgsum_b[:], ps_wb[:]),
                   wait=[(S.pe, S.pe.cnt)], inc=[(S.dve, 1)])
    if 1 in need:
        env.anot_d = w["anot"]
        env.n10k_d = w["n10k"]

    cfg0 = {
        "wqkv": w.get("wqkv0"), "bqkv": w.get("bqkv0"),
        "wout": w.get("wout0"), "bout": w.get("bout0"),
        "causal": True, "l0_bias": True,
        "qk_slices": lambda hh: [(384 * hh, 256, 0)],
        "v_slices": lambda g: [(384 * (4 * g + i) + 256, 128, 128 * i)
                               for i in range(4)],
    }
    cfg1 = {
        "wqkv": w.get("wqkv1"), "wout": w.get("wout1"),
        "causal": False, "l0_bias": False,
        "qk_slices": lambda hh: [(128 * hh, 128, 0), (2048 + 128 * hh, 128, 128)],
        "v_slices": lambda g: [(4096 + 512 * g, 512, 0)],
    }

    barrier(env)
    for i, sl_id in enumerate(sublayers):
        xi, xo = xs[i], xs[i + 1]
        if sl_id in (0, 2):
            emit_ffn0(env, xi, xo, w)
        elif sl_id == 1:
            emit_attn(env, xi, xo, cfg0)
        elif sl_id == 4:
            emit_attn(env, xi, xo, cfg1)
        else:
            emit_moe(env, xi, xo, {"wt": w["wt"], "bex": w["bex"]})

    p.wait("sp", *[(s, s.cnt) for s in p.sems if s.cnt > 0])

    with stack:
        p.alloc_sems(stack)
        with nc.Block() as block:
            p.replay(block)
    return nc, names


_BUILT = {}


def _get_group(key, sublayers):
    if key not in _BUILT:
        _BUILT[key] = build_group(sublayers)
    return _BUILT[key]


def kernel(L0_attn_x, L0_attn_w_qkv, L0_attn_b_qkv, L0_attn_w_out, L0_attn_b_out,
           L0_ffn_w_gate, L0_ffn_w_up, L0_ffn_w_down, L0_ffn_s_gate, L0_ffn_s_up,
           L0_ffn_s_down, L1_attn_Wqkv, L1_attn_Wout, L1_ffn_wg, L1_ffn_W,
           L1_ffn_b):
    x = np.asarray(L0_attn_x, dtype=np.float32)
    n_cores = 8

    pp, cc = np.meshgrid(np.arange(128), np.arange(512), indexing="ij")
    anot = np.stack([(128 * i + pp > cc) for i in range(4)], axis=0)
    anot = anot.reshape(4 * 128, 512).astype(np.uint8)

    weights = {
        "wg": np.ascontiguousarray(L0_ffn_w_gate),
        "wu": np.ascontiguousarray(L0_ffn_w_up),
        "wd": np.ascontiguousarray(L0_ffn_w_down),
        "sgv": np.ascontiguousarray(np.asarray(L0_ffn_s_gate, np.float32).reshape(64, 128).T),
        "suv": np.ascontiguousarray(np.asarray(L0_ffn_s_up, np.float32).reshape(64, 128).T),
        "sdv": np.ascontiguousarray(np.asarray(L0_ffn_s_down, np.float32).reshape(16, 128).T),
        "wqkv0": np.ascontiguousarray(np.asarray(L0_attn_w_qkv, np.float32)),
        "bqkv0": np.ascontiguousarray(np.asarray(L0_attn_b_qkv, np.float32).reshape(48, 128).T),
        "wout0": np.ascontiguousarray(np.asarray(L0_attn_w_out, np.float32)),
        "bout0": np.ascontiguousarray(np.asarray(L0_attn_b_out, np.float32).reshape(16, 128).T),
        "anot": anot,
        "n10k": np.full((128, 512), NEGMASK, dtype=np.float32),
        "wqkv1": np.ascontiguousarray(np.asarray(L1_attn_Wqkv, np.float32).T),
        "wout1": np.ascontiguousarray(np.asarray(L1_attn_Wout, np.float32).T),
        "wgr": np.ascontiguousarray(np.asarray(L1_ffn_wg, np.float32)
                                    .reshape(16, 128, 8).transpose(1, 0, 2).reshape(128, 128)),
        "wt": np.ascontiguousarray(np.asarray(L1_ffn_W, np.float32).transpose(0, 2, 1)),
        "bex": np.ascontiguousarray(np.asarray(L1_ffn_b, np.float32)),
        "idnf": np.eye(128, dtype=np.float32),
        "negc": np.full((128, 1), -30.0, dtype=np.float32),
        "onescb": np.ones((128, 1), dtype=ml_dtypes.bfloat16),
        "onesrf": np.ones((1, 128), dtype=np.float32),
        "onescf": np.ones((128, 1), dtype=np.float32),
    }

    xc = [np.ascontiguousarray(x[c].T) for c in range(n_cores)]
    for gi, group in enumerate(GROUPS):
        nc, names = _get_group(tuple(group), group)
        in_maps = []
        for c in range(n_cores):
            m = {"xin": xc[c]}
            for nm in names:
                if nm != "xin":
                    m[nm] = weights[nm]
            in_maps.append(m)
        res = run_bass_kernel_spmd(nc, in_maps, core_ids=list(range(n_cores)))
        xc = [np.asarray(res.results[c]["xout"]) for c in range(n_cores)]

    out = np.stack([np.asarray(xc[c], dtype=np.float32).T for c in range(n_cores)],
                   axis=0)
    return out
